# version 27
# baseline (speedup 1.0000x reference)
import numpy as np
from contextlib import ExitStack

import concourse.bass as bass
import concourse.mybir as mybir
from concourse.bass_utils import run_bass_kernel_spmd

F32 = mybir.dt.float32
BF16 = mybir.dt.bfloat16
AF = mybir.ActivationFunctionType
ALU = mybir.AluOpType
AXX = mybir.AxisListType.X

B, S, H, HEADS, HD, FF = 8, 4096, 1024, 8, 128, 4096
NT = S // 128
KT = H // 128
MT = FF // 128
QCW = 1024
NQC = S // QCW
SBW = 256
NSB = S // SBW
EPS = 1e-5
ISQ = float(1.0 / np.sqrt(HD))
LAMBDA_INIT = 0.8
NEG_HALF_LN_HD = float(-0.5 * np.log(HD))

ENGINES = ("tensor", "scalar", "vector", "gpsimd", "sync")
_DBG = False


class Prog:

    def __init__(self):
        self.streams = {e: [] for e in ENGINES}
        self.counts = {}

    def n(self, sem):
        return self.counts.get(sem, 0)

    def emit(self, eng, build, waits=None, inc=None, by=1):
        w = [(s, v) for (s, v) in (waits or []) if v > 0]
        self.streams[eng].append((build, w, inc, by))
        if inc is not None:
            self.counts[inc] = self.counts.get(inc, 0) + by
            return self.counts[inc]
        return None

    def dma(self, eng, dst, src, waits=None, inc="swg"):
        n = self.emit(eng, lambda e, d=dst, s=src: e.dma_start(d, s),
                      waits=waits, inc=inc, by=16)
        return (inc, n)

    def realize(self, block, sems):
        def section(eng_name):
            stream = self.streams[eng_name]

            def body(e):
                watermark = {}
                for build, waits, inc, by in stream:
                    for s_, v in waits:
                        if watermark.get(s_, -1) < v:
                            e.wait_ge(sems[s_], v)
                            watermark[s_] = v
                    inst = build(e)
                    if inc is not None:
                        inst.then_inc(sems[inc], by)
            return body

        block.sync(section("sync"))
        block.gpsimd(section("gpsimd"))
        block.tensor(section("tensor"))
        block.scalar(section("scalar"))
        block.vector(section("vector"))


def build_nc():
    nc = bass.Bass("TRN2", target_bir_lowering=False)

    x_d = nc.dram_tensor("x", [S, H], F32, kind="ExternalInput").ap()
    Wq_d = nc.dram_tensor("Wq", [H, H], F32, kind="ExternalInput").ap()
    Wk_d = nc.dram_tensor("Wk", [H, H], F32, kind="ExternalInput").ap()
    Wv_d = nc.dram_tensor("Wv", [H, H], F32, kind="ExternalInput").ap()
    Wo_d = nc.dram_tensor("Wo", [H, H], F32, kind="ExternalInput").ap()
    bq_d = nc.dram_tensor("bq", [H], F32, kind="ExternalInput").ap()
    bk_d = nc.dram_tensor("bk", [H], F32, kind="ExternalInput").ap()
    bv_d = nc.dram_tensor("bv", [H], F32, kind="ExternalInput").ap()
    bo_d = nc.dram_tensor("bo", [H], F32, kind="ExternalInput").ap()
    lq1_d = nc.dram_tensor("lq1", [HD], F32, kind="ExternalInput").ap()
    lk1_d = nc.dram_tensor("lk1", [HD], F32, kind="ExternalInput").ap()
    lq2_d = nc.dram_tensor("lq2", [HD], F32, kind="ExternalInput").ap()
    lk2_d = nc.dram_tensor("lk2", [HD], F32, kind="ExternalInput").ap()
    lnw_d = nc.dram_tensor("lnw", [H], F32, kind="ExternalInput").ap()
    lnb_d = nc.dram_tensor("lnb", [H], F32, kind="ExternalInput").ap()
    w1_d = nc.dram_tensor("w1", [H, FF], F32, kind="ExternalInput").ap()
    b1_d = nc.dram_tensor("b1", [FF], F32, kind="ExternalInput").ap()
    w2_d = nc.dram_tensor("w2", [FF, H], F32, kind="ExternalInput").ap()
    b2_d = nc.dram_tensor("b2", [H], F32, kind="ExternalInput").ap()
    c_d = nc.dram_tensor("c", [H], F32, kind="ExternalInput").ap()
    wmod_d = nc.dram_tensor("wmod", [H, 6 * H], F32, kind="ExternalInput").ap()
    out_d = nc.dram_tensor("out", [S, H], F32, kind="ExternalOutput").ap()
    ao_scr = nc.dram_tensor("ao_scr", [S, H], BF16, kind="Internal").ap()
    qT_scr = nc.dram_tensor("qT_scr", [HEADS, 128, S], BF16, kind="Internal").ap()
    kT_scr = nc.dram_tensor("kT_scr", [HEADS, 128, S], BF16, kind="Internal").ap()
    x1_scr = nc.dram_tensor("x1_scr", [S, H], F32, kind="Internal").ap()

    root = ExitStack()
    taps = {}

    def SB(stack, name, shape, dt):
        return stack.enter_context(nc.sbuf_tensor(name, shape, dt))

    def PS(stack, name, shape, dt=F32):
        return stack.enter_context(nc.psum_tensor(name, shape, dt))

    ones_row_f = SB(root, "ones_row_f", [1, 128], F32)
    p2_row_f = SB(root, "p2_row_f", [1, 128], F32)
    ones_col_f = SB(root, "ones_col_f", [128, 1], F32)
    ones_col_b = SB(root, "ones_col_b", [128, 1], BF16)
    ident_b = SB(root, "ident_b", [128, 128], BF16)
    ident_f = SB(root, "ident_f", [128, 128], F32)

    gac_row = SB(root, "gac_row", [1, H], F32)
    gbc_row = SB(root, "gbc_row", [1, H], F32)
    ca1T = SB(root, "ca1T", [128, KT], F32)
    cb1T = SB(root, "cb1T", [128, KT], F32)
    saT_b = SB(root, "saT_b", [128, KT], BF16)
    sbT_b = SB(root, "sbT_b", [128, KT], BF16)
    bqT = SB(root, "bqT", [128, KT], F32)
    bkT = SB(root, "bkT", [128, KT], F32)
    lnwT = SB(root, "lnwT", [128, KT], F32)
    lnbT_f = SB(root, "lnbT_f", [128, KT], F32)
    lnbT_b = SB(root, "lnbT_b", [128, KT], BF16)
    b2row = SB(root, "b2row", [1, H], F32)
    cT = SB(root, "cT", [128, KT], F32)
    silu_cT = SB(root, "silu_cT", [128, KT], BF16)
    lam_in = SB(root, "lam_in", [128, 4], F32)
    lamneg = SB(root, "lamneg", [128, 1], F32)
    lam11 = SB(root, "lam11", [1, 4], F32)
    Bneg = SB(root, "Bneg", [128, HEADS], F32)
    bnd = SB(root, "bnd", [1, 8], F32)
    bnd16 = SB(root, "bnd16", [1, 16], F32)
    biasqT = SB(root, "biasqT", [128, 4], F32)
    nrow = SB(root, "nrow", [1, 512], F32)
    eps_col = SB(root, "eps_col", [128, 1], F32)
    nhl_col = SB(root, "nhl_col", [128, 1], F32)
    maxcol = SB(root, "maxcol", [128, 2], F32)

    stats = {}
    for nm in ("mu1s", "ssq1", "mu1", "rstd1", "v1t", "m1t",
               "mu2s", "ssq2", "mu2", "rstd2", "v2t", "m2t",
               "amus", "assq", "amu", "arstd", "avt", "amt"):
        stats[nm] = SB(root, nm, [128, NT], F32)

    sems = {}
    for s_ in ("sx", "swg", "sao", "sout", "pe", "act", "dve", "pl",
               "sx0", "sx1", "sx2", "sx3", "wm0", "wm1", "swv", "swq",
               "sqk0", "sqk1", "sal0", "sal1", "sal2", "sal3",
               "sal4", "sal5", "sal6", "sal7", "swo",
               "sw1", "sw2", "sb1", "sas0", "sas1", "sx10", "sx11",
               "sd0", "sd1", "sd2", "sd3", "sf0", "sf1", "so0", "so1"):
        sems[s_] = root.enter_context(nc.semaphore(s_))

    p = Prog()
    last_dead = [0]

    def fence(dma=()):
        w = [("pe", p.n("pe")), ("act", p.n("act")), ("dve", p.n("dve")),
             ("pl", p.n("pl"))]
        for s_ in dma:
            w.append((s_, p.n(s_)))
        return w

    def tap(name, src_ap, waits):
        if not _DBG:
            return
        t = nc.dram_tensor("dbg_" + name, list(src_ap.shape), src_ap.dtype,
                           kind="ExternalOutput").ap()
        p.dma("sync", t[tuple(slice(None) for _ in src_ap.shape)], src_ap,
              waits=waits, inc="sout")

    pre = ExitStack()
    x_tiles = SB(pre, "x_tiles", [128, 4 * H], F32)
    h_bf = SB(pre, "h_bf", [128, 2 * H], BF16)
    deadout = SB(pre, "deadout", [128, 1024], F32)

    p.emit("gpsimd", lambda e: e.memset(ones_row_f[:, :], 1.0))
    p.emit("gpsimd", lambda e: e.memset(p2_row_f[:, :], 5.0))
    p.emit("gpsimd", lambda e: e.memset(ones_col_f[:, :], 1.0))
    p.emit("gpsimd", lambda e: e.memset(ones_col_b[:, :], 1.0))
    p.emit("gpsimd", lambda e: e.iota(
        deadout[:, 0:128], pattern=[[-1, 128]], base=0, channel_multiplier=1,
        allow_small_or_imprecise_dtypes=True), inc="pl")
    pl_iota = p.n("pl")
    p.emit("vector", lambda e: e.tensor_scalar(
        ident_b[:, :], deadout[:, 0:128], 0.0, None, ALU.is_equal),
        waits=[("pl", pl_iota)], inc="dve")
    p.emit("vector", lambda e: e.tensor_scalar(
        ident_f[:, :], deadout[:, 0:128], 0.0, None, ALU.is_equal),
        inc="dve")
    p.emit("vector", lambda e: e.reciprocal(p2_row_f[:, :], p2_row_f[:, :]),
           waits=[("pl", pl_iota)], inc="dve")
    d_ident = p.n("dve")
    p.emit("gpsimd", lambda e: e.memset(eps_col[:, :], EPS),
        waits=[("dve", d_ident)])
    p.emit("gpsimd", lambda e: e.memset(nhl_col[:, :], NEG_HALF_LN_HD), inc="pl")
    pl_const = p.n("pl")

    for kt in range(KT):
        p.dma("sync", bqT[:, kt:kt + 1], bq_d[kt * 128:(kt + 1) * 128][:, None], inc="sx")
        p.dma("sync", bkT[:, kt:kt + 1], bk_d[kt * 128:(kt + 1) * 128][:, None], inc="sx")
        p.dma("sync", lnwT[:, kt:kt + 1], lnw_d[kt * 128:(kt + 1) * 128][:, None], inc="sx")
        p.dma("sync", lnbT_f[:, kt:kt + 1], lnb_d[kt * 128:(kt + 1) * 128][:, None], inc="sx")
        p.dma("sync", cT[:, kt:kt + 1], c_d[kt * 128:(kt + 1) * 128][:, None], inc="sx")
    for i, d in enumerate([lq1_d, lk1_d, lq2_d, lk2_d]):
        p.dma("sync", lam_in[:, i:i + 1], d[:, None], inc="sx")
    p.dma("sync", b2row[0:1, :], b2_d[None, :], inc="sx")
    n_small = p.n("sx")

    p.emit("scalar", lambda e: e.activation(lnbT_b[:, :], lnbT_f[:, :], AF.Copy),
           waits=[("sx", n_small)], inc="act")

    mscope = ExitStack()
    wmod_sb = SB(mscope, "wmod_sb", [128, 2 * 2048], BF16)
    modrow = SB(mscope, "modrow", [1, 6 * H], F32)
    sgn_scr = SB(mscope, "sgn_scr", [1, H], F32)
    saT_f = SB(mscope, "saT_f", [128, KT], F32)
    sbT_f = SB(mscope, "sbT_f", [128, KT], F32)
    lamprod = SB(mscope, "lamprod", [128, 2], F32)
    mod_ps = PS(mscope, "mod_ps", [1, 1024])
    mod_ps2 = [mod_ps, PS(mscope, "mod_psB", [1, 1024])]
    modt_ps = PS(mscope, "modt_ps", [128, 32])
    p_bc = PS(mscope, "p_bc", [128, 512])

    p.emit("scalar", lambda e: e.activation(silu_cT[:, :], cT[:, :], AF.Silu),
           waits=[("sx", n_small)], inc="act")
    a_silu = p.n("act")

    mod_pe = {}
    mod_cp = 0
    for g in range(3):
        for kt in range(KT):
            slot = (g * KT + kt) % 2
            dep = [("pl", pl_const)]
            if (g * KT + kt) >= 2:
                dep.append(("pe", mod_pe[g * KT + kt - 2]))
            wsem = "wm%d" % slot
            p.dma("gpsimd", wmod_sb[:, slot * 2048:(slot + 1) * 2048],
                  wmod_d[kt * 128:(kt + 1) * 128, g * 2048:(g + 1) * 2048],
                  waits=dep, inc=wsem)
            wn = (wsem, p.n(wsem))
            for ch in range(4):
                p.emit("tensor", lambda e, slot=slot, kt=kt, ch=ch: e.matmul(
                    mod_ps[0:1, (ch % 2) * 512:(ch % 2 + 1) * 512]
                    if False else mod_ps2[ch // 2][0:1, (ch % 2) * 512:(ch % 2 + 1) * 512],
                    silu_cT[:, kt:kt + 1],
                    wmod_sb[:, slot * 2048 + ch * 512: slot * 2048 + (ch + 1) * 512],
                    start=(kt == 0), stop=(kt == KT - 1), skip_group_check=True),
                    waits=[wn, ("act", a_silu), ("act", mod_cp)],
                    inc="pe" if ch == 3 else None)
            mod_pe[g * KT + kt] = p.n("pe")
        pn = p.n("pe")
        for half in range(2):
            p.emit("scalar", lambda e, g=g, half=half: e.activation(
                modrow[0:1, g * 2048 + half * 1024:(g * 2048) + (half + 1) * 1024],
                mod_ps2[half][0:1, :], AF.Copy),
                waits=[("pe", pn)], inc="act" if half == 1 else None)
        mod_cp = p.n("act")
    a_modrow = p.n("act")

    for row, off in ((gac_row, 2 * H), (gbc_row, 5 * H)):
        p.emit("vector", lambda e, r=row, o=off: e.tensor_scalar(
            r[0:1, :], modrow[0:1, o:o + H], 0.0, 2.0, ALU.is_ge, ALU.mult),
            waits=[("act", a_modrow)], inc="dve")
        c1 = p.n("dve")
        p.emit("vector", lambda e, r=row: e.tensor_scalar_add(r[0:1, :], r[0:1, :], -1.0),
               waits=[("dve", c1)], inc="dve")
        c2 = p.n("dve")
        p.emit("vector", lambda e, o=off: e.tensor_scalar_mul(
            sgn_scr[0:1, :], modrow[0:1, o:o + H], -1.0), inc="dve")
        c3 = p.n("dve")
        p.emit("vector", lambda e, o=off: e.tensor_max(
            sgn_scr[0:1, :], sgn_scr[0:1, :], modrow[0:1, o:o + H]),
            waits=[("dve", c3)], inc="dve")
        c4 = p.n("dve")
        p.emit("vector", lambda e: e.tensor_scalar_max(
            sgn_scr[0:1, :], sgn_scr[0:1, :], 0.25),
            waits=[("dve", c4)], inc="dve")
        c5 = p.n("dve")
        p.emit("vector", lambda e, r=row: e.tensor_mul(r[0:1, :], r[0:1, :], sgn_scr[0:1, :]),
               waits=[("dve", c2), ("dve", c5)], inc="dve")
    d_clamp = p.n("dve")

    for i, off in enumerate((0, H, 3 * H, 4 * H)):
        for kt in range(KT):
            p.emit("tensor", lambda e, i=i, off=off, kt=kt: e.matmul(
                modt_ps[:, i * KT + kt: i * KT + kt + 1],
                modrow[0:1, off + kt * 128: off + (kt + 1) * 128],
                ones_row_f[0:1, 0:1],
                start=True, stop=True, skip_group_check=True),
                waits=[("act", a_modrow), ("pl", pl_const)],
                inc="pe" if (i == 3 and kt == KT - 1) else None)
    pn = p.n("pe")
    p.emit("scalar", lambda e: e.activation(saT_f[:, :], modt_ps[:, 0:KT], AF.Copy),
           waits=[("pe", pn)])
    p.emit("scalar", lambda e: e.activation(ca1T[:, :], modt_ps[:, KT:2 * KT], AF.Copy))
    p.emit("scalar", lambda e: e.activation(sbT_f[:, :], modt_ps[:, 2 * KT:3 * KT], AF.Copy))
    p.emit("scalar", lambda e: e.activation(cb1T[:, :], modt_ps[:, 3 * KT:4 * KT], AF.Copy),
           inc="act")
    a_mcp = p.n("act")
    p.emit("scalar", lambda e: e.activation(saT_b[:, :], saT_f[:, :], AF.Copy),
           waits=[("act", a_mcp)])
    p.emit("scalar", lambda e: e.activation(sbT_b[:, :], sbT_f[:, :], AF.Copy), inc="act")
    a_modT = p.n("act")
    p.emit("vector", lambda e: e.tensor_scalar_add(ca1T[:, :], ca1T[:, :], 1.0),
           waits=[("act", a_modT)])
    p.emit("vector", lambda e: e.tensor_scalar_add(cb1T[:, :], cb1T[:, :], 1.0), inc="dve")
    d_ca1 = p.n("dve")

    p.emit("vector", lambda e: e.tensor_mul(lamprod[:, 0:1], lam_in[:, 0:1], lam_in[:, 1:2]),
           waits=[("sx", n_small)])
    p.emit("vector", lambda e: e.tensor_mul(lamprod[:, 1:2], lam_in[:, 2:3], lam_in[:, 3:4]),
           inc="dve")
    dn = p.n("dve")
    p.emit("tensor", lambda e: e.matmul(p_bc[0:1, 0:1], lamprod[:, 0:1], ones_col_f[:, 0:1],
                                        start=True, stop=True, skip_group_check=True),
           waits=[("dve", dn)])
    p.emit("tensor", lambda e: e.matmul(p_bc[0:1, 1:2], lamprod[:, 1:2], ones_col_f[:, 0:1],
                                        start=False, stop=True, skip_group_check=True),
           inc="pe")
    pn = p.n("pe")
    p.emit("scalar", lambda e: e.activation(lam11[0:1, 0:2], p_bc[0:1, 0:2], AF.Exp),
           waits=[("pe", pn)], inc="act")
    an = p.n("act")
    p.emit("vector", lambda e: e.tensor_sub(lam11[0:1, 2:3], lam11[0:1, 0:1], lam11[0:1, 1:2]),
           waits=[("act", an)], inc="dve")
    lsn = p.n("dve")
    p.emit("vector", lambda e: e.tensor_scalar(lam11[0:1, 3:4], lam11[0:1, 2:3],
                                               LAMBDA_INIT, -1.0, ALU.add, ALU.mult),
           waits=[("dve", lsn)], inc="dve")
    dn = p.n("dve")
    p.emit("tensor", lambda e: e.matmul(p_bc[:, 4:5], ones_row_f[0:1, :], lam11[0:1, 3:4],
                                        start=True, stop=True, skip_group_check=True),
           waits=[("dve", dn)], inc="pe")
    pn = p.n("pe")
    p.emit("scalar", lambda e: e.activation(lamneg[:, 0:1], p_bc[:, 4:5], AF.Copy),
           waits=[("pe", pn)], inc="act")
    a_lam = p.n("act")
    tap("modrow", modrow[0:1, :], [("act", a_modrow)])
    tap("gac_row", gac_row[0:1, :], [("dve", d_clamp)])
    tap("gbc_row", gbc_row[0:1, :], [("dve", d_clamp)])
    tap("lam11", lam11[0:1, :], [("dve", p.n("dve")), ("act", a_lam)])
    tap("lamneg", lamneg[:, :], [("act", a_lam)])
    tap("ca1T", ca1T[:, :], [("dve", d_ca1)])
    tap("saTb", saT_b[:, :], [("act", a_modT)])
    mscope.close()

    tsc = ExitStack()
    tT = SB(tsc, "tT", [128, KT * S], BF16)
    asc = ExitStack()
    v_aug = SB(asc, "v_aug", [128, NT * 516], BF16)
    biasq8 = SB(asc, "biasq8", [128, KT], F32)
    biask8 = SB(asc, "biask8", [128, KT], F32)

    cmb = ExitStack()
    Wv_sb = SB(cmb, "Wv_sb", [128, KT * H], BF16)
    Wqk_sb = SB(cmb, "Wqk_sb", [128, KT * H], BF16)
    bvrow = SB(cmb, "bvrow", [1, H], F32)
    bvprow = SB(cmb, "bvprow", [1, H], F32)
    beffrow = SB(cmb, "beffrow", [1, 512], F32)
    bveff_c = SB(cmb, "bveff_c", [128, 512], F32)
    vA_scr = SB(cmb, "vA_scr", [128, 2 * 512], BF16)
    nrow2 = SB(cmb, "nrow2", [1, 1024], F32)
    qstg = SB(cmb, "qstg", [128, 4 * 512], BF16)
    p_tr = [PS(cmb, "ptr0", [128, 1024], BF16), PS(cmb, "ptr1", [128, 1024], BF16)]
    p_v = [PS(cmb, "pv%d" % i, [128, 512]) for i in range(4)]
    p_dn = [PS(cmb, "pdn0", [128, 512]), PS(cmb, "pdn1", [128, 512])]

    vfence = fence()
    p.dma("sync", bvrow[0:1, :], bv_d[None, :], waits=vfence, inc="sx")
    nbv = p.n("sx")
    p.dma("gpsimd", Wv_sb[:, :].rearrange("p (k n) -> p k n", k=KT),
          Wv_d[:, :].rearrange("(k p) n -> p k n", p=128), waits=vfence, inc="swv")
    wvn = ("swv", p.n("swv"))
    p.dma("gpsimd", Wqk_sb[:, :].rearrange("p (k n) -> p k n", k=KT),
          Wq_d[:, :].rearrange("(k p) n -> p k n", p=128), waits=vfence, inc="swq")
    wqkn = ("swq", p.n("swq"))
    p.emit("gpsimd", lambda e: e.memset(
        v_aug[:, :].rearrange("p (t j c) -> p (t j) c", t=NT, j=4)[:, :, 128:129], 1.0),
        waits=vfence, inc="pl")
    pl_vones = p.n("pl")

    for ch in range(2):
        for kt in range(KT):
            p.emit("tensor", lambda e, kt=kt, ch=ch: e.matmul(
                p_v[0][0:1, 0:512], saT_b[:, kt:kt + 1],
                Wv_sb[:, kt * H + ch * 512: kt * H + (ch + 1) * 512],
                start=(kt == 0), stop=(kt == KT - 1), skip_group_check=True),
                waits=[wvn, ("act", a_modT), ("dve", p.n("dve"))]
                if kt == 0 else None,
                inc="pe" if kt == KT - 1 else None)
        pn = p.n("pe")
        p.emit("vector", lambda e, ch=ch: e.tensor_copy(
            nrow2[0:1, ch * 512:(ch + 1) * 512], p_v[0][0:1, 0:512]),
            waits=[("pe", pn)], inc="dve")
    dn = p.n("dve")
    p.emit("vector", lambda e: e.tensor_add(bvprow[0:1, :], nrow2[0:1, :], bvrow[0:1, :]),
           waits=[("dve", dn), ("sx", nbv)], inc="dve")
    dn = p.n("dve")
    for j in range(4):
        p.emit("vector", lambda e, j=j: e.scalar_tensor_tensor(
            beffrow[0:1, j * 128:(j + 1) * 128],
            bvprow[0:1, (4 + j) * 128:(5 + j) * 128], lamneg[0:1, 0:1],
            bvprow[0:1, j * 128:(j + 1) * 128], ALU.mult, ALU.add),
            waits=[("dve", dn), ("act", a_lam)], inc="dve")
    dn = p.n("dve")
    p.emit("tensor", lambda e: e.matmul(p_v[1][:, :], ones_row_f[0:1, :], beffrow[0:1, :],
                                        start=True, stop=True, skip_group_check=True),
           waits=[("dve", dn), ("pe", p.n("pe"))], inc="pe")
    pn = p.n("pe")
    p.emit("scalar", lambda e: e.activation(bveff_c[:, :], p_v[1][:, :], AF.Copy),
           waits=[("pe", pn)], inc="act")
    a_bveff = p.n("act")
    for kt in range(KT):
        p.emit("vector", lambda e, kt=kt: e.tensor_scalar_mul(
            Wv_sb[:, kt * H:(kt + 1) * H], Wv_sb[:, kt * H:(kt + 1) * H],
            ca1T[:, kt:kt + 1]),
            waits=[("pe", pn), ("dve", d_ca1)] if kt == 0 else None,
            inc="dve" if kt == KT - 1 else None)
    d_wv = p.n("dve")
    for ch in range(2):
        for kt in range(KT):
            p.emit("tensor", lambda e, kt=kt, ch=ch: e.matmul(
                p_dn[0][0:1, 0:512], saT_b[:, kt:kt + 1],
                Wqk_sb[:, kt * H + ch * 512: kt * H + (ch + 1) * 512],
                start=(kt == 0), stop=(kt == KT - 1), skip_group_check=True),
                waits=[wqkn, ("act", a_modT), ("dve", p.n("dve"))]
                if kt == 0 else None,
                inc="pe" if kt == KT - 1 else None)
        pn = p.n("pe")
        p.emit("vector", lambda e, ch=ch: e.tensor_copy(
            nrow2[0:1, ch * 512:(ch + 1) * 512], p_dn[0][0:1, 0:512]),
            waits=[("pe", pn)], inc="dve")
    dn = p.n("dve")
    for dt_ in range(KT):
        p.emit("tensor", lambda e, dt_=dt_: e.matmul(
            p_dn[1][:, dt_:dt_ + 1], nrow2[0:1, dt_ * 128:(dt_ + 1) * 128],
            ones_row_f[0:1, 0:1], start=True, stop=True, skip_group_check=True),
            waits=[("dve", dn)], inc="pe" if dt_ == KT - 1 else None)
    pn = p.n("pe")
    p.emit("vector", lambda e: e.tensor_add(biasq8[:, :], p_dn[1][:, 0:KT], bqT[:, :]),
           waits=[("pe", pn)], inc="dve")
    d_bq8 = p.n("dve")
    for kt in range(KT):
        p.emit("vector", lambda e, kt=kt: e.tensor_scalar_mul(
            Wqk_sb[:, kt * H:(kt + 1) * H], Wqk_sb[:, kt * H:(kt + 1) * H],
            ca1T[:, kt:kt + 1]),
            waits=[("pe", pn), ("dve", d_ca1)] if kt == 0 else None,
            inc="dve" if kt == KT - 1 else None)
    d_wqk = p.n("dve")

    ln1_dve, ln1_act, tr_pe, tr_act = {}, {}, {}, {}
    l1_ms, l1_mu, l1_rs, l1_xn = {}, {}, {}, {}
    v_comb = {}
    pd_dma = {}
    dn_evac = {}
    nchunk = 0

    def emit_l1_s1(st):
        xv = x_tiles[:, (st % 4) * H:(st % 4 + 1) * H]
        dep = []
        if st >= 4:
            dep += [("dve", ln1_dve.get(st - 4, 0)), ("act", ln1_act.get(st - 4, 0))]
        xn = p.dma("sync", xv, x_d[st * 128:(st + 1) * 128, :], waits=dep,
                   inc="sx%d" % (st % 4))
        p.emit("scalar", lambda e, st=st, xv=xv: e.activation(
            deadout[:, :], xv, AF.Copy, accum_out=stats["mu1s"][:, st:st + 1]),
            waits=[xn, ("act", last_dead[0])] + ([("dve", d_ident)] if st == 0 else []),
            inc="act")
        l1_ms[st] = p.n("act")
        last_dead[0] = l1_ms[st]
        p.emit("scalar", lambda e, st=st, xv=xv: e.activation(
            deadout[:, :], xv, AF.Square, accum_out=stats["ssq1"][:, st:st + 1]),
            waits=[("act", last_dead[0])], inc="act")
        ln1_act[st] = p.n("act")
        last_dead[0] = ln1_act[st]
        l1_xn[st] = xn

    def emit_l1_s2(st):
        p.emit("vector", lambda e, st=st: e.tensor_mul(
            stats["m1t"][:, st:st + 1], stats["mu1s"][:, st:st + 1],
            stats["mu1s"][:, st:st + 1]), waits=[("act", l1_ms[st])], inc="dve")
        t1n = p.n("dve")
        p.emit("vector", lambda e, st=st: e.scalar_tensor_tensor(
            stats["v1t"][:, st:st + 1], stats["m1t"][:, st:st + 1], 1.0 / H,
            stats["ssq1"][:, st:st + 1], ALU.mult, ALU.subtract),
            waits=[("act", ln1_act[st]), ("dve", t1n)], inc="dve")
        dn = p.n("dve")
        p.emit("vector", lambda e, st=st: e.tensor_scalar_mul(
            stats["mu1"][:, st:st + 1], stats["mu1s"][:, st:st + 1], 1.0 / H),
            inc="dve")
        l1_mu[st] = p.n("dve")
        p.emit("scalar", lambda e, st=st: e.activation(
            stats["v1t"][:, st:st + 1], stats["v1t"][:, st:st + 1], AF.Ln,
            scale=-1.0 / H, bias=eps_col[:, 0:1]),
            waits=[("dve", dn), ("pl", pl_const)], inc="act")
        lnn = p.n("act")
        p.emit("scalar", lambda e, st=st: e.activation(
            stats["rstd1"][:, st:st + 1], stats["v1t"][:, st:st + 1], AF.Exp, scale=-0.5),
            waits=[("act", lnn)], inc="act")
        l1_rs[st] = p.n("act")

    def emit_l1_s3(st):
        pp = st % 2
        xv = x_tiles[:, (st % 4) * H:(st % 4 + 1) * H]
        hv = h_bf[:, pp * H:(pp + 1) * H]
        hdep = [("act", l1_rs[st]), ("dve", l1_mu[st])]
        if st >= 2:
            hdep.append(("pe", tr_pe[st - 2]))
        p.emit("vector", lambda e, xv=xv, hv=hv, st=st: e.tensor_scalar(
            hv, xv, stats["mu1"][:, st:st + 1], stats["rstd1"][:, st:st + 1],
            ALU.subtract, ALU.mult), waits=hdep, inc="dve")
        ln1_dve[st] = p.n("dve")
        trdep = [("dve", ln1_dve[st])]
        if st >= 2:
            trdep.append(tr_act[st - 2])
        for kt in range(KT):
            p.emit("tensor", lambda e, kt=kt, pp=pp, hv=hv: e.transpose(
                p_tr[pp][:, kt * 128:(kt + 1) * 128],
                hv.rearrange("p (k c) -> p k c", k=KT)[:, kt, :], ident_b[:, :]),
                waits=trdep if kt == 0 else None,
                inc="pe" if kt == KT - 1 else None)
        tr_pe[st] = p.n("pe")
        p.emit("vector", lambda e, st=st, pp=pp: e.tensor_copy(
            tT[:, :].rearrange("p (k s) -> p k s", k=KT)[:, :, st * 128:(st + 1) * 128],
            p_tr[pp][:, :].rearrange("p (k s) -> p k s", k=KT)),
            waits=[("pe", tr_pe[st])], inc="dve")
        tr_act[st] = ("dve", p.n("dve"))

    def emit_v(st):
        pp = st % 2
        psA, psB = p_v[2 * pp], p_v[2 * pp + 1]
        dep = [tr_act[st], ("dve", d_wv), ("act", a_bveff)]
        if st >= 2:
            dep.append(("dve", v_comb[st - 2]))
        for kt in range(KT):
            hT_t = tT[:, kt * S + st * 128: kt * S + (st + 1) * 128]
            p.emit("tensor", lambda e, psA=psA, hT_t=hT_t, kt=kt: e.matmul(
                psA[:, :], hT_t, Wv_sb[:, kt * H: kt * H + 512],
                start=(kt == 0), stop=(kt == KT - 1), skip_group_check=True),
                waits=dep if kt == 0 else None)
            p.emit("tensor", lambda e, psB=psB, hT_t=hT_t, kt=kt: e.matmul(
                psB[:, :], hT_t, Wv_sb[:, kt * H + 512: (kt + 1) * H],
                start=(kt == 0), stop=(kt == KT - 1), skip_group_check=True),
                inc="pe" if kt == KT - 1 else None)
        pn2 = p.n("pe")
        p.emit("scalar", lambda e, psA=psA, pp=pp: e.activation(
            vA_scr[:, pp * 512:(pp + 1) * 512], psA[:, :], AF.Copy),
            waits=[("pe", pn2)], inc="act")
        acn = p.n("act")
        p.emit("vector", lambda e, psB=psB, pp=pp: e.scalar_tensor_tensor(
            vA_scr[:, pp * 512:(pp + 1) * 512], psB[:, :], lamneg[:, 0:1],
            vA_scr[:, pp * 512:(pp + 1) * 512], ALU.mult, ALU.add),
            waits=[("act", acn), ("act", a_bveff), ("pl", pl_vones)], inc="dve")
        sttn = p.n("dve")
        p.emit("vector", lambda e, st=st, pp=pp: e.tensor_tensor(
            v_aug[:, st * 516:(st + 1) * 516].rearrange("p (j c) -> p j c", j=4)[:, :, 0:128],
            vA_scr[:, pp * 512:(pp + 1) * 512].rearrange("p (j c) -> p j c", j=4),
            bveff_c[:, :].rearrange("p (j c) -> p j c", j=4), ALU.add),
            waits=[("dve", sttn)], inc="dve")
        v_comb[st] = p.n("dve")

    def emit_dense_chunk(ti, dt_, qi, wbuf, scr, bcol, dscale, ev_eng):
        nonlocal_ns = emit_dense_chunk
        n = nonlocal_ns.nchunk
        bank = n % 2
        slot = n % 4
        mmw = [("dve", dscale), tr_act[min(4 * qi + 3, NT - 1)]]
        if n >= 2:
            mmw.append(dn_evac[n - 2])
        for kt in range(KT):
            p.emit("tensor", lambda e, bank=bank, wbuf=wbuf, kt=kt, dt_=dt_, qi=qi: e.matmul(
                p_dn[bank][:, :], wbuf[:, kt * H + dt_ * 128: kt * H + (dt_ + 1) * 128],
                tT[:, kt * S + qi * 512: kt * S + (qi + 1) * 512],
                start=(kt == 0), stop=(kt == KT - 1), skip_group_check=True),
                waits=mmw if kt == 0 else None,
                inc="pe" if kt == KT - 1 else None)
        pn = p.n("pe")
        sdep = [pd_dma[nonlocal_ns.slot_dma[slot]]] if slot in nonlocal_ns.slot_dma else []
        if ev_eng == "scalar":
            p.emit("scalar", lambda e, bank=bank, slot=slot, bcol=bcol, dt_=dt_: e.activation(
                qstg[:, slot * 512:(slot + 1) * 512], p_dn[bank][:, :], AF.Identity,
                bias=bcol[:, dt_:dt_ + 1]),
                waits=[("pe", pn)] + sdep, inc="act")
            evd = ("act", p.n("act"))
        else:
            p.emit("vector", lambda e, bank=bank, slot=slot, bcol=bcol, dt_=dt_: e.tensor_scalar_add(
                qstg[:, slot * 512:(slot + 1) * 512], p_dn[bank][:, :], bcol[:, dt_:dt_ + 1]),
                waits=[("pe", pn)] + sdep, inc="dve")
            evd = ("dve", p.n("dve"))
        dn_evac[n] = evd
        pd_dma[(ti, dt_, qi)] = p.dma(
            "gpsimd", scr[dt_, :, qi * 512:(qi + 1) * 512],
            qstg[:, slot * 512:(slot + 1) * 512], waits=[evd], inc="sd%d" % slot)
        nonlocal_ns.slot_dma[slot] = (ti, dt_, qi)
        nonlocal_ns.nchunk += 1

    emit_dense_chunk.nchunk = 0
    emit_dense_chunk.slot_dma = {}

    for st in range(NT + 2):
        if st < NT:
            emit_l1_s1(st)
        if 1 <= st <= NT:
            emit_l1_s2(st - 1)
        if 2 <= st <= NT + 1:
            emit_l1_s3(st - 2)
        if st >= 4:
            emit_v(st - 4)
        if st >= 7 and ((st - 7) % 4 == 0) and (st - 7) // 4 < 7:
            qi = (st - 7) // 4
            for dt_ in range(KT):
                emit_dense_chunk(0, dt_, qi, Wqk_sb, qT_scr, biasq8, d_wqk, "scalar")
    for st in range(NT - 2, NT):
        emit_v(st)
    for dt_ in range(KT):
        emit_dense_chunk(0, dt_, 7, Wqk_sb, qT_scr, biasq8, d_wqk, "scalar")

    kdep = [("pe", p.n("pe"))]
    p.dma("gpsimd", Wqk_sb[:, :].rearrange("p (k n) -> p k n", k=KT),
          Wk_d[:, :].rearrange("(k p) n -> p k n", p=128), waits=kdep, inc="swg")
    wqkn = p.n("swg")
    for ch in range(2):
        for kt in range(KT):
            p.emit("tensor", lambda e, kt=kt, ch=ch: e.matmul(
                p_dn[0][0:1, 0:512], saT_b[:, kt:kt + 1],
                Wqk_sb[:, kt * H + ch * 512: kt * H + (ch + 1) * 512],
                start=(kt == 0), stop=(kt == KT - 1), skip_group_check=True),
                waits=[("swg", wqkn), ("dve", p.n("dve")),
                       dn_evac.get(emit_dense_chunk.nchunk - 2, ("pl", 0)),
                       dn_evac.get(emit_dense_chunk.nchunk - 1, ("pl", 0))]
                if kt == 0 else None,
                inc="pe" if kt == KT - 1 else None)
        pn = p.n("pe")
        p.emit("vector", lambda e, ch=ch: e.tensor_copy(
            nrow2[0:1, ch * 512:(ch + 1) * 512], p_dn[0][0:1, 0:512]),
            waits=[("pe", pn)], inc="dve")
    dn = p.n("dve")
    for dt_ in range(KT):
        p.emit("tensor", lambda e, dt_=dt_: e.matmul(
            p_dn[1][:, dt_:dt_ + 1], nrow2[0:1, dt_ * 128:(dt_ + 1) * 128],
            ones_row_f[0:1, 0:1], start=True, stop=True, skip_group_check=True),
            waits=[("dve", dn)], inc="pe" if dt_ == KT - 1 else None)
    pn = p.n("pe")
    p.emit("vector", lambda e: e.tensor_add(biask8[:, :], p_dn[1][:, 0:KT], bkT[:, :]),
           waits=[("pe", pn)], inc="dve")
    for kt in range(KT):
        p.emit("vector", lambda e, kt=kt: e.tensor_scalar_mul(
            Wqk_sb[:, kt * H:(kt + 1) * H], Wqk_sb[:, kt * H:(kt + 1) * H],
            ca1T[:, kt:kt + 1]),
            waits=[("pe", pn), ("dve", d_ca1)] if kt == 0 else None,
            inc="dve" if kt == KT - 1 else None)
    d_wk2 = p.n("dve")
    for dt_ in range(KT):
        for qi in range(KT):
            emit_dense_chunk(1, dt_, qi, Wqk_sb, kT_scr, biask8, d_wk2, "vector")
    n_pd_all = [("sd%d" % s_, p.n("sd%d" % s_)) for s_ in range(4)]
    n_pd = {}
    for hh in range(HEADS):
        n_pd[(0, hh)] = [pd_dma[(0, hh, qi)] for qi in range(KT)]
        n_pd[(1, hh)] = [pd_dma[(1, hh, qi)] for qi in range(KT)]
    cmb.close()

    qT2 = SB(asc, "qT2", [128, 2 * S], BF16)
    kT2 = SB(asc, "kT2", [128, 2 * S], BF16)
    sq_scr = SB(asc, "sq_scr", [128, 2 * 1024], BF16)
    expT = SB(asc, "expT", [128, 3 * QCW], BF16)
    ao_ch = SB(asc, "ao_ch", [128, 2 * QCW], BF16)
    rec8 = SB(asc, "rec8", [128, 8], F32)
    qk_ps = [PS(asc, "qk0", [128, 1024]), PS(asc, "qk1", [128, 1024])]
    av_ps = [PS(asc, "avA", [128, 387]), PS(asc, "avB", [128, 387]),
             PS(asc, "avC", [128, 258])]
    p_pr = PS(asc, "p_pr", [128, 512])

    AVJ = [(0, 0), (0, 129), (0, 258), (1, 0), (1, 129), (1, 258), (2, 0), (2, 129)]

    gidx = 0
    exp_act = {}
    evac_dve = {}
    evac_banks = {}
    aoch_dma = {}
    slot_proj_pe = {0: 0, 1: 0}
    av_bank_pe = {}
    proj_dve = {}
    bnd_act = {}

    def prep_units(h):
        sl = h % 2
        qT = qT2[:, sl * S:(sl + 1) * S]
        kTs = kT2[:, sl * S:(sl + 1) * S]
        wdep = ([("pe", slot_proj_pe[sl])] if h > 1
                else fence(dma=("sd0", "sd1", "sd2", "sd3")))
        qsem = "sqk%d" % sl
        p.dma("gpsimd", qT, qT_scr[h, :, :],
              waits=wdep + n_pd[(0, h)], inc=qsem)
        p.dma("gpsimd", kTs, kT_scr[h, :, :],
              waits=n_pd[(1, h)], inc=qsem)
        proj_dve[h] = (qsem, p.n(qsem))
        yield
        rowcp = 0
        rmaxn = [0]
        for t, src_ in enumerate((qT, kTs)):
            for c4 in range(4):
                sq_half = (t * 4 + c4) % 2
                sqdep = [proj_dve[h]]
                if (t * 4 + c4) >= 2:
                    sqdep.append(("pe", p.n("pe")))
                p.emit("vector", lambda e, src_=src_, c4=c4, sq_half=sq_half: e.tensor_mul(
                    sq_scr[:, sq_half * 1024:(sq_half + 1) * 1024],
                    src_[:, c4 * 1024:(c4 + 1) * 1024],
                    src_[:, c4 * 1024:(c4 + 1) * 1024]), waits=sqdep, inc="dve")
                sqn = ("dve", p.n("dve"))
                yield
                for c2 in range(2):
                    cid = c4 * 2 + c2
                    p.emit("tensor", lambda e, c2=c2, sq_half=sq_half: e.matmul(
                        p_pr[0:1, 0:512], ones_col_b[:, 0:1],
                        sq_scr[:, sq_half * 1024 + c2 * 512: sq_half * 1024 + (c2 + 1) * 512],
                        start=True, stop=True, skip_group_check=True),
                        waits=[sqn, ("dve", rowcp)], inc="pe")
                    pn = p.n("pe")
                    p.emit("vector", lambda e: e.tensor_copy(
                        nrow[0:1, 0:512], p_pr[0:1, 0:512]),
                        waits=[("pe", pn), ("dve", rmaxn[0])], inc="dve")
                    an = p.n("dve")
                    rowcp = an
                    p.emit("vector", lambda e, t=t, cid=cid: e.reduce_max(
                        bnd16[0:1, t * 8 + cid: t * 8 + cid + 1], nrow[0:1, 0:512],
                        axis=AXX), waits=[("dve", an)], inc="dve")
                    rmaxn[0] = p.n("dve")
                    yield
            p.emit("vector", lambda e, t=t: e.reduce_max(
                bnd[0:1, t:t + 1], bnd16[0:1, t * 8:(t + 1) * 8], axis=AXX),
                waits=[("dve", p.n("dve"))], inc="dve")
        dn = p.n("dve")
        p.emit("vector", lambda e: e.tensor_mul(bnd[0:1, 2:3], bnd[0:1, 0:1], bnd[0:1, 1:2]),
               waits=[("dve", dn)], inc="dve")
        dn = p.n("dve")
        p.emit("scalar", lambda e: e.activation(bnd[0:1, 3:4], bnd[0:1, 2:3], AF.Ln),
               waits=[("dve", dn)], inc="act")
        lnn = p.n("act")
        p.emit("scalar", lambda e: e.activation(bnd[0:1, 4:5], bnd[0:1, 3:4], AF.Exp,
                                                scale=0.5, bias=nhl_col[0:1, 0:1]),
               waits=[("act", lnn)], inc="act")
        an = p.n("act")
        p.emit("tensor", lambda e: e.matmul(
            p_pr[:, 400:401], ones_row_f[0:1, :], bnd[0:1, 4:5],
            start=True, stop=True, skip_group_check=True),
            waits=[("act", an), ("dve", p.n("dve"))], inc="pe")
        pn = p.n("pe")
        p.emit("scalar", lambda e, h=h: e.activation(
            Bneg[:, h:h + 1], p_pr[:, 400:401], AF.Copy, scale=-1.0),
            waits=[("pe", pn)], inc="act")
        bnd_act[h] = p.n("act")
        yield

    for _ in prep_units(0):
        pass

    for h in range(HEADS):
        sl = h % 2
        qT = qT2[:, sl * S:(sl + 1) * S]
        kTs = kT2[:, sl * S:(sl + 1) * S]
        nxt = prep_units(h + 1) if h + 1 < HEADS else iter(())

        def emit_qk(kt, g, qc):
            pp = g % 2
            qkdep = [("act", bnd_act[h]), proj_dve[h], ("dve", v_comb[NT - 1])]
            if g >= 2:
                qkdep.append(("act", exp_act[g - 2]))
            for half in range(2):
                p.emit("tensor", lambda e, pp=pp, kt=kt, qc=qc, half=half, qT=qT, kTs=kTs: e.matmul(
                    qk_ps[pp][:, half * 512:(half + 1) * 512],
                    kTs[:, kt * 128:(kt + 1) * 128],
                    qT[:, qc * QCW + half * 512: qc * QCW + (half + 1) * 512],
                    start=True, stop=True, skip_group_check=True),
                    waits=qkdep if half == 0 else None,
                    inc="pe" if half == 1 else None)
            qkn = p.n("pe")
            ppe = g % 3
            p.emit("scalar", lambda e, pp=pp, ppe=ppe, h=h: e.activation(
                expT[:, ppe * QCW:(ppe + 1) * QCW], qk_ps[pp][:, :], AF.Exp,
                scale=ISQ, bias=Bneg[:, h:h + 1]),
                waits=[("pe", qkn), ("act", bnd_act[h])], inc="act")
            exp_act[g] = p.n("act")

        def emit_av(kt, g, qc):
            ppe = g % 3
            avdep = [("act", exp_act[g])]
            m_prev = (h * NQC + qc) - 1
            for j in range(8):
                ti, off = AVJ[j]
                fib = (kt == 0) and (off == 0)
                w_ = None
                if j == 0:
                    w_ = list(avdep)
                if kt == 0 and fib and g >= NT:
                    w_ = (w_ or []) + [("dve", evac_banks[m_prev][ti])]
                p.emit("tensor", lambda e, ti=ti, off=off, j=j, kt=kt, ppe=ppe, h=h, fib=fib: e.matmul(
                    av_ps[ti][:, off:off + 129],
                    expT[:, ppe * QCW + j * 128: ppe * QCW + (j + 1) * 128],
                    v_aug[:, kt * 516 + (h % 4) * 129: kt * 516 + (h % 4) * 129 + 129],
                    start=fib, stop=(kt == NT - 1), skip_group_check=True),
                    waits=w_,
                    inc="pe" if (kt == NT - 1 and j in (2, 5, 7)) else None)
                if kt == NT - 1 and j in (2, 5, 7):
                    av_bank_pe[ti] = p.n("pe")

        for qc in range(NQC):
            g0 = gidx
            for kt in range(NT):
                emit_qk(kt, g0 + kt, qc)
                if kt >= 2:
                    emit_av(kt - 2, g0 + kt - 2, qc)
                gk = qc * NT + kt
                if gk == 0 or gk >= 16:
                    next(nxt, None)
            emit_av(NT - 2, g0 + NT - 2, qc)
            emit_av(NT - 1, g0 + NT - 1, qc)
            gidx = g0 + NT
            m = h * NQC + qc
            ppq = qc % 2
            evdep = []
            if m >= 2:
                evdep.append(aoch_dma[m - 2])
            p.emit("vector", lambda e: e.reciprocal(
                rec8[:, 0:3], av_ps[0][:, :].rearrange("p (j c) -> p j c", j=3)[:, :, 128]),
                waits=evdep + [("pe", av_bank_pe[0])])
            p.emit("vector", lambda e: e.reciprocal(
                rec8[:, 3:6], av_ps[1][:, :].rearrange("p (j c) -> p j c", j=3)[:, :, 128]),
                waits=[("pe", av_bank_pe[1])])
            p.emit("vector", lambda e: e.reciprocal(
                rec8[:, 6:8], av_ps[2][:, :].rearrange("p (j c) -> p j c", j=2)[:, :, 128]),
                waits=[("pe", av_bank_pe[2])], inc="dve")
            rcn = p.n("dve")
            bank_free = {}
            for j in range(8):
                ti, off = AVJ[j]
                p.emit("vector", lambda e, ti=ti, off=off, j=j, ppq=ppq: e.tensor_scalar_mul(
                    ao_ch[:, ppq * QCW + j * 128: ppq * QCW + (j + 1) * 128],
                    av_ps[ti][:, off:off + 128], rec8[:, j:j + 1]),
                    waits=[("dve", rcn)] if j == 0 else None,
                    inc="dve" if j in (2, 5, 7) else None)
                if j in (2, 5, 7):
                    bank_free[ti] = p.n("dve")
            evac_dve[m] = p.n("dve")
            evac_banks[m] = bank_free
            aoch_dma[m] = p.dma(
                "sync",
                ao_scr[qc * QCW:(qc + 1) * QCW, h * 128:(h + 1) * 128]
                .rearrange("(j p) d -> p j d", p=128),
                ao_ch[:, ppq * QCW:(ppq + 1) * QCW].rearrange("p (j d) -> p j d", j=8),
                waits=[("dve", evac_dve[m])], inc="sas%d" % (m % 2))
        for _ in nxt:
            pass
        slot_proj_pe[sl] = p.n("pe")
    a_attn_pe_end = p.n("pe")
    n_ao_stores = [("sas0", p.n("sas0")), ("sas1", p.n("sas1"))]

    asc.close()
    aosc = ExitStack()
    ao_t = SB(aosc, "ao_t", [128, 8 * H], BF16)
    p_tr2 = [PS(aosc, "ptr2a", [128, 1024], BF16), PS(aosc, "ptr2b", [128, 1024], BF16)]

    ao_dve, ao_act, ao_trpe = {}, {}, {}
    ao_ms, ao_mu, ao_rs = {}, {}, {}

    def ao_s1(st):
        av = ao_t[:, (st % 8) * H:(st % 8 + 1) * H]
        dep = list(n_ao_stores)
        if st >= 8:
            dep += [("dve", ao_dve.get(st - 8, 0)), ("act", ao_act.get(st - 8, 0))]
        ldn = p.dma("sync", av, ao_scr[st * 128:(st + 1) * 128, :], waits=dep,
                    inc="sal%d" % (st % 8))
        p.emit("vector", lambda e, st=st, av=av: e.reduce_sum(
            stats["amus"][:, st:st + 1], av, axis=AXX),
            waits=[ldn], inc="dve")
        ao_ms[st] = p.n("dve")
        p.emit("scalar", lambda e, st=st, av=av: e.activation(
            deadout[:, :], av, AF.Square, accum_out=stats["assq"][:, st:st + 1]),
            waits=[ldn, ("act", last_dead[0])], inc="act")
        ao_act[st] = p.n("act")
        last_dead[0] = ao_act[st]

    def ao_s2(st):
        p.emit("vector", lambda e, st=st: e.tensor_mul(
            stats["amt"][:, st:st + 1], stats["amus"][:, st:st + 1],
            stats["amus"][:, st:st + 1]), waits=[("dve", ao_ms[st])], inc="dve")
        t1n = p.n("dve")
        p.emit("vector", lambda e, st=st: e.scalar_tensor_tensor(
            stats["avt"][:, st:st + 1], stats["amt"][:, st:st + 1], 1.0 / H,
            stats["assq"][:, st:st + 1], ALU.mult, ALU.subtract),
            waits=[("act", ao_act[st]), ("dve", t1n)], inc="dve")
        dn = p.n("dve")
        p.emit("vector", lambda e, st=st: e.tensor_scalar_mul(
            stats["amu"][:, st:st + 1], stats["amus"][:, st:st + 1], 1.0 / H),
            inc="dve")
        ao_mu[st] = p.n("dve")
        p.emit("scalar", lambda e, st=st: e.activation(
            stats["avt"][:, st:st + 1], stats["avt"][:, st:st + 1], AF.Ln,
            scale=-1.0 / H, bias=eps_col[:, 0:1]),
            waits=[("dve", dn), ("pl", pl_const)], inc="act")
        lnn = p.n("act")
        p.emit("scalar", lambda e, st=st: e.activation(
            stats["arstd"][:, st:st + 1], stats["avt"][:, st:st + 1], AF.Exp, scale=-0.5),
            waits=[("act", lnn)], inc="act")
        ao_rs[st] = p.n("act")

    def ao_s3(st):
        pp = st % 2
        av = ao_t[:, (st % 8) * H:(st % 8 + 1) * H]
        hv = h_bf[:, pp * H:(pp + 1) * H]
        hdep = [("act", ao_rs[st]), ("dve", ao_mu[st])]
        if st >= 2:
            hdep.append(("pe", ao_trpe[st - 2]))
        p.emit("vector", lambda e, av=av, hv=hv, st=st: e.tensor_scalar(
            hv, av, stats["amu"][:, st:st + 1], stats["arstd"][:, st:st + 1],
            ALU.subtract, ALU.mult), waits=hdep, inc="dve")
        ao_dve[st] = p.n("dve")
        trdep = [("dve", ao_dve[st]), ("pe", a_attn_pe_end)]
        if st >= 2:
            trdep.append(("act", ao_act.get(st - 2, 0)))
        for kt in range(KT):
            p.emit("tensor", lambda e, kt=kt, pp=pp, hv=hv: e.transpose(
                p_tr2[pp][:, kt * 128:(kt + 1) * 128],
                hv.rearrange("p (k c) -> p k c", k=KT)[:, kt, :], ident_b[:, :]),
                waits=trdep if kt == 0 else None,
                inc="pe" if kt == KT - 1 else None)
        ao_trpe[st] = p.n("pe")
        p.emit("scalar", lambda e, st=st, pp=pp: e.activation(
            tT[:, :].rearrange("p (k s) -> p k s", k=KT)[:, :, st * 128:(st + 1) * 128],
            p_tr2[pp][:, :].rearrange("p (k s) -> p k s", k=KT), AF.Copy),
            waits=[("pe", ao_trpe[st])], inc="act")
        ao_act[st] = p.n("act")

    for st in range(NT + 2):
        if st < NT:
            ao_s1(st)
        if 1 <= st <= NT:
            ao_s2(st - 1)
        if 2 <= st <= NT + 1:
            ao_s3(st - 2)
    aosc.close()

    osc = ExitStack()
    Wo_sb = SB(osc, "Wo_sb", [128, KT * H], BF16)
    gac_r = SB(osc, "gac_r", [128, H], F32)
    ga_eff_r = SB(osc, "ga_eff_r", [128, H], F32)
    gabo_r = SB(osc, "gabo_r", [128, H], F32)
    boe_r = SB(osc, "boe_r", [128, H], F32)
    boerow = SB(osc, "boerow", [1, H], F32)
    borow = SB(osc, "borow", [1, H], F32)
    x1_sb = SB(osc, "x1_sb", [128, 2 * H], F32)
    evac32 = SB(osc, "evac32", [128, 1024], F32)
    p_y = [PS(osc, "py%d" % i, [128, 512]) for i in range(4)]
    p_ob = PS(osc, "p_ob", [1, 1024])
    p_bc3 = PS(osc, "p_bc3", [128, 512])

    ofence = fence(dma=("sqk0", "sqk1", "sas0", "sas1", "sal0", "sal1",
                        "sal2", "sal3", "sal4", "sal5", "sal6", "sal7"))
    nbo = p.dma("sync", borow[0:1, :], bo_d[None, :], waits=ofence, inc="sb1")
    won = p.dma("gpsimd", Wo_sb[:, :].rearrange("p (k n) -> p k n", k=KT),
                Wo_d[:, :].rearrange("(k p) n -> p k n", p=128), waits=ofence, inc="swo")
    for kt in range(KT):
        for ch in range(2):
            p.emit("tensor", lambda e, kt=kt, ch=ch: e.matmul(
                p_ob[0:1, ch * 512:(ch + 1) * 512], lnbT_b[:, kt:kt + 1],
                Wo_sb[:, kt * H + ch * 512: kt * H + (ch + 1) * 512],
                start=(kt == 0), stop=(kt == KT - 1), skip_group_check=True),
                waits=[won, ("act", 1)],
                inc="pe" if (kt == KT - 1 and ch == 1) else None)
    pn = p.n("pe")
    p.emit("vector", lambda e: e.scalar_tensor_tensor(
        boerow[0:1, :], p_ob[0:1, :], 0.2, borow[0:1, :], ALU.mult, ALU.add),
        waits=[("pe", pn), nbo], inc="dve")
    d_boe = p.n("dve")
    for kt in range(KT):
        p.emit("vector", lambda e, kt=kt: e.tensor_scalar_mul(
            Wo_sb[:, kt * H:(kt + 1) * H], Wo_sb[:, kt * H:(kt + 1) * H],
            lnwT[:, kt:kt + 1]),
            waits=[("sx", n_small)] if kt == 0 else None,
            inc="dve" if kt == KT - 1 else None)
    d_wo = p.n("dve")
    last_bc = 0
    for dst, lhs, srcrow, wdep in ((gac_r, ones_row_f, gac_row, d_clamp),
                                   (ga_eff_r, p2_row_f, gac_row, d_clamp),
                                   (boe_r, ones_row_f, boerow, d_boe)):
        for ch in range(2):
            dep = [("dve", wdep), ("pl", pl_const)]
            if last_bc:
                dep.append(("act", last_bc))
            p.emit("tensor", lambda e, lhs=lhs, srcrow=srcrow, ch=ch: e.matmul(
                p_bc3[:, :], lhs[0:1, :], srcrow[0:1, ch * 512:(ch + 1) * 512],
                start=True, stop=True, skip_group_check=True),
                waits=dep, inc="pe")
            pn = p.n("pe")
            p.emit("scalar", lambda e, dst=dst, ch=ch: e.activation(
                dst[:, ch * 512:(ch + 1) * 512], p_bc3[:, :], AF.Copy),
                waits=[("pe", pn)], inc="act")
            last_bc = p.n("act")
    p.emit("vector", lambda e: e.tensor_mul(gabo_r[:, :], gac_r[:, :], boe_r[:, :]),
           waits=[("act", last_bc)], inc="dve")
    d_gabo = p.n("dve")

    o_dve, o_act, x1_dma, o_pl, o_sq = {}, {}, {}, {}, {}

    def emit_o_stats2(st):
        p.emit("vector", lambda e, st=st: e.tensor_mul(
            stats["m2t"][:, st:st + 1], stats["mu2s"][:, st:st + 1],
            stats["mu2s"][:, st:st + 1]), waits=[("act", o_sq[st])], inc="dve")
        t1n = p.n("dve")
        p.emit("vector", lambda e, st=st: e.scalar_tensor_tensor(
            stats["v2t"][:, st:st + 1], stats["m2t"][:, st:st + 1], 1.0 / H,
            stats["ssq2"][:, st:st + 1], ALU.mult, ALU.subtract),
            waits=[("dve", t1n)], inc="dve")
        dn = p.n("dve")
        p.emit("vector", lambda e, st=st: e.tensor_scalar_mul(
            stats["mu2"][:, st:st + 1], stats["mu2s"][:, st:st + 1], 1.0 / H),
            inc="dve")
        p.emit("scalar", lambda e, st=st: e.activation(
            stats["v2t"][:, st:st + 1], stats["v2t"][:, st:st + 1], AF.Ln,
            scale=-1.0 / H, bias=eps_col[:, 0:1]),
            waits=[("dve", dn), ("pl", pl_const)], inc="act")
        lnn = p.n("act")
        p.emit("scalar", lambda e, st=st: e.activation(
            stats["rstd2"][:, st:st + 1], stats["v2t"][:, st:st + 1], AF.Exp, scale=-0.5),
            waits=[("act", lnn)], inc="act")
        o_act[st] = p.n("act")

    for st in range(NT):
        pp = st % 2
        xv = x_tiles[:, (st % 4) * H:(st % 4 + 1) * H]
        x1v = x1_sb[:, pp * H:(pp + 1) * H]
        if st < 4:
            xdep = [("dve", ln1_dve[NT - 4 + st]), ("act", ln1_act[NT - 4 + st])]
        else:
            xdep = [("dve", o_dve[st - 4]), ("pl", o_pl[st - 4])]
        xn = p.dma("sync", xv, x_d[st * 128:(st + 1) * 128, :], waits=xdep,
                   inc="sx%d" % (st % 4))
        p.emit("gpsimd", lambda e, xv=xv: e.tensor_add(xv, xv, gabo_r[:, :]),
               waits=[xn, ("dve", d_gabo)], inc="pl")
        o_pl[st] = p.n("pl")
        mmdep = [("dve", d_wo), ("act", ao_act[st])]
        if st >= 2:
            mmdep.append(("dve", o_dve[st - 2]))
        for ch in range(2):
            pbank = p_y[2 * pp + ch]
            for kt in range(KT):
                aT_t = tT[:, kt * S + st * 128: kt * S + (st + 1) * 128]
                p.emit("tensor", lambda e, pbank=pbank, aT_t=aT_t, kt=kt, ch=ch: e.matmul(
                    pbank[:, :], aT_t, Wo_sb[:, kt * H + ch * 512: kt * H + (ch + 1) * 512],
                    start=(kt == 0), stop=(kt == KT - 1), skip_group_check=True),
                    waits=mmdep if (kt == 0 and ch == 0) else None,
                    inc="pe" if (kt == KT - 1 and ch == 1) else None)
        opn = p.n("pe")
        evdep = [("pe", opn), xn, ("dve", d_gabo)]
        if st >= 2:
            evdep.append(x1_dma[st - 2])
        if st >= 1:
            evdep.append(("dve", o_dve[st - 1]))
        for ch in range(2):
            pbank = p_y[2 * pp + ch]
            cs = slice(ch * 512, (ch + 1) * 512)
            p.emit("vector", lambda e, pbank=pbank, cs=cs: e.tensor_mul(
                evac32[:, cs], pbank[:, :], ga_eff_r[:, cs]),
                waits=evdep if ch == 0 else None, inc="dve")
        mn = p.n("dve")
        for ch in range(2):
            cs = slice(ch * 512, (ch + 1) * 512)
            p.emit("vector", lambda e, cs=cs, x1v=x1v, xv=xv: e.tensor_add(
                x1v[:, cs], evac32[:, cs], xv[:, cs]),
                waits=[("pl", o_pl[st]), ("dve", mn)] if ch == 0 else None,
                inc="dve" if ch == 1 else None)
        o_dve[st] = p.n("dve")
        p.emit("scalar", lambda e, st=st, x1v=x1v: e.activation(
            deadout[:, :], x1v, AF.Copy, accum_out=stats["mu2s"][:, st:st + 1]),
            waits=[("dve", o_dve[st]), ("act", last_dead[0])], inc="act")
        last_dead[0] = p.n("act")
        p.emit("scalar", lambda e, st=st, x1v=x1v: e.activation(
            deadout[:, :], x1v, AF.Square, accum_out=stats["ssq2"][:, st:st + 1]),
            waits=[("act", last_dead[0])], inc="act")
        o_sq[st] = p.n("act")
        last_dead[0] = o_sq[st]
        if st >= 1:
            x1_dma[st - 1] = p.dma(
                "gpsimd", x1_scr[(st - 1) * 128:st * 128, :],
                x1_sb[:, ((st - 1) % 2) * H:((st - 1) % 2 + 1) * H],
                waits=[("dve", o_dve[st - 1])], inc="sx1%d" % ((st - 1) % 2))
        if st == NT - 1:
            x1_dma[st] = p.dma("gpsimd", x1_scr[st * 128:(st + 1) * 128, :], x1v,
                               waits=[("dve", o_dve[st])], inc="sx1%d" % (st % 2))
        if st >= 1:
            emit_o_stats2(st - 1)
        if st == NT - 1:
            emit_o_stats2(st)
    tap("x1t0", x1_sb[:, 0:H], [("dve", o_dve[NT-2])])
    tap("mu2", stats["mu2"][:, :], [("act", o_act[NT-1])])
    tap("rstd2", stats["rstd2"][:, :], [("act", o_act[NT-1])])
    tap("gabo", gabo_r[:, :], [("dve", d_gabo)])
    tap("gaeff", ga_eff_r[:, :], [("dve", d_gabo)])
    osc.close()
    tsc.close()
    pre.close()

    fsc = ExitStack()
    w1_sb = SB(fsc, "w1_sb", [128, KT * FF], BF16)
    w2_sb = SB(fsc, "w2_sb", [128, MT * H], BF16)
    g2_sb = SB(fsc, "g2_sb", [128, MT * SBW], BF16)
    h2T = SB(fsc, "h2T", [128, 2 * KT * SBW], BF16)
    h2_bf = SB(fsc, "h2_bf", [128, 2 * H], BF16)
    gb_eff_r = SB(fsc, "gb_eff_r", [128, H], BF16)
    gbb2_r = SB(fsc, "gbb2_r", [128, H], F32)
    b1T = SB(fsc, "b1T", [128, MT], F32)
    b1pT = SB(fsc, "b1pT", [128, MT], F32)
    x1f = SB(fsc, "x1f", [128, 4 * H], F32)
    x2st = SB(fsc, "x2st", [128, 2 * H], F32)
    b2r_f = x2st
    p_u = [PS(fsc, "pu0", [128, 256]), PS(fsc, "pu1", [128, 256])]
    p_y2 = [PS(fsc, "py2%d" % i, [128, 512]) for i in range(4)]
    p_trF = PS(fsc, "p_trF", [128, 1024], BF16)
    p_fb = PS(fsc, "p_fb", [128, 512])

    ffence = fence(dma=("sx10", "sx11", "sx0", "sx1", "sx2", "sx3", "swo",
                        "sal0", "sal1", "sal2", "sal3", "sal4", "sal5",
                        "sal6", "sal7", "sas0", "sas1"))
    for kt in range(MT):
        p.dma("sync", b1T[:, kt:kt + 1], b1_d[kt * 128:(kt + 1) * 128][:, None],
              waits=ffence, inc="sb1")
    n_b1 = ("sb1", p.n("sb1"))
    for q4 in range(4):
        p.dma("gpsimd",
              w1_sb[:, :].rearrange("p (k f) -> p k f", k=KT)[:, 2 * q4:2 * q4 + 2, :],
              w1_d[:, :].rearrange("(k p) f -> p k f", p=128)[:, 2 * q4:2 * q4 + 2, :],
              waits=ffence, inc="sw1")
    w1n = ("sw1", p.n("sw1"))
    for q4 in range(4):
        p.dma("gpsimd",
              w2_sb[:, :].rearrange("p (k f) -> p k f", k=MT)[:, 8 * q4:8 * q4 + 8, :],
              w2_d[:, :].rearrange("(k p) f -> p k f", p=128)[:, 8 * q4:8 * q4 + 8, :],
              waits=ffence, inc="sw2")
    w2n = ("sw2", p.n("sw2"))

    last_bc = 0
    first_bc = True
    for dst, srcrow, dt_ in ((gb_eff_r, gbc_row, BF16), (gbb2_r, gbc_row, F32),
                             (b2r_f, b2row, F32)):
        for ch in range(2):
            dep = [("dve", d_clamp), ("sx", n_small)]
            if first_bc:
                dep += ffence
                first_bc = False
            if last_bc:
                dep.append(("act", last_bc))
            p.emit("tensor", lambda e, srcrow=srcrow, ch=ch: e.matmul(
                p_fb[:, :], ones_row_f[0:1, :], srcrow[0:1, ch * 512:(ch + 1) * 512],
                start=True, stop=True, skip_group_check=True),
                waits=dep, inc="pe")
            pn = p.n("pe")
            p.emit("scalar", lambda e, dst=dst, ch=ch: e.activation(
                dst[:, ch * 512:(ch + 1) * 512], p_fb[:, :], AF.Copy),
                waits=[("pe", pn)], inc="act")
            last_bc = p.n("act")
    p.emit("vector", lambda e: e.tensor_mul(gbb2_r[:, :], gbb2_r[:, :], b2r_f[:, 0:H]),
           waits=[("act", last_bc)] + ffence, inc="dve")
    d_gbb2 = p.n("dve")

    b1cp = p.n("act")
    for c8 in range(8):
        for kt in range(KT):
            p.emit("tensor", lambda e, c8=c8, kt=kt: e.matmul(
                p_fb[0:1, 0:512], sbT_b[:, kt:kt + 1],
                w1_sb[:, kt * FF + c8 * 512: kt * FF + (c8 + 1) * 512],
                start=(kt == 0), stop=(kt == KT - 1), skip_group_check=True),
                waits=[w1n, ("act", a_modT), ("act", b1cp)] if kt == 0 else None,
                inc="pe" if kt == KT - 1 else None)
        pn = p.n("pe")
        p.emit("scalar", lambda e: e.activation(nrow[0:1, 0:512], p_fb[0:1, 0:512], AF.Copy),
               waits=[("pe", pn)], inc="act")
        an = p.n("act")
        b1cp = an
        for t4 in range(4):
            p.emit("tensor", lambda e, c8=c8, t4=t4: e.matmul(
                p_y2[0][:, c8 * 4 + t4: c8 * 4 + t4 + 1],
                nrow[0:1, t4 * 128:(t4 + 1) * 128], ones_row_f[0:1, 0:1],
                start=True, stop=True, skip_group_check=True),
                waits=[("act", an)], inc="pe" if t4 == 3 else None)
    pn = p.n("pe")
    p.emit("scalar", lambda e: e.activation(b1pT[:, 0:MT], p_y2[0][:, 0:MT], AF.Copy),
           waits=[("pe", pn)], inc="act")
    an = p.n("act")
    p.emit("vector", lambda e: e.tensor_add(b1pT[:, :], b1pT[:, :], b1T[:, :]),
           waits=[("act", an), n_b1], inc="dve")
    d_b1p = p.n("dve")
    for kt in range(KT):
        p.emit("vector", lambda e, kt=kt: e.tensor_scalar_mul(
            w1_sb[:, kt * FF:(kt + 1) * FF], w1_sb[:, kt * FF:(kt + 1) * FF],
            cb1T[:, kt:kt + 1]), inc="dve" if kt == KT - 1 else None)
    d_w1 = p.n("dve")

    f_dve, out_dma, gelu_act, trF_act, u_pe, y2_pe, trF_pe = {}, {}, {}, {}, {}, {}, {}

    def f_prep(sb_i):
        ppb = sb_i % 2
        x1v = x1f[:, ppb * 2 * H:(ppb + 1) * 2 * H]
        xdep = [x1_dma[2 * sb_i], x1_dma[2 * sb_i + 1]] + (ffence if sb_i < 2 else [])
        if sb_i >= 2:
            xdep.append(("dve", f_dve[2 * (sb_i - 2) + 1]))
        xn = p.dma(
            "sync",
            x1v.rearrange("p (i c) -> p i c", i=2),
            x1_scr[sb_i * SBW:(sb_i + 1) * SBW, :].rearrange("(i p) c -> p i c", p=128),
            waits=xdep, inc="sf%d" % (sb_i % 2))
        for i in range(2):
            st = 2 * sb_i + i
            hdep = [xn, ("act", o_act[st])]
            if sb_i < 2:
                hdep = hdep + ffence
            if sb_i >= 1:
                hdep.append(("pe", trF_pe[sb_i - 1]))
            p.emit("vector", lambda e, st=st, i=i, ppb=ppb: e.tensor_scalar(
                h2_bf[:, i * H:(i + 1) * H],
                x1f[:, :].rearrange("p (s i c) -> p s i c", s=2, i=2)[:, ppb, i, :],
                stats["mu2"][:, st:st + 1], stats["rstd2"][:, st:st + 1],
                ALU.subtract, ALU.mult),
                waits=hdep, inc="dve" if i == 1 else None)
        dh = p.n("dve")
        for i in range(2):
            trdep = [("dve", dh)]
            if (sb_i, i - 1) in trF_act:
                trdep.append(("act", trF_act[(sb_i, i - 1)]))
            elif (sb_i - 1, 1) in trF_act:
                trdep.append(("act", trF_act[(sb_i - 1, 1)]))
            for kt in range(KT):
                p.emit("tensor", lambda e, kt=kt, i=i: e.transpose(
                    p_trF[:, kt * 128:(kt + 1) * 128],
                    h2_bf[:, i * H + kt * 128: i * H + (kt + 1) * 128],
                    ident_b[:, :]),
                    waits=trdep if kt == 0 else None,
                    inc="pe" if kt == KT - 1 else None)
            ptn = p.n("pe")
            trF_pe[sb_i] = ptn
            p.emit("scalar", lambda e, i=i, ppb=ppb: e.activation(
                h2T.rearrange("p (s k i c) -> p s k i c", s=2, k=KT, i=2)[:, ppb, :, i, :],
                p_trF[:, :].rearrange("p (k c) -> p k c", k=KT), AF.Copy),
                waits=[("pe", ptn)], inc="act")
            trF_act[(sb_i, i)] = p.n("act")

    f_prep(0)
    for sb_i in range(NSB):
        ppb = sb_i % 2
        a_h2T = trF_act[(sb_i, 1)]
        for mt in range(MT):
            ppu = mt % 2
            udep = [("act", a_h2T), ("dve", d_w1), ("dve", d_b1p)]
            gi = sb_i * MT + mt
            if gi >= 2:
                udep.append(("act", gelu_act[gi - 2]))
            for kt in range(KT):
                p.emit("tensor", lambda e, ppu=ppu, mt=mt, kt=kt, ppb=ppb: e.matmul(
                    p_u[ppu][:, :],
                    w1_sb[:, kt * FF + mt * 128: kt * FF + (mt + 1) * 128],
                    h2T[:, ppb * KT * SBW + kt * SBW: ppb * KT * SBW + (kt + 1) * SBW],
                    start=(kt == 0), stop=(kt == KT - 1), skip_group_check=True),
                    waits=udep if kt == 0 else None,
                    inc="pe" if kt == KT - 1 else None)
            pn = p.n("pe")
            gdep = [("pe", pn)]
            if sb_i >= 1 and mt == 0:
                gdep.append(("pe", y2_pe[sb_i - 1]))
            p.emit("scalar", lambda e, ppu=ppu, mt=mt: e.activation(
                g2_sb[:, mt * SBW:(mt + 1) * SBW], p_u[ppu][:, :], AF.Gelu,
                bias=b1pT[:, mt:mt + 1]),
                waits=gdep, inc="act")
            gelu_act[gi] = p.n("act")
        u_pe[sb_i] = p.n("pe")
        if sb_i + 1 < NSB:
            f_prep(sb_i + 1)
        for i in range(2):
            st = 2 * sb_i + i
            ppy = st % 2
            for ch in range(2):
                pbank = p_y2[2 * ppy + ch]
                for kt in range(MT):
                    ydep = [("act", gelu_act[sb_i * MT + kt])]
                    if st >= 2 and kt == 0 and ch == 0:
                        ydep.append(("dve", f_dve[st - 2]))
                    p.emit("tensor", lambda e, pbank=pbank, kt=kt, i=i, ch=ch: e.matmul(
                        pbank[:, :],
                        g2_sb[:, kt * SBW + i * 128: kt * SBW + i * 128 + 128],
                        w2_sb[:, kt * H + ch * 512: kt * H + (ch + 1) * 512],
                        start=(kt == 0), stop=(kt == MT - 1), skip_group_check=True),
                        waits=(ydep + [w2n]) if kt == 0 else
                              [("act", gelu_act[sb_i * MT + kt])],
                        inc="pe" if kt == MT - 1 else None)
            ypn = p.n("pe")
            y2_pe[sb_i] = ypn
            evdep = [("pe", ypn), ("dve", d_gbb2)]
            if st >= 2:
                evdep.append(out_dma[st - 2])
            xt = x2st[:, ppy * H:(ppy + 1) * H]
            for ch in range(2):
                pbank = p_y2[2 * ppy + ch]
                cs = slice(ch * 512, (ch + 1) * 512)
                p.emit("vector", lambda e, pbank=pbank, cs=cs, xt=xt: e.tensor_mul(
                    xt[:, cs], pbank[:, :], gb_eff_r[:, cs]),
                    waits=evdep if ch == 0 else None, inc="dve")
            mn = p.n("dve")
            p.emit("gpsimd", lambda e, xt=xt: e.tensor_add(
                xt[:, :], xt[:, :], gbb2_r[:, :]),
                waits=[("dve", mn), ("dve", d_gbb2)], inc="pl")
            gn = p.n("pl")
            for ch in range(2):
                cs = slice(ch * 512, (ch + 1) * 512)
                p.emit("vector", lambda e, cs=cs, xt=xt, i=i, ppb=ppb: e.tensor_add(
                    xt[:, cs],
                    xt[:, cs],
                    x1f.rearrange("p (s i c) -> p s i c", s=2, i=2)[:, ppb, i, :][:, cs]),
                    waits=[("pl", gn)] if ch == 0 else None,
                    inc="dve" if ch == 1 else None)
            f_dve[st] = p.n("dve")
            out_dma[st] = p.dma("sync", out_d[st * 128:(st + 1) * 128, :],
                                x2st[:, ppy * H:(ppy + 1) * H],
                                waits=[("dve", f_dve[st])],
                                inc="so%d" % (st % 2))
    p.emit("sync", lambda e: e.nop(),
           waits=[("so0", p.n("so0")), ("so1", p.n("so1"))])
    fsc.close()

    with nc.Block() as block:
        p.realize(block, sems)
    root.close()
    return nc


_NC_CACHE = None


def kernel(**inputs):
    global _NC_CACHE
    if _NC_CACHE is None:
        _NC_CACHE = build_nc()
    nc = _NC_CACHE

    x = np.ascontiguousarray(inputs["L0_attn_hidden_states"])
    c = np.ascontiguousarray(inputs["L0_c"])
    shared = {
        "Wq": np.ascontiguousarray(inputs["L0_attn_Wq"]),
        "Wk": np.ascontiguousarray(inputs["L0_attn_Wk"]),
        "Wv": np.ascontiguousarray(inputs["L0_attn_Wv"]),
        "Wo": np.ascontiguousarray(inputs["L0_attn_Wo"]),
        "bq": np.ascontiguousarray(inputs["L0_attn_bq"]),
        "bk": np.ascontiguousarray(inputs["L0_attn_bk"]),
        "bv": np.ascontiguousarray(inputs["L0_attn_bv"]),
        "bo": np.ascontiguousarray(inputs["L0_attn_bo"]),
        "lq1": np.ascontiguousarray(inputs["L0_attn_lambda_q1"]),
        "lk1": np.ascontiguousarray(inputs["L0_attn_lambda_k1"]),
        "lq2": np.ascontiguousarray(inputs["L0_attn_lambda_q2"]),
        "lk2": np.ascontiguousarray(inputs["L0_attn_lambda_k2"]),
        "lnw": np.ascontiguousarray(inputs["L0_attn_ln_w"]),
        "lnb": np.ascontiguousarray(inputs["L0_attn_ln_b"]),
        "w1": np.ascontiguousarray(inputs["L0_ffn_w1"]),
        "b1": np.ascontiguousarray(inputs["L0_ffn_b1"]),
        "w2": np.ascontiguousarray(inputs["L0_ffn_w2"]),
        "b2": np.ascontiguousarray(inputs["L0_ffn_b2"]),
        "wmod": np.ascontiguousarray(inputs["L0_wmod"]),
    }
    in_maps = []
    for b in range(B):
        m = dict(shared)
        m["x"] = np.ascontiguousarray(x[b])
        m["c"] = np.ascontiguousarray(c[b])
        in_maps.append(m)

    res = run_bass_kernel_spmd(nc, in_maps, core_ids=list(range(B)))
    out = np.stack([np.asarray(r["out"]) for r in res.results], axis=0).astype(np.float32)
    return out


if __name__ == "__main__":
    nc = build_nc()
    n = sum(1 for _ in nc.all_instructions())
    print("instructions:", n)


# version 28
# speedup vs baseline: 1.0272x; 1.0272x over previous
import numpy as np
from contextlib import ExitStack

import concourse.bass as bass
import concourse.mybir as mybir
from concourse.bass_utils import run_bass_kernel_spmd

F32 = mybir.dt.float32
BF16 = mybir.dt.bfloat16
AF = mybir.ActivationFunctionType
ALU = mybir.AluOpType
AXX = mybir.AxisListType.X

B, S, H, HEADS, HD, FF = 8, 4096, 1024, 8, 128, 4096
NT = S // 128
KT = H // 128
MT = FF // 128
QCW = 1024
NQC = S // QCW
SBW = 256
NSB = S // SBW
EPS = 1e-5
ISQ = float(1.0 / np.sqrt(HD))
LAMBDA_INIT = 0.8
NEG_HALF_LN_HD = float(-0.5 * np.log(HD))

ENGINES = ("tensor", "scalar", "vector", "gpsimd", "sync")
_DBG = False


class Prog:

    def __init__(self):
        self.streams = {e: [] for e in ENGINES}
        self.counts = {}

    def n(self, sem):
        return self.counts.get(sem, 0)

    def emit(self, eng, build, waits=None, inc=None, by=1):
        w = [(s, v) for (s, v) in (waits or []) if v > 0]
        self.streams[eng].append((build, w, inc, by))
        if inc is not None:
            self.counts[inc] = self.counts.get(inc, 0) + by
            return self.counts[inc]
        return None

    def dma(self, eng, dst, src, waits=None, inc="swg"):
        n = self.emit(eng, lambda e, d=dst, s=src: e.dma_start(d, s),
                      waits=waits, inc=inc, by=16)
        return (inc, n)

    def realize(self, block, sems):
        def section(eng_name):
            stream = self.streams[eng_name]

            def body(e):
                watermark = {}
                for build, waits, inc, by in stream:
                    for s_, v in waits:
                        if watermark.get(s_, -1) < v:
                            e.wait_ge(sems[s_], v)
                            watermark[s_] = v
                    inst = build(e)
                    if inc is not None:
                        inst.then_inc(sems[inc], by)
            return body

        block.sync(section("sync"))
        block.gpsimd(section("gpsimd"))
        block.tensor(section("tensor"))
        block.scalar(section("scalar"))
        block.vector(section("vector"))


def build_nc():
    nc = bass.Bass("TRN2", target_bir_lowering=False)

    x_d = nc.dram_tensor("x", [S, H], F32, kind="ExternalInput").ap()
    Wq_d = nc.dram_tensor("Wq", [H, H], F32, kind="ExternalInput").ap()
    Wk_d = nc.dram_tensor("Wk", [H, H], F32, kind="ExternalInput").ap()
    Wv_d = nc.dram_tensor("Wv", [H, H], F32, kind="ExternalInput").ap()
    Wo_d = nc.dram_tensor("Wo", [H, H], F32, kind="ExternalInput").ap()
    bq_d = nc.dram_tensor("bq", [H], F32, kind="ExternalInput").ap()
    bk_d = nc.dram_tensor("bk", [H], F32, kind="ExternalInput").ap()
    bv_d = nc.dram_tensor("bv", [H], F32, kind="ExternalInput").ap()
    bo_d = nc.dram_tensor("bo", [H], F32, kind="ExternalInput").ap()
    lq1_d = nc.dram_tensor("lq1", [HD], F32, kind="ExternalInput").ap()
    lk1_d = nc.dram_tensor("lk1", [HD], F32, kind="ExternalInput").ap()
    lq2_d = nc.dram_tensor("lq2", [HD], F32, kind="ExternalInput").ap()
    lk2_d = nc.dram_tensor("lk2", [HD], F32, kind="ExternalInput").ap()
    lnw_d = nc.dram_tensor("lnw", [H], F32, kind="ExternalInput").ap()
    lnb_d = nc.dram_tensor("lnb", [H], F32, kind="ExternalInput").ap()
    w1_d = nc.dram_tensor("w1", [H, FF], F32, kind="ExternalInput").ap()
    b1_d = nc.dram_tensor("b1", [FF], F32, kind="ExternalInput").ap()
    w2_d = nc.dram_tensor("w2", [FF, H], F32, kind="ExternalInput").ap()
    b2_d = nc.dram_tensor("b2", [H], F32, kind="ExternalInput").ap()
    c_d = nc.dram_tensor("c", [H], F32, kind="ExternalInput").ap()
    wmod_d = nc.dram_tensor("wmod", [H, 6 * H], F32, kind="ExternalInput").ap()
    out_d = nc.dram_tensor("out", [S, H], F32, kind="ExternalOutput").ap()
    ao_scr = nc.dram_tensor("ao_scr", [S, H], BF16, kind="Internal").ap()
    qT_scr = nc.dram_tensor("qT_scr", [HEADS, 128, S], BF16, kind="Internal").ap()
    kT_scr = nc.dram_tensor("kT_scr", [HEADS, 128, S], BF16, kind="Internal").ap()
    x1_scr = nc.dram_tensor("x1_scr", [S, H], F32, kind="Internal").ap()

    root = ExitStack()
    taps = {}

    def SB(stack, name, shape, dt):
        return stack.enter_context(nc.sbuf_tensor(name, shape, dt))

    def PS(stack, name, shape, dt=F32):
        return stack.enter_context(nc.psum_tensor(name, shape, dt))

    ones_row_f = SB(root, "ones_row_f", [1, 128], F32)
    p2_row_f = SB(root, "p2_row_f", [1, 128], F32)
    ones_col_f = SB(root, "ones_col_f", [128, 1], F32)
    ones_col_b = SB(root, "ones_col_b", [128, 1], BF16)
    ident_b = SB(root, "ident_b", [128, 128], BF16)
    ident_f = SB(root, "ident_f", [128, 128], F32)

    gac_row = SB(root, "gac_row", [1, H], F32)
    gbc_row = SB(root, "gbc_row", [1, H], F32)
    ca1T = SB(root, "ca1T", [128, KT], F32)
    cb1T = SB(root, "cb1T", [128, KT], F32)
    saT_b = SB(root, "saT_b", [128, KT], BF16)
    sbT_b = SB(root, "sbT_b", [128, KT], BF16)
    bqT = SB(root, "bqT", [128, KT], F32)
    bkT = SB(root, "bkT", [128, KT], F32)
    lnwT = SB(root, "lnwT", [128, KT], F32)
    lnbT_f = SB(root, "lnbT_f", [128, KT], F32)
    lnbT_b = SB(root, "lnbT_b", [128, KT], BF16)
    b2row = SB(root, "b2row", [1, H], F32)
    cT = SB(root, "cT", [128, KT], F32)
    silu_cT = SB(root, "silu_cT", [128, KT], BF16)
    lam_in = SB(root, "lam_in", [128, 4], F32)
    lamneg = SB(root, "lamneg", [128, 1], F32)
    lam11 = SB(root, "lam11", [1, 4], F32)
    Bneg = SB(root, "Bneg", [128, HEADS], F32)
    bnd = SB(root, "bnd", [1, 8], F32)
    bnd16 = SB(root, "bnd16", [1, 16], F32)
    biasqT = SB(root, "biasqT", [128, 4], F32)
    nrow = SB(root, "nrow", [1, 512], F32)
    eps_col = SB(root, "eps_col", [128, 1], F32)
    nhl_col = SB(root, "nhl_col", [128, 1], F32)
    maxcol = SB(root, "maxcol", [128, 2], F32)

    stats = {}
    for nm in ("mu1s", "ssq1", "mu1", "rstd1", "v1t", "m1t",
               "mu2s", "ssq2", "mu2", "rstd2", "v2t", "m2t",
               "amus", "assq", "amu", "arstd", "avt", "amt"):
        stats[nm] = SB(root, nm, [128, NT], F32)

    sems = {}
    for s_ in ("sx", "swg", "sao", "sout", "pe", "act", "dve", "pl",
               "sx0", "sx1", "sx2", "sx3", "wm0", "wm1", "swv", "swq",
               "sqk0", "sqk1", "sal0", "sal1", "sal2", "sal3",
               "sal4", "sal5", "sal6", "sal7", "swo",
               "sw1", "sw2", "sb1", "sas0", "sas1", "sx10", "sx11",
               "sd0", "sd1", "sd2", "sd3", "sf0", "sf1", "so0", "so1"):
        sems[s_] = root.enter_context(nc.semaphore(s_))

    p = Prog()
    last_dead = [0]

    def fence(dma=()):
        w = [("pe", p.n("pe")), ("act", p.n("act")), ("dve", p.n("dve")),
             ("pl", p.n("pl"))]
        for s_ in dma:
            w.append((s_, p.n(s_)))
        return w

    def tap(name, src_ap, waits):
        if not _DBG:
            return
        t = nc.dram_tensor("dbg_" + name, list(src_ap.shape), src_ap.dtype,
                           kind="ExternalOutput").ap()
        p.dma("sync", t[tuple(slice(None) for _ in src_ap.shape)], src_ap,
              waits=waits, inc="sout")

    pre = ExitStack()
    x_tiles = SB(pre, "x_tiles", [128, 4 * H], F32)
    h_bf = SB(pre, "h_bf", [128, 2 * H], BF16)
    deadout = SB(pre, "deadout", [128, 1024], F32)

    p.emit("gpsimd", lambda e: e.memset(ones_row_f[:, :], 1.0))
    p.emit("gpsimd", lambda e: e.memset(p2_row_f[:, :], 5.0))
    p.emit("gpsimd", lambda e: e.memset(ones_col_f[:, :], 1.0))
    p.emit("gpsimd", lambda e: e.memset(ones_col_b[:, :], 1.0))
    p.emit("gpsimd", lambda e: e.iota(
        deadout[:, 0:128], pattern=[[-1, 128]], base=0, channel_multiplier=1,
        allow_small_or_imprecise_dtypes=True), inc="pl")
    pl_iota = p.n("pl")
    p.emit("vector", lambda e: e.tensor_scalar(
        ident_b[:, :], deadout[:, 0:128], 0.0, None, ALU.is_equal),
        waits=[("pl", pl_iota)], inc="dve")
    p.emit("vector", lambda e: e.tensor_scalar(
        ident_f[:, :], deadout[:, 0:128], 0.0, None, ALU.is_equal),
        inc="dve")
    p.emit("vector", lambda e: e.reciprocal(p2_row_f[:, :], p2_row_f[:, :]),
           waits=[("pl", pl_iota)], inc="dve")
    d_ident = p.n("dve")
    p.emit("gpsimd", lambda e: e.memset(eps_col[:, :], EPS),
        waits=[("dve", d_ident)])
    p.emit("gpsimd", lambda e: e.memset(nhl_col[:, :], NEG_HALF_LN_HD), inc="pl")
    pl_const = p.n("pl")

    for kt in range(KT):
        p.dma("sync", bqT[:, kt:kt + 1], bq_d[kt * 128:(kt + 1) * 128][:, None], inc="sx")
        p.dma("sync", bkT[:, kt:kt + 1], bk_d[kt * 128:(kt + 1) * 128][:, None], inc="sx")
        p.dma("sync", lnwT[:, kt:kt + 1], lnw_d[kt * 128:(kt + 1) * 128][:, None], inc="sx")
        p.dma("sync", lnbT_f[:, kt:kt + 1], lnb_d[kt * 128:(kt + 1) * 128][:, None], inc="sx")
        p.dma("sync", cT[:, kt:kt + 1], c_d[kt * 128:(kt + 1) * 128][:, None], inc="sx")
    for i, d in enumerate([lq1_d, lk1_d, lq2_d, lk2_d]):
        p.dma("sync", lam_in[:, i:i + 1], d[:, None], inc="sx")
    p.dma("sync", b2row[0:1, :], b2_d[None, :], inc="sx")
    n_small = p.n("sx")

    p.emit("scalar", lambda e: e.activation(lnbT_b[:, :], lnbT_f[:, :], AF.Copy),
           waits=[("sx", n_small)], inc="act")

    mscope = ExitStack()
    wmod_sb = SB(mscope, "wmod_sb", [128, 2 * 2048], BF16)
    modrow = SB(mscope, "modrow", [1, 6 * H], F32)
    sgn_scr = SB(mscope, "sgn_scr", [1, H], F32)
    saT_f = SB(mscope, "saT_f", [128, KT], F32)
    sbT_f = SB(mscope, "sbT_f", [128, KT], F32)
    lamprod = SB(mscope, "lamprod", [128, 2], F32)
    mod_ps = PS(mscope, "mod_ps", [1, 1024])
    mod_ps2 = [mod_ps, PS(mscope, "mod_psB", [1, 1024])]
    modt_ps = PS(mscope, "modt_ps", [128, 32])
    p_bc = PS(mscope, "p_bc", [128, 512])

    p.emit("scalar", lambda e: e.activation(silu_cT[:, :], cT[:, :], AF.Silu),
           waits=[("sx", n_small)], inc="act")
    a_silu = p.n("act")

    mod_pe = {}
    mod_cp = 0
    for g in range(3):
        for kt in range(KT):
            slot = (g * KT + kt) % 2
            dep = [("pl", pl_const)]
            if (g * KT + kt) >= 2:
                dep.append(("pe", mod_pe[g * KT + kt - 2]))
            wsem = "wm%d" % slot
            p.dma("gpsimd", wmod_sb[:, slot * 2048:(slot + 1) * 2048],
                  wmod_d[kt * 128:(kt + 1) * 128, g * 2048:(g + 1) * 2048],
                  waits=dep, inc=wsem)
            wn = (wsem, p.n(wsem))
            for ch in range(4):
                p.emit("tensor", lambda e, slot=slot, kt=kt, ch=ch: e.matmul(
                    mod_ps[0:1, (ch % 2) * 512:(ch % 2 + 1) * 512]
                    if False else mod_ps2[ch // 2][0:1, (ch % 2) * 512:(ch % 2 + 1) * 512],
                    silu_cT[:, kt:kt + 1],
                    wmod_sb[:, slot * 2048 + ch * 512: slot * 2048 + (ch + 1) * 512],
                    start=(kt == 0), stop=(kt == KT - 1), skip_group_check=True),
                    waits=[wn, ("act", a_silu), ("act", mod_cp)],
                    inc="pe" if ch == 3 else None)
            mod_pe[g * KT + kt] = p.n("pe")
        pn = p.n("pe")
        for half in range(2):
            p.emit("scalar", lambda e, g=g, half=half: e.activation(
                modrow[0:1, g * 2048 + half * 1024:(g * 2048) + (half + 1) * 1024],
                mod_ps2[half][0:1, :], AF.Copy),
                waits=[("pe", pn)], inc="act" if half == 1 else None)
        mod_cp = p.n("act")
    a_modrow = p.n("act")

    for row, off in ((gac_row, 2 * H), (gbc_row, 5 * H)):
        p.emit("vector", lambda e, r=row, o=off: e.tensor_scalar(
            r[0:1, :], modrow[0:1, o:o + H], 0.0, 2.0, ALU.is_ge, ALU.mult),
            waits=[("act", a_modrow)], inc="dve")
        c1 = p.n("dve")
        p.emit("vector", lambda e, r=row: e.tensor_scalar_add(r[0:1, :], r[0:1, :], -1.0),
               waits=[("dve", c1)], inc="dve")
        c2 = p.n("dve")
        p.emit("vector", lambda e, o=off: e.tensor_scalar_mul(
            sgn_scr[0:1, :], modrow[0:1, o:o + H], -1.0), inc="dve")
        c3 = p.n("dve")
        p.emit("vector", lambda e, o=off: e.tensor_max(
            sgn_scr[0:1, :], sgn_scr[0:1, :], modrow[0:1, o:o + H]),
            waits=[("dve", c3)], inc="dve")
        c4 = p.n("dve")
        p.emit("vector", lambda e: e.tensor_scalar_max(
            sgn_scr[0:1, :], sgn_scr[0:1, :], 0.25),
            waits=[("dve", c4)], inc="dve")
        c5 = p.n("dve")
        p.emit("vector", lambda e, r=row: e.tensor_mul(r[0:1, :], r[0:1, :], sgn_scr[0:1, :]),
               waits=[("dve", c2), ("dve", c5)], inc="dve")
    d_clamp = p.n("dve")

    for i, off in enumerate((0, H, 3 * H, 4 * H)):
        for kt in range(KT):
            p.emit("tensor", lambda e, i=i, off=off, kt=kt: e.matmul(
                modt_ps[:, i * KT + kt: i * KT + kt + 1],
                modrow[0:1, off + kt * 128: off + (kt + 1) * 128],
                ones_row_f[0:1, 0:1],
                start=True, stop=True, skip_group_check=True),
                waits=[("act", a_modrow), ("pl", pl_const)],
                inc="pe" if (i == 3 and kt == KT - 1) else None)
    pn = p.n("pe")
    p.emit("scalar", lambda e: e.activation(saT_f[:, :], modt_ps[:, 0:KT], AF.Copy),
           waits=[("pe", pn)])
    p.emit("scalar", lambda e: e.activation(ca1T[:, :], modt_ps[:, KT:2 * KT], AF.Copy))
    p.emit("scalar", lambda e: e.activation(sbT_f[:, :], modt_ps[:, 2 * KT:3 * KT], AF.Copy))
    p.emit("scalar", lambda e: e.activation(cb1T[:, :], modt_ps[:, 3 * KT:4 * KT], AF.Copy),
           inc="act")
    a_mcp = p.n("act")
    p.emit("scalar", lambda e: e.activation(saT_b[:, :], saT_f[:, :], AF.Copy),
           waits=[("act", a_mcp)])
    p.emit("scalar", lambda e: e.activation(sbT_b[:, :], sbT_f[:, :], AF.Copy), inc="act")
    a_modT = p.n("act")
    p.emit("vector", lambda e: e.tensor_scalar_add(ca1T[:, :], ca1T[:, :], 1.0),
           waits=[("act", a_modT)])
    p.emit("vector", lambda e: e.tensor_scalar_add(cb1T[:, :], cb1T[:, :], 1.0), inc="dve")
    d_ca1 = p.n("dve")

    p.emit("vector", lambda e: e.tensor_mul(lamprod[:, 0:1], lam_in[:, 0:1], lam_in[:, 1:2]),
           waits=[("sx", n_small)])
    p.emit("vector", lambda e: e.tensor_mul(lamprod[:, 1:2], lam_in[:, 2:3], lam_in[:, 3:4]),
           inc="dve")
    dn = p.n("dve")
    p.emit("tensor", lambda e: e.matmul(p_bc[0:1, 0:1], lamprod[:, 0:1], ones_col_f[:, 0:1],
                                        start=True, stop=True, skip_group_check=True),
           waits=[("dve", dn)])
    p.emit("tensor", lambda e: e.matmul(p_bc[0:1, 1:2], lamprod[:, 1:2], ones_col_f[:, 0:1],
                                        start=False, stop=True, skip_group_check=True),
           inc="pe")
    pn = p.n("pe")
    p.emit("scalar", lambda e: e.activation(lam11[0:1, 0:2], p_bc[0:1, 0:2], AF.Exp),
           waits=[("pe", pn)], inc="act")
    an = p.n("act")
    p.emit("vector", lambda e: e.tensor_sub(lam11[0:1, 2:3], lam11[0:1, 0:1], lam11[0:1, 1:2]),
           waits=[("act", an)], inc="dve")
    lsn = p.n("dve")
    p.emit("vector", lambda e: e.tensor_scalar(lam11[0:1, 3:4], lam11[0:1, 2:3],
                                               LAMBDA_INIT, -1.0, ALU.add, ALU.mult),
           waits=[("dve", lsn)], inc="dve")
    dn = p.n("dve")
    p.emit("tensor", lambda e: e.matmul(p_bc[:, 4:5], ones_row_f[0:1, :], lam11[0:1, 3:4],
                                        start=True, stop=True, skip_group_check=True),
           waits=[("dve", dn)], inc="pe")
    pn = p.n("pe")
    p.emit("scalar", lambda e: e.activation(lamneg[:, 0:1], p_bc[:, 4:5], AF.Copy),
           waits=[("pe", pn)], inc="act")
    a_lam = p.n("act")
    tap("modrow", modrow[0:1, :], [("act", a_modrow)])
    tap("gac_row", gac_row[0:1, :], [("dve", d_clamp)])
    tap("gbc_row", gbc_row[0:1, :], [("dve", d_clamp)])
    tap("lam11", lam11[0:1, :], [("dve", p.n("dve")), ("act", a_lam)])
    tap("lamneg", lamneg[:, :], [("act", a_lam)])
    tap("ca1T", ca1T[:, :], [("dve", d_ca1)])
    tap("saTb", saT_b[:, :], [("act", a_modT)])
    mscope.close()

    tsc = ExitStack()
    tT = SB(tsc, "tT", [128, KT * S], BF16)
    asc = ExitStack()
    v_aug = SB(asc, "v_aug", [128, NT * 516], BF16)
    biasq8 = SB(asc, "biasq8", [128, KT], F32)
    biask8 = SB(asc, "biask8", [128, KT], F32)

    cmb = ExitStack()
    Wv_sb = SB(cmb, "Wv_sb", [128, KT * H], BF16)
    Wqk_sb = SB(cmb, "Wqk_sb", [128, KT * H], BF16)
    bvrow = SB(cmb, "bvrow", [1, H], F32)
    bvprow = SB(cmb, "bvprow", [1, H], F32)
    beffrow = SB(cmb, "beffrow", [1, 512], F32)
    bveff_c = SB(cmb, "bveff_c", [128, 512], F32)
    vA_scr = SB(cmb, "vA_scr", [128, 2 * 512], BF16)
    nrow2 = SB(cmb, "nrow2", [1, 1024], F32)
    qstg = SB(cmb, "qstg", [128, 4 * 512], BF16)
    p_tr = [PS(cmb, "ptr0", [128, 1024], BF16), PS(cmb, "ptr1", [128, 1024], BF16)]
    p_v = [PS(cmb, "pv%d" % i, [128, 512]) for i in range(4)]
    p_dn = [PS(cmb, "pdn0", [128, 512]), PS(cmb, "pdn1", [128, 512])]

    vfence = fence()
    p.dma("sync", bvrow[0:1, :], bv_d[None, :], waits=vfence, inc="sx")
    nbv = p.n("sx")
    p.dma("gpsimd", Wv_sb[:, :].rearrange("p (k n) -> p k n", k=KT),
          Wv_d[:, :].rearrange("(k p) n -> p k n", p=128), waits=vfence, inc="swv")
    wvn = ("swv", p.n("swv"))
    p.dma("gpsimd", Wqk_sb[:, :].rearrange("p (k n) -> p k n", k=KT),
          Wq_d[:, :].rearrange("(k p) n -> p k n", p=128), waits=vfence, inc="swq")
    wqkn = ("swq", p.n("swq"))
    p.emit("gpsimd", lambda e: e.memset(
        v_aug[:, :].rearrange("p (t j c) -> p (t j) c", t=NT, j=4)[:, :, 128:129], 1.0),
        waits=vfence, inc="pl")
    pl_vones = p.n("pl")

    for ch in range(2):
        for kt in range(KT):
            p.emit("tensor", lambda e, kt=kt, ch=ch: e.matmul(
                p_v[0][0:1, 0:512], saT_b[:, kt:kt + 1],
                Wv_sb[:, kt * H + ch * 512: kt * H + (ch + 1) * 512],
                start=(kt == 0), stop=(kt == KT - 1), skip_group_check=True),
                waits=[wvn, ("act", a_modT), ("dve", p.n("dve"))]
                if kt == 0 else None,
                inc="pe" if kt == KT - 1 else None)
        pn = p.n("pe")
        p.emit("vector", lambda e, ch=ch: e.tensor_copy(
            nrow2[0:1, ch * 512:(ch + 1) * 512], p_v[0][0:1, 0:512]),
            waits=[("pe", pn)], inc="dve")
    dn = p.n("dve")
    p.emit("vector", lambda e: e.tensor_add(bvprow[0:1, :], nrow2[0:1, :], bvrow[0:1, :]),
           waits=[("dve", dn), ("sx", nbv)], inc="dve")
    dn = p.n("dve")
    for j in range(4):
        p.emit("vector", lambda e, j=j: e.scalar_tensor_tensor(
            beffrow[0:1, j * 128:(j + 1) * 128],
            bvprow[0:1, (4 + j) * 128:(5 + j) * 128], lamneg[0:1, 0:1],
            bvprow[0:1, j * 128:(j + 1) * 128], ALU.mult, ALU.add),
            waits=[("dve", dn), ("act", a_lam)], inc="dve")
    dn = p.n("dve")
    p.emit("tensor", lambda e: e.matmul(p_v[1][:, :], ones_row_f[0:1, :], beffrow[0:1, :],
                                        start=True, stop=True, skip_group_check=True),
           waits=[("dve", dn), ("pe", p.n("pe"))], inc="pe")
    pn = p.n("pe")
    p.emit("scalar", lambda e: e.activation(bveff_c[:, :], p_v[1][:, :], AF.Copy),
           waits=[("pe", pn)], inc="act")
    a_bveff = p.n("act")
    for kt in range(KT):
        p.emit("vector", lambda e, kt=kt: e.tensor_scalar_mul(
            Wv_sb[:, kt * H:(kt + 1) * H], Wv_sb[:, kt * H:(kt + 1) * H],
            ca1T[:, kt:kt + 1]),
            waits=[("pe", pn), ("dve", d_ca1)] if kt == 0 else None,
            inc="dve" if kt == KT - 1 else None)
    d_wv = p.n("dve")
    for ch in range(2):
        for kt in range(KT):
            p.emit("tensor", lambda e, kt=kt, ch=ch: e.matmul(
                p_dn[0][0:1, 0:512], saT_b[:, kt:kt + 1],
                Wqk_sb[:, kt * H + ch * 512: kt * H + (ch + 1) * 512],
                start=(kt == 0), stop=(kt == KT - 1), skip_group_check=True),
                waits=[wqkn, ("act", a_modT), ("dve", p.n("dve"))]
                if kt == 0 else None,
                inc="pe" if kt == KT - 1 else None)
        pn = p.n("pe")
        p.emit("vector", lambda e, ch=ch: e.tensor_copy(
            nrow2[0:1, ch * 512:(ch + 1) * 512], p_dn[0][0:1, 0:512]),
            waits=[("pe", pn)], inc="dve")
    dn = p.n("dve")
    for dt_ in range(KT):
        p.emit("tensor", lambda e, dt_=dt_: e.matmul(
            p_dn[1][:, dt_:dt_ + 1], nrow2[0:1, dt_ * 128:(dt_ + 1) * 128],
            ones_row_f[0:1, 0:1], start=True, stop=True, skip_group_check=True),
            waits=[("dve", dn)], inc="pe" if dt_ == KT - 1 else None)
    pn = p.n("pe")
    p.emit("vector", lambda e: e.tensor_add(biasq8[:, :], p_dn[1][:, 0:KT], bqT[:, :]),
           waits=[("pe", pn)], inc="dve")
    d_bq8 = p.n("dve")
    for kt in range(KT):
        p.emit("vector", lambda e, kt=kt: e.tensor_scalar_mul(
            Wqk_sb[:, kt * H:(kt + 1) * H], Wqk_sb[:, kt * H:(kt + 1) * H],
            ca1T[:, kt:kt + 1]),
            waits=[("pe", pn), ("dve", d_ca1)] if kt == 0 else None,
            inc="dve" if kt == KT - 1 else None)
    d_wqk = p.n("dve")

    ln1_dve, ln1_act, tr_pe, tr_act = {}, {}, {}, {}
    l1_ms, l1_mu, l1_rs, l1_xn = {}, {}, {}, {}
    v_comb = {}
    pd_dma = {}
    dn_evac = {}
    nchunk = 0

    def emit_l1_s1(st):
        xv = x_tiles[:, (st % 4) * H:(st % 4 + 1) * H]
        dep = []
        if st >= 4:
            dep += [("dve", ln1_dve.get(st - 4, 0)), ("act", ln1_act.get(st - 4, 0))]
        xn = p.dma("sync", xv, x_d[st * 128:(st + 1) * 128, :], waits=dep,
                   inc="sx%d" % (st % 4))
        p.emit("scalar", lambda e, st=st, xv=xv: e.activation(
            deadout[:, :], xv, AF.Copy, accum_out=stats["mu1s"][:, st:st + 1]),
            waits=[xn, ("act", last_dead[0])] + ([("dve", d_ident)] if st == 0 else []),
            inc="act")
        l1_ms[st] = p.n("act")
        last_dead[0] = l1_ms[st]
        p.emit("scalar", lambda e, st=st, xv=xv: e.activation(
            deadout[:, :], xv, AF.Square, accum_out=stats["ssq1"][:, st:st + 1]),
            waits=[("act", last_dead[0])], inc="act")
        ln1_act[st] = p.n("act")
        last_dead[0] = ln1_act[st]
        l1_xn[st] = xn

    def emit_l1_s2(st):
        p.emit("vector", lambda e, st=st: e.tensor_mul(
            stats["m1t"][:, st:st + 1], stats["mu1s"][:, st:st + 1],
            stats["mu1s"][:, st:st + 1]), waits=[("act", l1_ms[st])], inc="dve")
        t1n = p.n("dve")
        p.emit("vector", lambda e, st=st: e.scalar_tensor_tensor(
            stats["v1t"][:, st:st + 1], stats["m1t"][:, st:st + 1], 1.0 / H,
            stats["ssq1"][:, st:st + 1], ALU.mult, ALU.subtract),
            waits=[("act", ln1_act[st]), ("dve", t1n)], inc="dve")
        dn = p.n("dve")
        p.emit("vector", lambda e, st=st: e.tensor_scalar_mul(
            stats["mu1"][:, st:st + 1], stats["mu1s"][:, st:st + 1], 1.0 / H),
            inc="dve")
        l1_mu[st] = p.n("dve")
        p.emit("scalar", lambda e, st=st: e.activation(
            stats["v1t"][:, st:st + 1], stats["v1t"][:, st:st + 1], AF.Ln,
            scale=-1.0 / H, bias=eps_col[:, 0:1]),
            waits=[("dve", dn), ("pl", pl_const)], inc="act")
        lnn = p.n("act")
        p.emit("scalar", lambda e, st=st: e.activation(
            stats["rstd1"][:, st:st + 1], stats["v1t"][:, st:st + 1], AF.Exp, scale=-0.5),
            waits=[("act", lnn)], inc="act")
        l1_rs[st] = p.n("act")

    def emit_l1_s3(st):
        pp = st % 2
        xv = x_tiles[:, (st % 4) * H:(st % 4 + 1) * H]
        hv = h_bf[:, pp * H:(pp + 1) * H]
        hdep = [("act", l1_rs[st]), ("dve", l1_mu[st])]
        if st >= 2:
            hdep.append(("pe", tr_pe[st - 2]))
        p.emit("vector", lambda e, xv=xv, hv=hv, st=st: e.tensor_scalar(
            hv, xv, stats["mu1"][:, st:st + 1], stats["rstd1"][:, st:st + 1],
            ALU.subtract, ALU.mult), waits=hdep, inc="dve")
        ln1_dve[st] = p.n("dve")
        trdep = [("dve", ln1_dve[st])]
        if st >= 2:
            trdep.append(tr_act[st - 2])
        for kt in range(KT):
            p.emit("tensor", lambda e, kt=kt, pp=pp, hv=hv: e.transpose(
                p_tr[pp][:, kt * 128:(kt + 1) * 128],
                hv.rearrange("p (k c) -> p k c", k=KT)[:, kt, :], ident_b[:, :]),
                waits=trdep if kt == 0 else None,
                inc="pe" if kt == KT - 1 else None)
        tr_pe[st] = p.n("pe")
        p.emit("vector", lambda e, st=st, pp=pp: e.tensor_copy(
            tT[:, :].rearrange("p (k s) -> p k s", k=KT)[:, :, st * 128:(st + 1) * 128],
            p_tr[pp][:, :].rearrange("p (k s) -> p k s", k=KT)),
            waits=[("pe", tr_pe[st])], inc="dve")
        tr_act[st] = ("dve", p.n("dve"))

    def emit_v(st):
        pp = st % 2
        psA, psB = p_v[2 * pp], p_v[2 * pp + 1]
        dep = [tr_act[st], ("dve", d_wv), ("act", a_bveff)]
        if st >= 2:
            dep.append(("dve", v_comb[st - 2]))
        for kt in range(KT):
            hT_t = tT[:, kt * S + st * 128: kt * S + (st + 1) * 128]
            p.emit("tensor", lambda e, psA=psA, hT_t=hT_t, kt=kt: e.matmul(
                psA[:, :], hT_t, Wv_sb[:, kt * H: kt * H + 512],
                start=(kt == 0), stop=(kt == KT - 1), skip_group_check=True),
                waits=dep if kt == 0 else None)
            p.emit("tensor", lambda e, psB=psB, hT_t=hT_t, kt=kt: e.matmul(
                psB[:, :], hT_t, Wv_sb[:, kt * H + 512: (kt + 1) * H],
                start=(kt == 0), stop=(kt == KT - 1), skip_group_check=True),
                inc="pe" if kt == KT - 1 else None)
        pn2 = p.n("pe")
        p.emit("scalar", lambda e, psA=psA, pp=pp: e.activation(
            vA_scr[:, pp * 512:(pp + 1) * 512], psA[:, :], AF.Copy),
            waits=[("pe", pn2)], inc="act")
        acn = p.n("act")
        p.emit("vector", lambda e, psB=psB, pp=pp: e.scalar_tensor_tensor(
            vA_scr[:, pp * 512:(pp + 1) * 512], psB[:, :], lamneg[:, 0:1],
            vA_scr[:, pp * 512:(pp + 1) * 512], ALU.mult, ALU.add),
            waits=[("act", acn), ("act", a_bveff), ("pl", pl_vones)], inc="dve")
        sttn = p.n("dve")
        p.emit("vector", lambda e, st=st, pp=pp: e.tensor_tensor(
            v_aug[:, st * 516:(st + 1) * 516].rearrange("p (j c) -> p j c", j=4)[:, :, 0:128],
            vA_scr[:, pp * 512:(pp + 1) * 512].rearrange("p (j c) -> p j c", j=4),
            bveff_c[:, :].rearrange("p (j c) -> p j c", j=4), ALU.add),
            waits=[("dve", sttn)], inc="dve")
        v_comb[st] = p.n("dve")

    def emit_dense_chunk(ti, dt_, qi, wbuf, scr, bcol, dscale, ev_eng):
        nonlocal_ns = emit_dense_chunk
        n = nonlocal_ns.nchunk
        bank = n % 2
        slot = n % 4
        mmw = [("dve", dscale), tr_act[min(4 * qi + 3, NT - 1)]]
        if n >= 2:
            mmw.append(dn_evac[n - 2])
        for kt in range(KT):
            p.emit("tensor", lambda e, bank=bank, wbuf=wbuf, kt=kt, dt_=dt_, qi=qi: e.matmul(
                p_dn[bank][:, :], wbuf[:, kt * H + dt_ * 128: kt * H + (dt_ + 1) * 128],
                tT[:, kt * S + qi * 512: kt * S + (qi + 1) * 512],
                start=(kt == 0), stop=(kt == KT - 1), skip_group_check=True),
                waits=mmw if kt == 0 else None,
                inc="pe" if kt == KT - 1 else None)
        pn = p.n("pe")
        sdep = [pd_dma[nonlocal_ns.slot_dma[slot]]] if slot in nonlocal_ns.slot_dma else []
        if ev_eng == "scalar":
            p.emit("scalar", lambda e, bank=bank, slot=slot, bcol=bcol, dt_=dt_: e.activation(
                qstg[:, slot * 512:(slot + 1) * 512], p_dn[bank][:, :], AF.Identity,
                bias=bcol[:, dt_:dt_ + 1]),
                waits=[("pe", pn)] + sdep, inc="act")
            evd = ("act", p.n("act"))
        else:
            p.emit("vector", lambda e, bank=bank, slot=slot, bcol=bcol, dt_=dt_: e.tensor_scalar_add(
                qstg[:, slot * 512:(slot + 1) * 512], p_dn[bank][:, :], bcol[:, dt_:dt_ + 1]),
                waits=[("pe", pn)] + sdep, inc="dve")
            evd = ("dve", p.n("dve"))
        dn_evac[n] = evd
        pd_dma[(ti, dt_, qi)] = p.dma(
            "gpsimd", scr[dt_, :, qi * 512:(qi + 1) * 512],
            qstg[:, slot * 512:(slot + 1) * 512], waits=[evd], inc="sd%d" % slot)
        nonlocal_ns.slot_dma[slot] = (ti, dt_, qi)
        nonlocal_ns.nchunk += 1

    emit_dense_chunk.nchunk = 0
    emit_dense_chunk.slot_dma = {}

    for st in range(NT + 2):
        if st < NT:
            emit_l1_s1(st)
        if 1 <= st <= NT:
            emit_l1_s2(st - 1)
        if 2 <= st <= NT + 1:
            emit_l1_s3(st - 2)
        if st >= 4:
            emit_v(st - 4)
        if st >= 7 and ((st - 7) % 4 == 0) and (st - 7) // 4 < 7:
            qi = (st - 7) // 4
            for dt_ in range(KT):
                emit_dense_chunk(0, dt_, qi, Wqk_sb, qT_scr, biasq8, d_wqk, "scalar")
    for st in range(NT - 2, NT):
        emit_v(st)
    for dt_ in range(KT):
        emit_dense_chunk(0, dt_, 7, Wqk_sb, qT_scr, biasq8, d_wqk, "scalar")

    kdep = [("pe", p.n("pe"))]
    p.dma("gpsimd", Wqk_sb[:, :].rearrange("p (k n) -> p k n", k=KT),
          Wk_d[:, :].rearrange("(k p) n -> p k n", p=128), waits=kdep, inc="swg")
    wqkn = p.n("swg")
    for ch in range(2):
        for kt in range(KT):
            p.emit("tensor", lambda e, kt=kt, ch=ch: e.matmul(
                p_dn[0][0:1, 0:512], saT_b[:, kt:kt + 1],
                Wqk_sb[:, kt * H + ch * 512: kt * H + (ch + 1) * 512],
                start=(kt == 0), stop=(kt == KT - 1), skip_group_check=True),
                waits=[("swg", wqkn), ("dve", p.n("dve")),
                       dn_evac.get(emit_dense_chunk.nchunk - 2, ("pl", 0)),
                       dn_evac.get(emit_dense_chunk.nchunk - 1, ("pl", 0))]
                if kt == 0 else None,
                inc="pe" if kt == KT - 1 else None)
        pn = p.n("pe")
        p.emit("vector", lambda e, ch=ch: e.tensor_copy(
            nrow2[0:1, ch * 512:(ch + 1) * 512], p_dn[0][0:1, 0:512]),
            waits=[("pe", pn)], inc="dve")
    dn = p.n("dve")
    for dt_ in range(KT):
        p.emit("tensor", lambda e, dt_=dt_: e.matmul(
            p_dn[1][:, dt_:dt_ + 1], nrow2[0:1, dt_ * 128:(dt_ + 1) * 128],
            ones_row_f[0:1, 0:1], start=True, stop=True, skip_group_check=True),
            waits=[("dve", dn)], inc="pe" if dt_ == KT - 1 else None)
    pn = p.n("pe")
    p.emit("vector", lambda e: e.tensor_add(biask8[:, :], p_dn[1][:, 0:KT], bkT[:, :]),
           waits=[("pe", pn)], inc="dve")
    for kt in range(KT):
        p.emit("vector", lambda e, kt=kt: e.tensor_scalar_mul(
            Wqk_sb[:, kt * H:(kt + 1) * H], Wqk_sb[:, kt * H:(kt + 1) * H],
            ca1T[:, kt:kt + 1]),
            waits=[("pe", pn), ("dve", d_ca1)] if kt == 0 else None,
            inc="dve" if kt == KT - 1 else None)
    d_wk2 = p.n("dve")
    for dt_ in range(KT):
        for qi in range(KT):
            emit_dense_chunk(1, dt_, qi, Wqk_sb, kT_scr, biask8, d_wk2, "vector")
    n_pd_all = [("sd%d" % s_, p.n("sd%d" % s_)) for s_ in range(4)]
    n_pd = {}
    for hh in range(HEADS):
        n_pd[(0, hh)] = [pd_dma[(0, hh, qi)] for qi in range(KT)]
        n_pd[(1, hh)] = [pd_dma[(1, hh, qi)] for qi in range(KT)]
    cmb.close()

    qT2 = SB(asc, "qT2", [128, 2 * S], BF16)
    kT2 = SB(asc, "kT2", [128, 2 * S], BF16)
    sq_scr = SB(asc, "sq_scr", [128, 2 * 1024], BF16)
    expT = SB(asc, "expT", [128, 3 * QCW], BF16)
    ao_ch = SB(asc, "ao_ch", [128, 2 * QCW], BF16)
    rec8 = SB(asc, "rec8", [128, 8], F32)
    qk_ps = [PS(asc, "qk0", [128, 1024]), PS(asc, "qk1", [128, 1024])]
    av_ps = [PS(asc, "avA", [128, 387]), PS(asc, "avB", [128, 387]),
             PS(asc, "avC", [128, 258])]
    p_pr = PS(asc, "p_pr", [128, 512])

    AVJ = [(0, 0), (0, 129), (0, 258), (1, 0), (1, 129), (1, 258), (2, 0), (2, 129)]

    gidx = 0
    exp_act = {}
    evac_dve = {}
    evac_banks = {}
    aoch_dma = {}
    slot_proj_pe = {0: 0, 1: 0}
    av_bank_pe = {}
    proj_dve = {}
    bnd_act = {}

    def prep_units(h):
        sl = h % 2
        qT = qT2[:, sl * S:(sl + 1) * S]
        kTs = kT2[:, sl * S:(sl + 1) * S]
        wdep = ([("pe", slot_proj_pe[sl])] if h > 1
                else fence(dma=("sd0", "sd1", "sd2", "sd3")))
        qsem = "sqk%d" % sl
        p.dma("gpsimd", qT, qT_scr[h, :, :],
              waits=wdep + n_pd[(0, h)], inc=qsem)
        p.dma("gpsimd", kTs, kT_scr[h, :, :],
              waits=n_pd[(1, h)], inc=qsem)
        proj_dve[h] = (qsem, p.n(qsem))
        yield
        rowcp = 0
        rmaxn = [0]
        for t, src_ in enumerate((qT, kTs)):
            for c4 in range(4):
                sq_half = (t * 4 + c4) % 2
                sqdep = [proj_dve[h]]
                if (t * 4 + c4) >= 2:
                    sqdep.append(("pe", p.n("pe")))
                p.emit("vector", lambda e, src_=src_, c4=c4, sq_half=sq_half: e.tensor_mul(
                    sq_scr[:, sq_half * 1024:(sq_half + 1) * 1024],
                    src_[:, c4 * 1024:(c4 + 1) * 1024],
                    src_[:, c4 * 1024:(c4 + 1) * 1024]), waits=sqdep, inc="dve")
                sqn = ("dve", p.n("dve"))
                yield
                yield
                for c2 in range(2):
                    cid = c4 * 2 + c2
                    p.emit("tensor", lambda e, c2=c2, sq_half=sq_half: e.matmul(
                        p_pr[0:1, 0:512], ones_col_b[:, 0:1],
                        sq_scr[:, sq_half * 1024 + c2 * 512: sq_half * 1024 + (c2 + 1) * 512],
                        start=True, stop=True, skip_group_check=True),
                        waits=[sqn, ("dve", rowcp)], inc="pe")
                    pn = p.n("pe")
                    p.emit("vector", lambda e: e.tensor_copy(
                        nrow[0:1, 0:512], p_pr[0:1, 0:512]),
                        waits=[("pe", pn), ("dve", rmaxn[0])], inc="dve")
                    an = p.n("dve")
                    rowcp = an
                    p.emit("vector", lambda e, t=t, cid=cid: e.reduce_max(
                        bnd16[0:1, t * 8 + cid: t * 8 + cid + 1], nrow[0:1, 0:512],
                        axis=AXX), waits=[("dve", an)], inc="dve")
                    rmaxn[0] = p.n("dve")
                    yield
            p.emit("vector", lambda e, t=t: e.reduce_max(
                bnd[0:1, t:t + 1], bnd16[0:1, t * 8:(t + 1) * 8], axis=AXX),
                waits=[("dve", p.n("dve"))], inc="dve")
        dn = p.n("dve")
        p.emit("vector", lambda e: e.tensor_mul(bnd[0:1, 2:3], bnd[0:1, 0:1], bnd[0:1, 1:2]),
               waits=[("dve", dn)], inc="dve")
        dn = p.n("dve")
        p.emit("scalar", lambda e: e.activation(bnd[0:1, 3:4], bnd[0:1, 2:3], AF.Ln),
               waits=[("dve", dn)], inc="act")
        lnn = p.n("act")
        p.emit("scalar", lambda e: e.activation(bnd[0:1, 4:5], bnd[0:1, 3:4], AF.Exp,
                                                scale=0.5, bias=nhl_col[0:1, 0:1]),
               waits=[("act", lnn)], inc="act")
        an = p.n("act")
        yield
        p.emit("tensor", lambda e: e.matmul(
            p_pr[:, 400:401], ones_row_f[0:1, :], bnd[0:1, 4:5],
            start=True, stop=True, skip_group_check=True),
            waits=[("act", an), ("dve", p.n("dve"))], inc="pe")
        pn = p.n("pe")
        yield
        p.emit("scalar", lambda e, h=h: e.activation(
            Bneg[:, h:h + 1], p_pr[:, 400:401], AF.Copy, scale=-1.0),
            waits=[("pe", pn)], inc="act")
        bnd_act[h] = p.n("act")
        yield

    for _ in prep_units(0):
        pass

    for h in range(HEADS):
        sl = h % 2
        qT = qT2[:, sl * S:(sl + 1) * S]
        kTs = kT2[:, sl * S:(sl + 1) * S]
        nxt = prep_units(h + 1) if h + 1 < HEADS else iter(())

        def emit_qk(kt, g, qc):
            pp = g % 2
            qkdep = [("act", bnd_act[h]), proj_dve[h], ("dve", v_comb[NT - 1])]
            if g >= 2:
                qkdep.append(("act", exp_act[g - 2]))
            for half in range(2):
                p.emit("tensor", lambda e, pp=pp, kt=kt, qc=qc, half=half, qT=qT, kTs=kTs: e.matmul(
                    qk_ps[pp][:, half * 512:(half + 1) * 512],
                    kTs[:, kt * 128:(kt + 1) * 128],
                    qT[:, qc * QCW + half * 512: qc * QCW + (half + 1) * 512],
                    start=True, stop=True, skip_group_check=True),
                    waits=qkdep if half == 0 else None,
                    inc="pe" if half == 1 else None)
            qkn = p.n("pe")
            ppe = g % 3
            p.emit("scalar", lambda e, pp=pp, ppe=ppe, h=h: e.activation(
                expT[:, ppe * QCW:(ppe + 1) * QCW], qk_ps[pp][:, :], AF.Exp,
                scale=ISQ, bias=Bneg[:, h:h + 1]),
                waits=[("pe", qkn), ("act", bnd_act[h])], inc="act")
            exp_act[g] = p.n("act")

        def emit_av(kt, g, qc):
            ppe = g % 3
            avdep = [("act", exp_act[g])]
            m_prev = (h * NQC + qc) - 1
            for j in range(8):
                ti, off = AVJ[j]
                fib = (kt == 0) and (off == 0)
                w_ = None
                if j == 0:
                    w_ = list(avdep)
                if kt == 0 and fib and g >= NT:
                    w_ = (w_ or []) + [("dve", evac_banks[m_prev][ti])]
                p.emit("tensor", lambda e, ti=ti, off=off, j=j, kt=kt, ppe=ppe, h=h, fib=fib: e.matmul(
                    av_ps[ti][:, off:off + 129],
                    expT[:, ppe * QCW + j * 128: ppe * QCW + (j + 1) * 128],
                    v_aug[:, kt * 516 + (h % 4) * 129: kt * 516 + (h % 4) * 129 + 129],
                    start=fib, stop=(kt == NT - 1), skip_group_check=True),
                    waits=w_,
                    inc="pe" if (kt == NT - 1 and j in (2, 5, 7)) else None)
                if kt == NT - 1 and j in (2, 5, 7):
                    av_bank_pe[ti] = p.n("pe")

        for qc in range(NQC):
            g0 = gidx
            for kt in range(NT):
                emit_qk(kt, g0 + kt, qc)
                if kt >= 2:
                    emit_av(kt - 2, g0 + kt - 2, qc)
                gk = qc * NT + kt
                if gk == 0 or gk >= 16:
                    next(nxt, None)
            emit_av(NT - 2, g0 + NT - 2, qc)
            emit_av(NT - 1, g0 + NT - 1, qc)
            gidx = g0 + NT
            m = h * NQC + qc
            ppq = qc % 2
            evdep = []
            if m >= 2:
                evdep.append(aoch_dma[m - 2])
            p.emit("vector", lambda e: e.reciprocal(
                rec8[:, 0:3], av_ps[0][:, :].rearrange("p (j c) -> p j c", j=3)[:, :, 128]),
                waits=evdep + [("pe", av_bank_pe[0])])
            p.emit("vector", lambda e: e.reciprocal(
                rec8[:, 3:6], av_ps[1][:, :].rearrange("p (j c) -> p j c", j=3)[:, :, 128]),
                waits=[("pe", av_bank_pe[1])])
            p.emit("vector", lambda e: e.reciprocal(
                rec8[:, 6:8], av_ps[2][:, :].rearrange("p (j c) -> p j c", j=2)[:, :, 128]),
                waits=[("pe", av_bank_pe[2])], inc="dve")
            rcn = p.n("dve")
            bank_free = {}
            for j in range(8):
                ti, off = AVJ[j]
                p.emit("vector", lambda e, ti=ti, off=off, j=j, ppq=ppq: e.tensor_scalar_mul(
                    ao_ch[:, ppq * QCW + j * 128: ppq * QCW + (j + 1) * 128],
                    av_ps[ti][:, off:off + 128], rec8[:, j:j + 1]),
                    waits=[("dve", rcn)] if j == 0 else None,
                    inc="dve" if j in (2, 5, 7) else None)
                if j in (2, 5, 7):
                    bank_free[ti] = p.n("dve")
            evac_dve[m] = p.n("dve")
            evac_banks[m] = bank_free
            aoch_dma[m] = p.dma(
                "sync",
                ao_scr[qc * QCW:(qc + 1) * QCW, h * 128:(h + 1) * 128]
                .rearrange("(j p) d -> p j d", p=128),
                ao_ch[:, ppq * QCW:(ppq + 1) * QCW].rearrange("p (j d) -> p j d", j=8),
                waits=[("dve", evac_dve[m])], inc="sas%d" % (m % 2))
        for _ in nxt:
            pass
        slot_proj_pe[sl] = p.n("pe")
    a_attn_pe_end = p.n("pe")
    n_ao_stores = [("sas0", p.n("sas0")), ("sas1", p.n("sas1"))]

    asc.close()
    aosc = ExitStack()
    ao_t = SB(aosc, "ao_t", [128, 8 * H], BF16)
    p_tr2 = [PS(aosc, "ptr2a", [128, 1024], BF16), PS(aosc, "ptr2b", [128, 1024], BF16)]

    ao_dve, ao_act, ao_trpe = {}, {}, {}
    ao_ms, ao_mu, ao_rs = {}, {}, {}

    def ao_s1(st):
        av = ao_t[:, (st % 8) * H:(st % 8 + 1) * H]
        dep = list(n_ao_stores)
        if st >= 8:
            dep += [("dve", ao_dve.get(st - 8, 0)), ("act", ao_act.get(st - 8, 0))]
        ldn = p.dma("sync", av, ao_scr[st * 128:(st + 1) * 128, :], waits=dep,
                    inc="sal%d" % (st % 8))
        p.emit("vector", lambda e, st=st, av=av: e.reduce_sum(
            stats["amus"][:, st:st + 1], av, axis=AXX),
            waits=[ldn], inc="dve")
        ao_ms[st] = p.n("dve")
        p.emit("scalar", lambda e, st=st, av=av: e.activation(
            deadout[:, :], av, AF.Square, accum_out=stats["assq"][:, st:st + 1]),
            waits=[ldn, ("act", last_dead[0])], inc="act")
        ao_act[st] = p.n("act")
        last_dead[0] = ao_act[st]

    def ao_s2(st):
        p.emit("vector", lambda e, st=st: e.tensor_mul(
            stats["amt"][:, st:st + 1], stats["amus"][:, st:st + 1],
            stats["amus"][:, st:st + 1]), waits=[("dve", ao_ms[st])], inc="dve")
        t1n = p.n("dve")
        p.emit("vector", lambda e, st=st: e.scalar_tensor_tensor(
            stats["avt"][:, st:st + 1], stats["amt"][:, st:st + 1], 1.0 / H,
            stats["assq"][:, st:st + 1], ALU.mult, ALU.subtract),
            waits=[("act", ao_act[st]), ("dve", t1n)], inc="dve")
        dn = p.n("dve")
        p.emit("vector", lambda e, st=st: e.tensor_scalar_mul(
            stats["amu"][:, st:st + 1], stats["amus"][:, st:st + 1], 1.0 / H),
            inc="dve")
        ao_mu[st] = p.n("dve")
        p.emit("scalar", lambda e, st=st: e.activation(
            stats["avt"][:, st:st + 1], stats["avt"][:, st:st + 1], AF.Ln,
            scale=-1.0 / H, bias=eps_col[:, 0:1]),
            waits=[("dve", dn), ("pl", pl_const)], inc="act")
        lnn = p.n("act")
        p.emit("scalar", lambda e, st=st: e.activation(
            stats["arstd"][:, st:st + 1], stats["avt"][:, st:st + 1], AF.Exp, scale=-0.5),
            waits=[("act", lnn)], inc="act")
        ao_rs[st] = p.n("act")

    def ao_s3(st):
        pp = st % 2
        av = ao_t[:, (st % 8) * H:(st % 8 + 1) * H]
        hv = h_bf[:, pp * H:(pp + 1) * H]
        hdep = [("act", ao_rs[st]), ("dve", ao_mu[st])]
        if st >= 2:
            hdep.append(("pe", ao_trpe[st - 2]))
        p.emit("vector", lambda e, av=av, hv=hv, st=st: e.tensor_scalar(
            hv, av, stats["amu"][:, st:st + 1], stats["arstd"][:, st:st + 1],
            ALU.subtract, ALU.mult), waits=hdep, inc="dve")
        ao_dve[st] = p.n("dve")
        trdep = [("dve", ao_dve[st]), ("pe", a_attn_pe_end)]
        if st >= 2:
            trdep.append(("act", ao_act.get(st - 2, 0)))
        for kt in range(KT):
            p.emit("tensor", lambda e, kt=kt, pp=pp, hv=hv: e.transpose(
                p_tr2[pp][:, kt * 128:(kt + 1) * 128],
                hv.rearrange("p (k c) -> p k c", k=KT)[:, kt, :], ident_b[:, :]),
                waits=trdep if kt == 0 else None,
                inc="pe" if kt == KT - 1 else None)
        ao_trpe[st] = p.n("pe")
        p.emit("scalar", lambda e, st=st, pp=pp: e.activation(
            tT[:, :].rearrange("p (k s) -> p k s", k=KT)[:, :, st * 128:(st + 1) * 128],
            p_tr2[pp][:, :].rearrange("p (k s) -> p k s", k=KT), AF.Copy),
            waits=[("pe", ao_trpe[st])], inc="act")
        ao_act[st] = p.n("act")

    for st in range(NT + 2):
        if st < NT:
            ao_s1(st)
        if 1 <= st <= NT:
            ao_s2(st - 1)
        if 2 <= st <= NT + 1:
            ao_s3(st - 2)
    aosc.close()

    osc = ExitStack()
    Wo_sb = SB(osc, "Wo_sb", [128, KT * H], BF16)
    gac_r = SB(osc, "gac_r", [128, H], F32)
    ga_eff_r = SB(osc, "ga_eff_r", [128, H], F32)
    gabo_r = SB(osc, "gabo_r", [128, H], F32)
    boe_r = SB(osc, "boe_r", [128, H], F32)
    boerow = SB(osc, "boerow", [1, H], F32)
    borow = SB(osc, "borow", [1, H], F32)
    x1_sb = SB(osc, "x1_sb", [128, 2 * H], F32)
    evac32 = SB(osc, "evac32", [128, 1024], F32)
    p_y = [PS(osc, "py%d" % i, [128, 512]) for i in range(4)]
    p_ob = PS(osc, "p_ob", [1, 1024])
    p_bc3 = PS(osc, "p_bc3", [128, 512])

    ofence = fence(dma=("sqk0", "sqk1", "sas0", "sas1", "sal0", "sal1",
                        "sal2", "sal3", "sal4", "sal5", "sal6", "sal7"))
    nbo = p.dma("sync", borow[0:1, :], bo_d[None, :], waits=ofence, inc="sb1")
    won = p.dma("gpsimd", Wo_sb[:, :].rearrange("p (k n) -> p k n", k=KT),
                Wo_d[:, :].rearrange("(k p) n -> p k n", p=128), waits=ofence, inc="swo")
    for kt in range(KT):
        for ch in range(2):
            p.emit("tensor", lambda e, kt=kt, ch=ch: e.matmul(
                p_ob[0:1, ch * 512:(ch + 1) * 512], lnbT_b[:, kt:kt + 1],
                Wo_sb[:, kt * H + ch * 512: kt * H + (ch + 1) * 512],
                start=(kt == 0), stop=(kt == KT - 1), skip_group_check=True),
                waits=[won, ("act", 1)],
                inc="pe" if (kt == KT - 1 and ch == 1) else None)
    pn = p.n("pe")
    p.emit("vector", lambda e: e.scalar_tensor_tensor(
        boerow[0:1, :], p_ob[0:1, :], 0.2, borow[0:1, :], ALU.mult, ALU.add),
        waits=[("pe", pn), nbo], inc="dve")
    d_boe = p.n("dve")
    for kt in range(KT):
        p.emit("vector", lambda e, kt=kt: e.tensor_scalar_mul(
            Wo_sb[:, kt * H:(kt + 1) * H], Wo_sb[:, kt * H:(kt + 1) * H],
            lnwT[:, kt:kt + 1]),
            waits=[("sx", n_small)] if kt == 0 else None,
            inc="dve" if kt == KT - 1 else None)
    d_wo = p.n("dve")
    last_bc = 0
    for dst, lhs, srcrow, wdep in ((gac_r, ones_row_f, gac_row, d_clamp),
                                   (ga_eff_r, p2_row_f, gac_row, d_clamp),
                                   (boe_r, ones_row_f, boerow, d_boe)):
        for ch in range(2):
            dep = [("dve", wdep), ("pl", pl_const)]
            if last_bc:
                dep.append(("act", last_bc))
            p.emit("tensor", lambda e, lhs=lhs, srcrow=srcrow, ch=ch: e.matmul(
                p_bc3[:, :], lhs[0:1, :], srcrow[0:1, ch * 512:(ch + 1) * 512],
                start=True, stop=True, skip_group_check=True),
                waits=dep, inc="pe")
            pn = p.n("pe")
            p.emit("scalar", lambda e, dst=dst, ch=ch: e.activation(
                dst[:, ch * 512:(ch + 1) * 512], p_bc3[:, :], AF.Copy),
                waits=[("pe", pn)], inc="act")
            last_bc = p.n("act")
    p.emit("vector", lambda e: e.tensor_mul(gabo_r[:, :], gac_r[:, :], boe_r[:, :]),
           waits=[("act", last_bc)], inc="dve")
    d_gabo = p.n("dve")

    o_dve, o_act, x1_dma, o_pl, o_sq = {}, {}, {}, {}, {}

    def emit_o_stats2(st):
        p.emit("vector", lambda e, st=st: e.tensor_mul(
            stats["m2t"][:, st:st + 1], stats["mu2s"][:, st:st + 1],
            stats["mu2s"][:, st:st + 1]), waits=[("act", o_sq[st])], inc="dve")
        t1n = p.n("dve")
        p.emit("vector", lambda e, st=st: e.scalar_tensor_tensor(
            stats["v2t"][:, st:st + 1], stats["m2t"][:, st:st + 1], 1.0 / H,
            stats["ssq2"][:, st:st + 1], ALU.mult, ALU.subtract),
            waits=[("dve", t1n)], inc="dve")
        dn = p.n("dve")
        p.emit("vector", lambda e, st=st: e.tensor_scalar_mul(
            stats["mu2"][:, st:st + 1], stats["mu2s"][:, st:st + 1], 1.0 / H),
            inc="dve")
        p.emit("scalar", lambda e, st=st: e.activation(
            stats["v2t"][:, st:st + 1], stats["v2t"][:, st:st + 1], AF.Ln,
            scale=-1.0 / H, bias=eps_col[:, 0:1]),
            waits=[("dve", dn), ("pl", pl_const)], inc="act")
        lnn = p.n("act")
        p.emit("scalar", lambda e, st=st: e.activation(
            stats["rstd2"][:, st:st + 1], stats["v2t"][:, st:st + 1], AF.Exp, scale=-0.5),
            waits=[("act", lnn)], inc="act")
        o_act[st] = p.n("act")

    for st in range(NT):
        pp = st % 2
        xv = x_tiles[:, (st % 4) * H:(st % 4 + 1) * H]
        x1v = x1_sb[:, pp * H:(pp + 1) * H]
        if st < 4:
            xdep = [("dve", ln1_dve[NT - 4 + st]), ("act", ln1_act[NT - 4 + st])]
        else:
            xdep = [("dve", o_dve[st - 4]), ("pl", o_pl[st - 4])]
        xn = p.dma("sync", xv, x_d[st * 128:(st + 1) * 128, :], waits=xdep,
                   inc="sx%d" % (st % 4))
        p.emit("gpsimd", lambda e, xv=xv: e.tensor_add(xv, xv, gabo_r[:, :]),
               waits=[xn, ("dve", d_gabo)], inc="pl")
        o_pl[st] = p.n("pl")
        mmdep = [("dve", d_wo), ("act", ao_act[st])]
        if st >= 2:
            mmdep.append(("dve", o_dve[st - 2]))
        for ch in range(2):
            pbank = p_y[2 * pp + ch]
            for kt in range(KT):
                aT_t = tT[:, kt * S + st * 128: kt * S + (st + 1) * 128]
                p.emit("tensor", lambda e, pbank=pbank, aT_t=aT_t, kt=kt, ch=ch: e.matmul(
                    pbank[:, :], aT_t, Wo_sb[:, kt * H + ch * 512: kt * H + (ch + 1) * 512],
                    start=(kt == 0), stop=(kt == KT - 1), skip_group_check=True),
                    waits=mmdep if (kt == 0 and ch == 0) else None,
                    inc="pe" if (kt == KT - 1 and ch == 1) else None)
        opn = p.n("pe")
        evdep = [("pe", opn), xn, ("dve", d_gabo)]
        if st >= 2:
            evdep.append(x1_dma[st - 2])
        if st >= 1:
            evdep.append(("dve", o_dve[st - 1]))
        for ch in range(2):
            pbank = p_y[2 * pp + ch]
            cs = slice(ch * 512, (ch + 1) * 512)
            p.emit("vector", lambda e, pbank=pbank, cs=cs: e.tensor_mul(
                evac32[:, cs], pbank[:, :], ga_eff_r[:, cs]),
                waits=evdep if ch == 0 else None, inc="dve")
        mn = p.n("dve")
        for ch in range(2):
            cs = slice(ch * 512, (ch + 1) * 512)
            p.emit("vector", lambda e, cs=cs, x1v=x1v, xv=xv: e.tensor_add(
                x1v[:, cs], evac32[:, cs], xv[:, cs]),
                waits=[("pl", o_pl[st]), ("dve", mn)] if ch == 0 else None,
                inc="dve" if ch == 1 else None)
        o_dve[st] = p.n("dve")
        p.emit("scalar", lambda e, st=st, x1v=x1v: e.activation(
            deadout[:, :], x1v, AF.Copy, accum_out=stats["mu2s"][:, st:st + 1]),
            waits=[("dve", o_dve[st]), ("act", last_dead[0])], inc="act")
        last_dead[0] = p.n("act")
        p.emit("scalar", lambda e, st=st, x1v=x1v: e.activation(
            deadout[:, :], x1v, AF.Square, accum_out=stats["ssq2"][:, st:st + 1]),
            waits=[("act", last_dead[0])], inc="act")
        o_sq[st] = p.n("act")
        last_dead[0] = o_sq[st]
        if st >= 1:
            x1_dma[st - 1] = p.dma(
                "gpsimd", x1_scr[(st - 1) * 128:st * 128, :],
                x1_sb[:, ((st - 1) % 2) * H:((st - 1) % 2 + 1) * H],
                waits=[("dve", o_dve[st - 1])], inc="sx1%d" % ((st - 1) % 2))
        if st == NT - 1:
            x1_dma[st] = p.dma("gpsimd", x1_scr[st * 128:(st + 1) * 128, :], x1v,
                               waits=[("dve", o_dve[st])], inc="sx1%d" % (st % 2))
        if st >= 1:
            emit_o_stats2(st - 1)
        if st == NT - 1:
            emit_o_stats2(st)
    tap("x1t0", x1_sb[:, 0:H], [("dve", o_dve[NT-2])])
    tap("mu2", stats["mu2"][:, :], [("act", o_act[NT-1])])
    tap("rstd2", stats["rstd2"][:, :], [("act", o_act[NT-1])])
    tap("gabo", gabo_r[:, :], [("dve", d_gabo)])
    tap("gaeff", ga_eff_r[:, :], [("dve", d_gabo)])
    osc.close()
    tsc.close()
    pre.close()

    fsc = ExitStack()
    w1_sb = SB(fsc, "w1_sb", [128, KT * FF], BF16)
    w2_sb = SB(fsc, "w2_sb", [128, MT * H], BF16)
    g2_sb = SB(fsc, "g2_sb", [128, MT * SBW], BF16)
    h2T = SB(fsc, "h2T", [128, 2 * KT * SBW], BF16)
    h2_bf = SB(fsc, "h2_bf", [128, 2 * H], BF16)
    gb_eff_r = SB(fsc, "gb_eff_r", [128, H], BF16)
    gbb2_r = SB(fsc, "gbb2_r", [128, H], F32)
    b1T = SB(fsc, "b1T", [128, MT], F32)
    b1pT = SB(fsc, "b1pT", [128, MT], F32)
    x1f = SB(fsc, "x1f", [128, 4 * H], F32)
    x2st = SB(fsc, "x2st", [128, 2 * H], F32)
    b2r_f = x2st
    p_u = [PS(fsc, "pu0", [128, 256]), PS(fsc, "pu1", [128, 256])]
    p_y2 = [PS(fsc, "py2%d" % i, [128, 512]) for i in range(4)]
    p_trF = PS(fsc, "p_trF", [128, 1024], BF16)
    p_fb = PS(fsc, "p_fb", [128, 512])

    ffence = fence(dma=("sx10", "sx11", "sx0", "sx1", "sx2", "sx3", "swo",
                        "sal0", "sal1", "sal2", "sal3", "sal4", "sal5",
                        "sal6", "sal7", "sas0", "sas1"))
    for kt in range(MT):
        p.dma("sync", b1T[:, kt:kt + 1], b1_d[kt * 128:(kt + 1) * 128][:, None],
              waits=ffence, inc="sb1")
    n_b1 = ("sb1", p.n("sb1"))
    for q4 in range(4):
        p.dma("gpsimd",
              w1_sb[:, :].rearrange("p (k f) -> p k f", k=KT)[:, 2 * q4:2 * q4 + 2, :],
              w1_d[:, :].rearrange("(k p) f -> p k f", p=128)[:, 2 * q4:2 * q4 + 2, :],
              waits=ffence, inc="sw1")
    w1n = ("sw1", p.n("sw1"))
    for q4 in range(4):
        p.dma("gpsimd",
              w2_sb[:, :].rearrange("p (k f) -> p k f", k=MT)[:, 8 * q4:8 * q4 + 8, :],
              w2_d[:, :].rearrange("(k p) f -> p k f", p=128)[:, 8 * q4:8 * q4 + 8, :],
              waits=ffence, inc="sw2")
    w2n = ("sw2", p.n("sw2"))

    last_bc = 0
    first_bc = True
    for dst, srcrow, dt_ in ((gb_eff_r, gbc_row, BF16), (gbb2_r, gbc_row, F32),
                             (b2r_f, b2row, F32)):
        for ch in range(2):
            dep = [("dve", d_clamp), ("sx", n_small)]
            if first_bc:
                dep += ffence
                first_bc = False
            if last_bc:
                dep.append(("act", last_bc))
            p.emit("tensor", lambda e, srcrow=srcrow, ch=ch: e.matmul(
                p_fb[:, :], ones_row_f[0:1, :], srcrow[0:1, ch * 512:(ch + 1) * 512],
                start=True, stop=True, skip_group_check=True),
                waits=dep, inc="pe")
            pn = p.n("pe")
            p.emit("scalar", lambda e, dst=dst, ch=ch: e.activation(
                dst[:, ch * 512:(ch + 1) * 512], p_fb[:, :], AF.Copy),
                waits=[("pe", pn)], inc="act")
            last_bc = p.n("act")
    p.emit("vector", lambda e: e.tensor_mul(gbb2_r[:, :], gbb2_r[:, :], b2r_f[:, 0:H]),
           waits=[("act", last_bc)] + ffence, inc="dve")
    d_gbb2 = p.n("dve")

    b1cp = p.n("act")
    for c8 in range(8):
        for kt in range(KT):
            p.emit("tensor", lambda e, c8=c8, kt=kt: e.matmul(
                p_fb[0:1, 0:512], sbT_b[:, kt:kt + 1],
                w1_sb[:, kt * FF + c8 * 512: kt * FF + (c8 + 1) * 512],
                start=(kt == 0), stop=(kt == KT - 1), skip_group_check=True),
                waits=[w1n, ("act", a_modT), ("act", b1cp)] if kt == 0 else None,
                inc="pe" if kt == KT - 1 else None)
        pn = p.n("pe")
        p.emit("scalar", lambda e: e.activation(nrow[0:1, 0:512], p_fb[0:1, 0:512], AF.Copy),
               waits=[("pe", pn)], inc="act")
        an = p.n("act")
        b1cp = an
        for t4 in range(4):
            p.emit("tensor", lambda e, c8=c8, t4=t4: e.matmul(
                p_y2[0][:, c8 * 4 + t4: c8 * 4 + t4 + 1],
                nrow[0:1, t4 * 128:(t4 + 1) * 128], ones_row_f[0:1, 0:1],
                start=True, stop=True, skip_group_check=True),
                waits=[("act", an)], inc="pe" if t4 == 3 else None)
    pn = p.n("pe")
    p.emit("scalar", lambda e: e.activation(b1pT[:, 0:MT], p_y2[0][:, 0:MT], AF.Copy),
           waits=[("pe", pn)], inc="act")
    an = p.n("act")
    p.emit("vector", lambda e: e.tensor_add(b1pT[:, :], b1pT[:, :], b1T[:, :]),
           waits=[("act", an), n_b1], inc="dve")
    d_b1p = p.n("dve")
    for kt in range(KT):
        p.emit("vector", lambda e, kt=kt: e.tensor_scalar_mul(
            w1_sb[:, kt * FF:(kt + 1) * FF], w1_sb[:, kt * FF:(kt + 1) * FF],
            cb1T[:, kt:kt + 1]), inc="dve" if kt == KT - 1 else None)
    d_w1 = p.n("dve")

    f_dve, out_dma, gelu_act, trF_act, u_pe, y2_pe, trF_pe = {}, {}, {}, {}, {}, {}, {}

    def f_prep(sb_i):
        ppb = sb_i % 2
        x1v = x1f[:, ppb * 2 * H:(ppb + 1) * 2 * H]
        xdep = [x1_dma[2 * sb_i], x1_dma[2 * sb_i + 1]] + (ffence if sb_i < 2 else [])
        if sb_i >= 2:
            xdep.append(("dve", f_dve[2 * (sb_i - 2) + 1]))
        xn = p.dma(
            "sync",
            x1v.rearrange("p (i c) -> p i c", i=2),
            x1_scr[sb_i * SBW:(sb_i + 1) * SBW, :].rearrange("(i p) c -> p i c", p=128),
            waits=xdep, inc="sf%d" % (sb_i % 2))
        for i in range(2):
            st = 2 * sb_i + i
            hdep = [xn, ("act", o_act[st])]
            if sb_i < 2:
                hdep = hdep + ffence
            if sb_i >= 1:
                hdep.append(("pe", trF_pe[sb_i - 1]))
            p.emit("vector", lambda e, st=st, i=i, ppb=ppb: e.tensor_scalar(
                h2_bf[:, i * H:(i + 1) * H],
                x1f[:, :].rearrange("p (s i c) -> p s i c", s=2, i=2)[:, ppb, i, :],
                stats["mu2"][:, st:st + 1], stats["rstd2"][:, st:st + 1],
                ALU.subtract, ALU.mult),
                waits=hdep, inc="dve" if i == 1 else None)
        dh = p.n("dve")
        for i in range(2):
            trdep = [("dve", dh)]
            if (sb_i, i - 1) in trF_act:
                trdep.append(("act", trF_act[(sb_i, i - 1)]))
            elif (sb_i - 1, 1) in trF_act:
                trdep.append(("act", trF_act[(sb_i - 1, 1)]))
            for kt in range(KT):
                p.emit("tensor", lambda e, kt=kt, i=i: e.transpose(
                    p_trF[:, kt * 128:(kt + 1) * 128],
                    h2_bf[:, i * H + kt * 128: i * H + (kt + 1) * 128],
                    ident_b[:, :]),
                    waits=trdep if kt == 0 else None,
                    inc="pe" if kt == KT - 1 else None)
            ptn = p.n("pe")
            trF_pe[sb_i] = ptn
            p.emit("scalar", lambda e, i=i, ppb=ppb: e.activation(
                h2T.rearrange("p (s k i c) -> p s k i c", s=2, k=KT, i=2)[:, ppb, :, i, :],
                p_trF[:, :].rearrange("p (k c) -> p k c", k=KT), AF.Copy),
                waits=[("pe", ptn)], inc="act")
            trF_act[(sb_i, i)] = p.n("act")

    f_prep(0)
    for sb_i in range(NSB):
        ppb = sb_i % 2
        a_h2T = trF_act[(sb_i, 1)]
        for mt in range(MT):
            ppu = mt % 2
            udep = [("act", a_h2T), ("dve", d_w1), ("dve", d_b1p)]
            gi = sb_i * MT + mt
            if gi >= 2:
                udep.append(("act", gelu_act[gi - 2]))
            for kt in range(KT):
                p.emit("tensor", lambda e, ppu=ppu, mt=mt, kt=kt, ppb=ppb: e.matmul(
                    p_u[ppu][:, :],
                    w1_sb[:, kt * FF + mt * 128: kt * FF + (mt + 1) * 128],
                    h2T[:, ppb * KT * SBW + kt * SBW: ppb * KT * SBW + (kt + 1) * SBW],
                    start=(kt == 0), stop=(kt == KT - 1), skip_group_check=True),
                    waits=udep if kt == 0 else None,
                    inc="pe" if kt == KT - 1 else None)
            pn = p.n("pe")
            gdep = [("pe", pn)]
            if sb_i >= 1 and mt == 0:
                gdep.append(("pe", y2_pe[sb_i - 1]))
            p.emit("scalar", lambda e, ppu=ppu, mt=mt: e.activation(
                g2_sb[:, mt * SBW:(mt + 1) * SBW], p_u[ppu][:, :], AF.Gelu,
                bias=b1pT[:, mt:mt + 1]),
                waits=gdep, inc="act")
            gelu_act[gi] = p.n("act")
        u_pe[sb_i] = p.n("pe")
        if sb_i + 1 < NSB:
            f_prep(sb_i + 1)
        for i in range(2):
            st = 2 * sb_i + i
            ppy = st % 2
            for ch in range(2):
                pbank = p_y2[2 * ppy + ch]
                for kt in range(MT):
                    ydep = [("act", gelu_act[sb_i * MT + kt])]
                    if st >= 2 and kt == 0 and ch == 0:
                        ydep.append(("dve", f_dve[st - 2]))
                    p.emit("tensor", lambda e, pbank=pbank, kt=kt, i=i, ch=ch: e.matmul(
                        pbank[:, :],
                        g2_sb[:, kt * SBW + i * 128: kt * SBW + i * 128 + 128],
                        w2_sb[:, kt * H + ch * 512: kt * H + (ch + 1) * 512],
                        start=(kt == 0), stop=(kt == MT - 1), skip_group_check=True),
                        waits=(ydep + [w2n]) if kt == 0 else
                              [("act", gelu_act[sb_i * MT + kt])],
                        inc="pe" if kt == MT - 1 else None)
            ypn = p.n("pe")
            y2_pe[sb_i] = ypn
            evdep = [("pe", ypn), ("dve", d_gbb2)]
            if st >= 2:
                evdep.append(out_dma[st - 2])
            xt = x2st[:, ppy * H:(ppy + 1) * H]
            for ch in range(2):
                pbank = p_y2[2 * ppy + ch]
                cs = slice(ch * 512, (ch + 1) * 512)
                p.emit("vector", lambda e, pbank=pbank, cs=cs, xt=xt: e.tensor_mul(
                    xt[:, cs], pbank[:, :], gb_eff_r[:, cs]),
                    waits=evdep if ch == 0 else None, inc="dve")
            mn = p.n("dve")
            p.emit("gpsimd", lambda e, xt=xt: e.tensor_add(
                xt[:, :], xt[:, :], gbb2_r[:, :]),
                waits=[("dve", mn), ("dve", d_gbb2)], inc="pl")
            gn = p.n("pl")
            for ch in range(2):
                cs = slice(ch * 512, (ch + 1) * 512)
                p.emit("vector", lambda e, cs=cs, xt=xt, i=i, ppb=ppb: e.tensor_add(
                    xt[:, cs],
                    xt[:, cs],
                    x1f.rearrange("p (s i c) -> p s i c", s=2, i=2)[:, ppb, i, :][:, cs]),
                    waits=[("pl", gn)] if ch == 0 else None,
                    inc="dve" if ch == 1 else None)
            f_dve[st] = p.n("dve")
            out_dma[st] = p.dma("sync", out_d[st * 128:(st + 1) * 128, :],
                                x2st[:, ppy * H:(ppy + 1) * H],
                                waits=[("dve", f_dve[st])],
                                inc="so%d" % (st % 2))
    p.emit("sync", lambda e: e.nop(),
           waits=[("so0", p.n("so0")), ("so1", p.n("so1"))])
    fsc.close()

    with nc.Block() as block:
        p.realize(block, sems)
    root.close()
    return nc


_NC_CACHE = None


def kernel(**inputs):
    global _NC_CACHE
    if _NC_CACHE is None:
        _NC_CACHE = build_nc()
    nc = _NC_CACHE

    x = np.ascontiguousarray(inputs["L0_attn_hidden_states"])
    c = np.ascontiguousarray(inputs["L0_c"])
    shared = {
        "Wq": np.ascontiguousarray(inputs["L0_attn_Wq"]),
        "Wk": np.ascontiguousarray(inputs["L0_attn_Wk"]),
        "Wv": np.ascontiguousarray(inputs["L0_attn_Wv"]),
        "Wo": np.ascontiguousarray(inputs["L0_attn_Wo"]),
        "bq": np.ascontiguousarray(inputs["L0_attn_bq"]),
        "bk": np.ascontiguousarray(inputs["L0_attn_bk"]),
        "bv": np.ascontiguousarray(inputs["L0_attn_bv"]),
        "bo": np.ascontiguousarray(inputs["L0_attn_bo"]),
        "lq1": np.ascontiguousarray(inputs["L0_attn_lambda_q1"]),
        "lk1": np.ascontiguousarray(inputs["L0_attn_lambda_k1"]),
        "lq2": np.ascontiguousarray(inputs["L0_attn_lambda_q2"]),
        "lk2": np.ascontiguousarray(inputs["L0_attn_lambda_k2"]),
        "lnw": np.ascontiguousarray(inputs["L0_attn_ln_w"]),
        "lnb": np.ascontiguousarray(inputs["L0_attn_ln_b"]),
        "w1": np.ascontiguousarray(inputs["L0_ffn_w1"]),
        "b1": np.ascontiguousarray(inputs["L0_ffn_b1"]),
        "w2": np.ascontiguousarray(inputs["L0_ffn_w2"]),
        "b2": np.ascontiguousarray(inputs["L0_ffn_b2"]),
        "wmod": np.ascontiguousarray(inputs["L0_wmod"]),
    }
    in_maps = []
    for b in range(B):
        m = dict(shared)
        m["x"] = np.ascontiguousarray(x[b])
        m["c"] = np.ascontiguousarray(c[b])
        in_maps.append(m)

    res = run_bass_kernel_spmd(nc, in_maps, core_ids=list(range(B)))
    out = np.stack([np.asarray(r["out"]) for r in res.results], axis=0).astype(np.float32)
    return out


if __name__ == "__main__":
    nc = build_nc()
    n = sum(1 for _ in nc.all_instructions())
    print("instructions:", n)


# version 29 (final)
# speedup vs baseline: 1.0355x; 1.0081x over previous
import numpy as np
from contextlib import ExitStack

import concourse.bass as bass
import concourse.mybir as mybir
from concourse.bass_utils import run_bass_kernel_spmd

F32 = mybir.dt.float32
BF16 = mybir.dt.bfloat16
AF = mybir.ActivationFunctionType
ALU = mybir.AluOpType
AXX = mybir.AxisListType.X

B, S, H, HEADS, HD, FF = 8, 4096, 1024, 8, 128, 4096
NT = S // 128
KT = H // 128
MT = FF // 128
QCW = 1024
NQC = S // QCW
SBW = 256
NSB = S // SBW
EPS = 1e-5
ISQ = float(1.0 / np.sqrt(HD))
LAMBDA_INIT = 0.8
NEG_HALF_LN_HD = float(-0.5 * np.log(HD))

ENGINES = ("tensor", "scalar", "vector", "gpsimd", "sync")
_DBG = False


class Prog:

    def __init__(self):
        self.streams = {e: [] for e in ENGINES}
        self.counts = {}

    def n(self, sem):
        return self.counts.get(sem, 0)

    def emit(self, eng, build, waits=None, inc=None, by=1):
        w = [(s, v) for (s, v) in (waits or []) if v > 0]
        self.streams[eng].append((build, w, inc, by))
        if inc is not None:
            self.counts[inc] = self.counts.get(inc, 0) + by
            return self.counts[inc]
        return None

    def dma(self, eng, dst, src, waits=None, inc="swg"):
        n = self.emit(eng, lambda e, d=dst, s=src: e.dma_start(d, s),
                      waits=waits, inc=inc, by=16)
        return (inc, n)

    def realize(self, block, sems):
        def section(eng_name):
            stream = self.streams[eng_name]

            def body(e):
                watermark = {}
                for build, waits, inc, by in stream:
                    for s_, v in waits:
                        if watermark.get(s_, -1) < v:
                            e.wait_ge(sems[s_], v)
                            watermark[s_] = v
                    inst = build(e)
                    if inc is not None:
                        inst.then_inc(sems[inc], by)
            return body

        block.sync(section("sync"))
        block.gpsimd(section("gpsimd"))
        block.tensor(section("tensor"))
        block.scalar(section("scalar"))
        block.vector(section("vector"))


def build_nc():
    nc = bass.Bass("TRN2", target_bir_lowering=False)

    x_d = nc.dram_tensor("x", [S, H], F32, kind="ExternalInput").ap()
    Wq_d = nc.dram_tensor("Wq", [H, H], F32, kind="ExternalInput").ap()
    Wk_d = nc.dram_tensor("Wk", [H, H], F32, kind="ExternalInput").ap()
    Wv_d = nc.dram_tensor("Wv", [H, H], F32, kind="ExternalInput").ap()
    Wo_d = nc.dram_tensor("Wo", [H, H], F32, kind="ExternalInput").ap()
    bq_d = nc.dram_tensor("bq", [H], F32, kind="ExternalInput").ap()
    bk_d = nc.dram_tensor("bk", [H], F32, kind="ExternalInput").ap()
    bv_d = nc.dram_tensor("bv", [H], F32, kind="ExternalInput").ap()
    bo_d = nc.dram_tensor("bo", [H], F32, kind="ExternalInput").ap()
    lq1_d = nc.dram_tensor("lq1", [HD], F32, kind="ExternalInput").ap()
    lk1_d = nc.dram_tensor("lk1", [HD], F32, kind="ExternalInput").ap()
    lq2_d = nc.dram_tensor("lq2", [HD], F32, kind="ExternalInput").ap()
    lk2_d = nc.dram_tensor("lk2", [HD], F32, kind="ExternalInput").ap()
    lnw_d = nc.dram_tensor("lnw", [H], F32, kind="ExternalInput").ap()
    lnb_d = nc.dram_tensor("lnb", [H], F32, kind="ExternalInput").ap()
    w1_d = nc.dram_tensor("w1", [H, FF], F32, kind="ExternalInput").ap()
    b1_d = nc.dram_tensor("b1", [FF], F32, kind="ExternalInput").ap()
    w2_d = nc.dram_tensor("w2", [FF, H], F32, kind="ExternalInput").ap()
    b2_d = nc.dram_tensor("b2", [H], F32, kind="ExternalInput").ap()
    c_d = nc.dram_tensor("c", [H], F32, kind="ExternalInput").ap()
    wmod_d = nc.dram_tensor("wmod", [H, 6 * H], F32, kind="ExternalInput").ap()
    out_d = nc.dram_tensor("out", [S, H], F32, kind="ExternalOutput").ap()
    ao_scr = nc.dram_tensor("ao_scr", [S, H], BF16, kind="Internal").ap()
    qT_scr = nc.dram_tensor("qT_scr", [HEADS, 128, S], BF16, kind="Internal").ap()
    kT_scr = nc.dram_tensor("kT_scr", [HEADS, 128, S], BF16, kind="Internal").ap()
    x1_scr = nc.dram_tensor("x1_scr", [S, H], F32, kind="Internal").ap()

    root = ExitStack()
    taps = {}

    def SB(stack, name, shape, dt):
        return stack.enter_context(nc.sbuf_tensor(name, shape, dt))

    def PS(stack, name, shape, dt=F32):
        return stack.enter_context(nc.psum_tensor(name, shape, dt))

    ones_row_f = SB(root, "ones_row_f", [1, 128], F32)
    p2_row_f = SB(root, "p2_row_f", [1, 128], F32)
    ones_col_f = SB(root, "ones_col_f", [128, 1], F32)
    ones_col_b = SB(root, "ones_col_b", [128, 1], BF16)
    ident_b = SB(root, "ident_b", [128, 128], BF16)
    ident_f = SB(root, "ident_f", [128, 128], F32)

    gac_row = SB(root, "gac_row", [1, H], F32)
    gbc_row = SB(root, "gbc_row", [1, H], F32)
    ca1T = SB(root, "ca1T", [128, KT], F32)
    cb1T = SB(root, "cb1T", [128, KT], F32)
    saT_b = SB(root, "saT_b", [128, KT], BF16)
    sbT_b = SB(root, "sbT_b", [128, KT], BF16)
    bqT = SB(root, "bqT", [128, KT], F32)
    bkT = SB(root, "bkT", [128, KT], F32)
    lnwT = SB(root, "lnwT", [128, KT], F32)
    lnbT_f = SB(root, "lnbT_f", [128, KT], F32)
    lnbT_b = SB(root, "lnbT_b", [128, KT], BF16)
    b2row = SB(root, "b2row", [1, H], F32)
    cT = SB(root, "cT", [128, KT], F32)
    silu_cT = SB(root, "silu_cT", [128, KT], BF16)
    lam_in = SB(root, "lam_in", [128, 4], F32)
    lamneg = SB(root, "lamneg", [128, 1], F32)
    lam11 = SB(root, "lam11", [1, 4], F32)
    Bneg = SB(root, "Bneg", [128, HEADS], F32)
    bnd = SB(root, "bnd", [1, 8], F32)
    bnd16 = SB(root, "bnd16", [1, 16], F32)
    biasqT = SB(root, "biasqT", [128, 4], F32)
    nrow = SB(root, "nrow", [1, 512], F32)
    eps_col = SB(root, "eps_col", [128, 1], F32)
    nhl_col = SB(root, "nhl_col", [128, 1], F32)
    maxcol = SB(root, "maxcol", [128, 2], F32)

    stats = {}
    for nm in ("mu1s", "ssq1", "mu1", "rstd1", "v1t", "m1t",
               "mu2s", "ssq2", "mu2", "rstd2", "v2t", "m2t",
               "amus", "assq", "amu", "arstd", "avt", "amt"):
        stats[nm] = SB(root, nm, [128, NT], F32)

    sems = {}
    for s_ in ("sx", "swg", "sao", "sout", "pe", "act", "dve", "pl",
               "sx0", "sx1", "sx2", "sx3", "wm0", "wm1", "swv", "swq",
               "sqk0", "sqk1", "sal0", "sal1", "sal2", "sal3",
               "sal4", "sal5", "sal6", "sal7", "swo",
               "sw1", "sw2", "sb1", "sas0", "sas1", "sx10", "sx11",
               "sd0", "sd1", "sd2", "sd3", "sf0", "sf1", "so0", "so1"):
        sems[s_] = root.enter_context(nc.semaphore(s_))

    p = Prog()
    last_dead = [0]

    def fence(dma=()):
        w = [("pe", p.n("pe")), ("act", p.n("act")), ("dve", p.n("dve")),
             ("pl", p.n("pl"))]
        for s_ in dma:
            w.append((s_, p.n(s_)))
        return w

    def tap(name, src_ap, waits):
        if not _DBG:
            return
        t = nc.dram_tensor("dbg_" + name, list(src_ap.shape), src_ap.dtype,
                           kind="ExternalOutput").ap()
        p.dma("sync", t[tuple(slice(None) for _ in src_ap.shape)], src_ap,
              waits=waits, inc="sout")

    pre = ExitStack()
    x_tiles = SB(pre, "x_tiles", [128, 4 * H], F32)
    h_bf = SB(pre, "h_bf", [128, 2 * H], BF16)
    deadout = SB(pre, "deadout", [128, 1024], F32)

    p.emit("gpsimd", lambda e: e.memset(ones_row_f[:, :], 1.0))
    p.emit("gpsimd", lambda e: e.memset(p2_row_f[:, :], 5.0))
    p.emit("gpsimd", lambda e: e.memset(ones_col_f[:, :], 1.0))
    p.emit("gpsimd", lambda e: e.memset(ones_col_b[:, :], 1.0))
    p.emit("gpsimd", lambda e: e.iota(
        deadout[:, 0:128], pattern=[[-1, 128]], base=0, channel_multiplier=1,
        allow_small_or_imprecise_dtypes=True), inc="pl")
    pl_iota = p.n("pl")
    p.emit("vector", lambda e: e.tensor_scalar(
        ident_b[:, :], deadout[:, 0:128], 0.0, None, ALU.is_equal),
        waits=[("pl", pl_iota)], inc="dve")
    p.emit("vector", lambda e: e.tensor_scalar(
        ident_f[:, :], deadout[:, 0:128], 0.0, None, ALU.is_equal),
        inc="dve")
    p.emit("vector", lambda e: e.reciprocal(p2_row_f[:, :], p2_row_f[:, :]),
           waits=[("pl", pl_iota)], inc="dve")
    d_ident = p.n("dve")
    p.emit("gpsimd", lambda e: e.memset(eps_col[:, :], EPS),
        waits=[("dve", d_ident)])
    p.emit("gpsimd", lambda e: e.memset(nhl_col[:, :], NEG_HALF_LN_HD), inc="pl")
    pl_const = p.n("pl")

    for kt in range(KT):
        p.dma("sync", bqT[:, kt:kt + 1], bq_d[kt * 128:(kt + 1) * 128][:, None], inc="sx")
        p.dma("sync", bkT[:, kt:kt + 1], bk_d[kt * 128:(kt + 1) * 128][:, None], inc="sx")
        p.dma("sync", lnwT[:, kt:kt + 1], lnw_d[kt * 128:(kt + 1) * 128][:, None], inc="sx")
        p.dma("sync", lnbT_f[:, kt:kt + 1], lnb_d[kt * 128:(kt + 1) * 128][:, None], inc="sx")
        p.dma("sync", cT[:, kt:kt + 1], c_d[kt * 128:(kt + 1) * 128][:, None], inc="sx")
    for i, d in enumerate([lq1_d, lk1_d, lq2_d, lk2_d]):
        p.dma("sync", lam_in[:, i:i + 1], d[:, None], inc="sx")
    p.dma("sync", b2row[0:1, :], b2_d[None, :], inc="sx")
    n_small = p.n("sx")

    p.emit("scalar", lambda e: e.activation(lnbT_b[:, :], lnbT_f[:, :], AF.Copy),
           waits=[("sx", n_small)], inc="act")

    mscope = ExitStack()
    wmod_sb = SB(mscope, "wmod_sb", [128, 2 * 2048], BF16)
    modrow = SB(mscope, "modrow", [1, 6 * H], F32)
    sgn_scr = SB(mscope, "sgn_scr", [1, H], F32)
    saT_f = SB(mscope, "saT_f", [128, KT], F32)
    sbT_f = SB(mscope, "sbT_f", [128, KT], F32)
    lamprod = SB(mscope, "lamprod", [128, 2], F32)
    mod_ps = PS(mscope, "mod_ps", [1, 1024])
    mod_ps2 = [mod_ps, PS(mscope, "mod_psB", [1, 1024])]
    modt_ps = PS(mscope, "modt_ps", [128, 32])
    p_bc = PS(mscope, "p_bc", [128, 512])

    p.emit("scalar", lambda e: e.activation(silu_cT[:, :], cT[:, :], AF.Silu),
           waits=[("sx", n_small)], inc="act")
    a_silu = p.n("act")

    mod_pe = {}
    mod_cp = 0
    for g in range(3):
        for kt in range(KT):
            slot = (g * KT + kt) % 2
            dep = [("pl", pl_const)]
            if (g * KT + kt) >= 2:
                dep.append(("pe", mod_pe[g * KT + kt - 2]))
            wsem = "wm%d" % slot
            p.dma("gpsimd", wmod_sb[:, slot * 2048:(slot + 1) * 2048],
                  wmod_d[kt * 128:(kt + 1) * 128, g * 2048:(g + 1) * 2048],
                  waits=dep, inc=wsem)
            wn = (wsem, p.n(wsem))
            for ch in range(4):
                p.emit("tensor", lambda e, slot=slot, kt=kt, ch=ch: e.matmul(
                    mod_ps[0:1, (ch % 2) * 512:(ch % 2 + 1) * 512]
                    if False else mod_ps2[ch // 2][0:1, (ch % 2) * 512:(ch % 2 + 1) * 512],
                    silu_cT[:, kt:kt + 1],
                    wmod_sb[:, slot * 2048 + ch * 512: slot * 2048 + (ch + 1) * 512],
                    start=(kt == 0), stop=(kt == KT - 1), skip_group_check=True),
                    waits=[wn, ("act", a_silu), ("act", mod_cp)],
                    inc="pe" if ch == 3 else None)
            mod_pe[g * KT + kt] = p.n("pe")
        pn = p.n("pe")
        for half in range(2):
            p.emit("scalar", lambda e, g=g, half=half: e.activation(
                modrow[0:1, g * 2048 + half * 1024:(g * 2048) + (half + 1) * 1024],
                mod_ps2[half][0:1, :], AF.Copy),
                waits=[("pe", pn)], inc="act" if half == 1 else None)
        mod_cp = p.n("act")
    a_modrow = p.n("act")

    for row, off in ((gac_row, 2 * H), (gbc_row, 5 * H)):
        p.emit("vector", lambda e, r=row, o=off: e.tensor_scalar(
            r[0:1, :], modrow[0:1, o:o + H], 0.0, 2.0, ALU.is_ge, ALU.mult),
            waits=[("act", a_modrow)], inc="dve")
        c1 = p.n("dve")
        p.emit("vector", lambda e, r=row: e.tensor_scalar_add(r[0:1, :], r[0:1, :], -1.0),
               waits=[("dve", c1)], inc="dve")
        c2 = p.n("dve")
        p.emit("vector", lambda e, o=off: e.tensor_scalar_mul(
            sgn_scr[0:1, :], modrow[0:1, o:o + H], -1.0), inc="dve")
        c3 = p.n("dve")
        p.emit("vector", lambda e, o=off: e.tensor_max(
            sgn_scr[0:1, :], sgn_scr[0:1, :], modrow[0:1, o:o + H]),
            waits=[("dve", c3)], inc="dve")
        c4 = p.n("dve")
        p.emit("vector", lambda e: e.tensor_scalar_max(
            sgn_scr[0:1, :], sgn_scr[0:1, :], 0.25),
            waits=[("dve", c4)], inc="dve")
        c5 = p.n("dve")
        p.emit("vector", lambda e, r=row: e.tensor_mul(r[0:1, :], r[0:1, :], sgn_scr[0:1, :]),
               waits=[("dve", c2), ("dve", c5)], inc="dve")
    d_clamp = p.n("dve")

    for i, off in enumerate((0, H, 3 * H, 4 * H)):
        for kt in range(KT):
            p.emit("tensor", lambda e, i=i, off=off, kt=kt: e.matmul(
                modt_ps[:, i * KT + kt: i * KT + kt + 1],
                modrow[0:1, off + kt * 128: off + (kt + 1) * 128],
                ones_row_f[0:1, 0:1],
                start=True, stop=True, skip_group_check=True),
                waits=[("act", a_modrow), ("pl", pl_const)],
                inc="pe" if (i == 3 and kt == KT - 1) else None)
    pn = p.n("pe")
    p.emit("scalar", lambda e: e.activation(saT_f[:, :], modt_ps[:, 0:KT], AF.Copy),
           waits=[("pe", pn)])
    p.emit("scalar", lambda e: e.activation(ca1T[:, :], modt_ps[:, KT:2 * KT], AF.Copy))
    p.emit("scalar", lambda e: e.activation(sbT_f[:, :], modt_ps[:, 2 * KT:3 * KT], AF.Copy))
    p.emit("scalar", lambda e: e.activation(cb1T[:, :], modt_ps[:, 3 * KT:4 * KT], AF.Copy),
           inc="act")
    a_mcp = p.n("act")
    p.emit("scalar", lambda e: e.activation(saT_b[:, :], saT_f[:, :], AF.Copy),
           waits=[("act", a_mcp)])
    p.emit("scalar", lambda e: e.activation(sbT_b[:, :], sbT_f[:, :], AF.Copy), inc="act")
    a_modT = p.n("act")
    p.emit("vector", lambda e: e.tensor_scalar_add(ca1T[:, :], ca1T[:, :], 1.0),
           waits=[("act", a_modT)])
    p.emit("vector", lambda e: e.tensor_scalar_add(cb1T[:, :], cb1T[:, :], 1.0), inc="dve")
    d_ca1 = p.n("dve")

    p.emit("vector", lambda e: e.tensor_mul(lamprod[:, 0:1], lam_in[:, 0:1], lam_in[:, 1:2]),
           waits=[("sx", n_small)])
    p.emit("vector", lambda e: e.tensor_mul(lamprod[:, 1:2], lam_in[:, 2:3], lam_in[:, 3:4]),
           inc="dve")
    dn = p.n("dve")
    p.emit("tensor", lambda e: e.matmul(p_bc[0:1, 0:1], lamprod[:, 0:1], ones_col_f[:, 0:1],
                                        start=True, stop=True, skip_group_check=True),
           waits=[("dve", dn)])
    p.emit("tensor", lambda e: e.matmul(p_bc[0:1, 1:2], lamprod[:, 1:2], ones_col_f[:, 0:1],
                                        start=False, stop=True, skip_group_check=True),
           inc="pe")
    pn = p.n("pe")
    p.emit("scalar", lambda e: e.activation(lam11[0:1, 0:2], p_bc[0:1, 0:2], AF.Exp),
           waits=[("pe", pn)], inc="act")
    an = p.n("act")
    p.emit("vector", lambda e: e.tensor_sub(lam11[0:1, 2:3], lam11[0:1, 0:1], lam11[0:1, 1:2]),
           waits=[("act", an)], inc="dve")
    lsn = p.n("dve")
    p.emit("vector", lambda e: e.tensor_scalar(lam11[0:1, 3:4], lam11[0:1, 2:3],
                                               LAMBDA_INIT, -1.0, ALU.add, ALU.mult),
           waits=[("dve", lsn)], inc="dve")
    dn = p.n("dve")
    p.emit("tensor", lambda e: e.matmul(p_bc[:, 4:5], ones_row_f[0:1, :], lam11[0:1, 3:4],
                                        start=True, stop=True, skip_group_check=True),
           waits=[("dve", dn)], inc="pe")
    pn = p.n("pe")
    p.emit("scalar", lambda e: e.activation(lamneg[:, 0:1], p_bc[:, 4:5], AF.Copy),
           waits=[("pe", pn)], inc="act")
    a_lam = p.n("act")
    tap("modrow", modrow[0:1, :], [("act", a_modrow)])
    tap("gac_row", gac_row[0:1, :], [("dve", d_clamp)])
    tap("gbc_row", gbc_row[0:1, :], [("dve", d_clamp)])
    tap("lam11", lam11[0:1, :], [("dve", p.n("dve")), ("act", a_lam)])
    tap("lamneg", lamneg[:, :], [("act", a_lam)])
    tap("ca1T", ca1T[:, :], [("dve", d_ca1)])
    tap("saTb", saT_b[:, :], [("act", a_modT)])
    mscope.close()

    tsc = ExitStack()
    tT = SB(tsc, "tT", [128, KT * S], BF16)
    asc = ExitStack()
    v_aug = SB(asc, "v_aug", [128, NT * 516], BF16)
    biasq8 = SB(asc, "biasq8", [128, KT], F32)
    biask8 = SB(asc, "biask8", [128, KT], F32)

    cmb = ExitStack()
    Wv_sb = SB(cmb, "Wv_sb", [128, KT * H], BF16)
    Wqk_sb = SB(cmb, "Wqk_sb", [128, KT * H], BF16)
    bvrow = SB(cmb, "bvrow", [1, H], F32)
    bvprow = SB(cmb, "bvprow", [1, H], F32)
    beffrow = SB(cmb, "beffrow", [1, 512], F32)
    bveff_c = SB(cmb, "bveff_c", [128, 512], F32)
    vA_scr = SB(cmb, "vA_scr", [128, 2 * 512], BF16)
    nrow2 = SB(cmb, "nrow2", [1, 1024], F32)
    qstg = SB(cmb, "qstg", [128, 4 * 512], BF16)
    p_tr = [PS(cmb, "ptr0", [128, 1024], BF16), PS(cmb, "ptr1", [128, 1024], BF16)]
    p_v = [PS(cmb, "pv%d" % i, [128, 512]) for i in range(4)]
    p_dn = [PS(cmb, "pdn0", [128, 512]), PS(cmb, "pdn1", [128, 512])]

    vfence = fence()
    p.dma("sync", bvrow[0:1, :], bv_d[None, :], waits=vfence, inc="sx")
    nbv = p.n("sx")
    p.dma("gpsimd", Wv_sb[:, :].rearrange("p (k n) -> p k n", k=KT),
          Wv_d[:, :].rearrange("(k p) n -> p k n", p=128), waits=vfence, inc="swv")
    wvn = ("swv", p.n("swv"))
    p.dma("gpsimd", Wqk_sb[:, :].rearrange("p (k n) -> p k n", k=KT),
          Wq_d[:, :].rearrange("(k p) n -> p k n", p=128), waits=vfence, inc="swq")
    wqkn = ("swq", p.n("swq"))
    p.emit("gpsimd", lambda e: e.memset(
        v_aug[:, :].rearrange("p (t j c) -> p (t j) c", t=NT, j=4)[:, :, 128:129], 1.0),
        waits=vfence, inc="pl")
    pl_vones = p.n("pl")

    for ch in range(2):
        for kt in range(KT):
            p.emit("tensor", lambda e, kt=kt, ch=ch: e.matmul(
                p_v[0][0:1, 0:512], saT_b[:, kt:kt + 1],
                Wv_sb[:, kt * H + ch * 512: kt * H + (ch + 1) * 512],
                start=(kt == 0), stop=(kt == KT - 1), skip_group_check=True),
                waits=[wvn, ("act", a_modT), ("dve", p.n("dve"))]
                if kt == 0 else None,
                inc="pe" if kt == KT - 1 else None)
        pn = p.n("pe")
        p.emit("vector", lambda e, ch=ch: e.tensor_copy(
            nrow2[0:1, ch * 512:(ch + 1) * 512], p_v[0][0:1, 0:512]),
            waits=[("pe", pn)], inc="dve")
    dn = p.n("dve")
    p.emit("vector", lambda e: e.tensor_add(bvprow[0:1, :], nrow2[0:1, :], bvrow[0:1, :]),
           waits=[("dve", dn), ("sx", nbv)], inc="dve")
    dn = p.n("dve")
    for j in range(4):
        p.emit("vector", lambda e, j=j: e.scalar_tensor_tensor(
            beffrow[0:1, j * 128:(j + 1) * 128],
            bvprow[0:1, (4 + j) * 128:(5 + j) * 128], lamneg[0:1, 0:1],
            bvprow[0:1, j * 128:(j + 1) * 128], ALU.mult, ALU.add),
            waits=[("dve", dn), ("act", a_lam)], inc="dve")
    dn = p.n("dve")
    p.emit("tensor", lambda e: e.matmul(p_v[1][:, :], ones_row_f[0:1, :], beffrow[0:1, :],
                                        start=True, stop=True, skip_group_check=True),
           waits=[("dve", dn), ("pe", p.n("pe"))], inc="pe")
    pn = p.n("pe")
    p.emit("scalar", lambda e: e.activation(bveff_c[:, :], p_v[1][:, :], AF.Copy),
           waits=[("pe", pn)], inc="act")
    a_bveff = p.n("act")
    for kt in range(KT):
        p.emit("vector", lambda e, kt=kt: e.tensor_scalar_mul(
            Wv_sb[:, kt * H:(kt + 1) * H], Wv_sb[:, kt * H:(kt + 1) * H],
            ca1T[:, kt:kt + 1]),
            waits=[("pe", pn), ("dve", d_ca1)] if kt == 0 else None,
            inc="dve" if kt == KT - 1 else None)
    d_wv = p.n("dve")
    for ch in range(2):
        for kt in range(KT):
            p.emit("tensor", lambda e, kt=kt, ch=ch: e.matmul(
                p_dn[0][0:1, 0:512], saT_b[:, kt:kt + 1],
                Wqk_sb[:, kt * H + ch * 512: kt * H + (ch + 1) * 512],
                start=(kt == 0), stop=(kt == KT - 1), skip_group_check=True),
                waits=[wqkn, ("act", a_modT), ("dve", p.n("dve"))]
                if kt == 0 else None,
                inc="pe" if kt == KT - 1 else None)
        pn = p.n("pe")
        p.emit("vector", lambda e, ch=ch: e.tensor_copy(
            nrow2[0:1, ch * 512:(ch + 1) * 512], p_dn[0][0:1, 0:512]),
            waits=[("pe", pn)], inc="dve")
    dn = p.n("dve")
    for dt_ in range(KT):
        p.emit("tensor", lambda e, dt_=dt_: e.matmul(
            p_dn[1][:, dt_:dt_ + 1], nrow2[0:1, dt_ * 128:(dt_ + 1) * 128],
            ones_row_f[0:1, 0:1], start=True, stop=True, skip_group_check=True),
            waits=[("dve", dn)], inc="pe" if dt_ == KT - 1 else None)
    pn = p.n("pe")
    p.emit("vector", lambda e: e.tensor_add(biasq8[:, :], p_dn[1][:, 0:KT], bqT[:, :]),
           waits=[("pe", pn)], inc="dve")
    d_bq8 = p.n("dve")
    for kt in range(KT):
        p.emit("vector", lambda e, kt=kt: e.tensor_scalar_mul(
            Wqk_sb[:, kt * H:(kt + 1) * H], Wqk_sb[:, kt * H:(kt + 1) * H],
            ca1T[:, kt:kt + 1]),
            waits=[("pe", pn), ("dve", d_ca1)] if kt == 0 else None,
            inc="dve" if kt == KT - 1 else None)
    d_wqk = p.n("dve")

    ln1_dve, ln1_act, tr_pe, tr_act = {}, {}, {}, {}
    l1_ms, l1_mu, l1_rs, l1_xn = {}, {}, {}, {}
    v_comb = {}
    pd_dma = {}
    dn_evac = {}
    nchunk = 0

    def emit_l1_s1(st):
        xv = x_tiles[:, (st % 4) * H:(st % 4 + 1) * H]
        dep = []
        if st >= 4:
            dep += [("dve", ln1_dve.get(st - 4, 0)), ("act", ln1_act.get(st - 4, 0))]
        xn = p.dma("sync", xv, x_d[st * 128:(st + 1) * 128, :], waits=dep,
                   inc="sx%d" % (st % 4))
        p.emit("scalar", lambda e, st=st, xv=xv: e.activation(
            deadout[:, :], xv, AF.Copy, accum_out=stats["mu1s"][:, st:st + 1]),
            waits=[xn, ("act", last_dead[0])] + ([("dve", d_ident)] if st == 0 else []),
            inc="act")
        l1_ms[st] = p.n("act")
        last_dead[0] = l1_ms[st]
        p.emit("scalar", lambda e, st=st, xv=xv: e.activation(
            deadout[:, :], xv, AF.Square, accum_out=stats["ssq1"][:, st:st + 1]),
            waits=[("act", last_dead[0])], inc="act")
        ln1_act[st] = p.n("act")
        last_dead[0] = ln1_act[st]
        l1_xn[st] = xn

    def emit_l1_s2(st):
        p.emit("vector", lambda e, st=st: e.tensor_mul(
            stats["m1t"][:, st:st + 1], stats["mu1s"][:, st:st + 1],
            stats["mu1s"][:, st:st + 1]), waits=[("act", l1_ms[st])], inc="dve")
        t1n = p.n("dve")
        p.emit("vector", lambda e, st=st: e.scalar_tensor_tensor(
            stats["v1t"][:, st:st + 1], stats["m1t"][:, st:st + 1], 1.0 / H,
            stats["ssq1"][:, st:st + 1], ALU.mult, ALU.subtract),
            waits=[("act", ln1_act[st]), ("dve", t1n)], inc="dve")
        dn = p.n("dve")
        p.emit("vector", lambda e, st=st: e.tensor_scalar_mul(
            stats["mu1"][:, st:st + 1], stats["mu1s"][:, st:st + 1], 1.0 / H),
            inc="dve")
        l1_mu[st] = p.n("dve")
        p.emit("scalar", lambda e, st=st: e.activation(
            stats["v1t"][:, st:st + 1], stats["v1t"][:, st:st + 1], AF.Ln,
            scale=-1.0 / H, bias=eps_col[:, 0:1]),
            waits=[("dve", dn), ("pl", pl_const)], inc="act")
        lnn = p.n("act")
        p.emit("scalar", lambda e, st=st: e.activation(
            stats["rstd1"][:, st:st + 1], stats["v1t"][:, st:st + 1], AF.Exp, scale=-0.5),
            waits=[("act", lnn)], inc="act")
        l1_rs[st] = p.n("act")

    def emit_l1_s3(st):
        pp = st % 2
        xv = x_tiles[:, (st % 4) * H:(st % 4 + 1) * H]
        hv = h_bf[:, pp * H:(pp + 1) * H]
        hdep = [("act", l1_rs[st]), ("dve", l1_mu[st])]
        if st >= 2:
            hdep.append(("pe", tr_pe[st - 2]))
        p.emit("vector", lambda e, xv=xv, hv=hv, st=st: e.tensor_scalar(
            hv, xv, stats["mu1"][:, st:st + 1], stats["rstd1"][:, st:st + 1],
            ALU.subtract, ALU.mult), waits=hdep, inc="dve")
        ln1_dve[st] = p.n("dve")
        trdep = [("dve", ln1_dve[st])]
        if st >= 2:
            trdep.append(tr_act[st - 2])
        for kt in range(KT):
            p.emit("tensor", lambda e, kt=kt, pp=pp, hv=hv: e.transpose(
                p_tr[pp][:, kt * 128:(kt + 1) * 128],
                hv.rearrange("p (k c) -> p k c", k=KT)[:, kt, :], ident_b[:, :]),
                waits=trdep if kt == 0 else None,
                inc="pe" if kt == KT - 1 else None)
        tr_pe[st] = p.n("pe")
        p.emit("vector", lambda e, st=st, pp=pp: e.tensor_copy(
            tT[:, :].rearrange("p (k s) -> p k s", k=KT)[:, :, st * 128:(st + 1) * 128],
            p_tr[pp][:, :].rearrange("p (k s) -> p k s", k=KT)),
            waits=[("pe", tr_pe[st])], inc="dve")
        tr_act[st] = ("dve", p.n("dve"))

    def emit_v(st):
        pp = st % 2
        psA, psB = p_v[2 * pp], p_v[2 * pp + 1]
        dep = [tr_act[st], ("dve", d_wv), ("act", a_bveff)]
        if st >= 2:
            dep.append(("dve", v_comb[st - 2]))
        for kt in range(KT):
            hT_t = tT[:, kt * S + st * 128: kt * S + (st + 1) * 128]
            p.emit("tensor", lambda e, psA=psA, hT_t=hT_t, kt=kt: e.matmul(
                psA[:, :], hT_t, Wv_sb[:, kt * H: kt * H + 512],
                start=(kt == 0), stop=(kt == KT - 1), skip_group_check=True),
                waits=dep if kt == 0 else None)
            p.emit("tensor", lambda e, psB=psB, hT_t=hT_t, kt=kt: e.matmul(
                psB[:, :], hT_t, Wv_sb[:, kt * H + 512: (kt + 1) * H],
                start=(kt == 0), stop=(kt == KT - 1), skip_group_check=True),
                inc="pe" if kt == KT - 1 else None)
        pn2 = p.n("pe")
        p.emit("scalar", lambda e, psA=psA, pp=pp: e.activation(
            vA_scr[:, pp * 512:(pp + 1) * 512], psA[:, :], AF.Copy),
            waits=[("pe", pn2)], inc="act")
        acn = p.n("act")
        p.emit("vector", lambda e, psB=psB, pp=pp: e.scalar_tensor_tensor(
            vA_scr[:, pp * 512:(pp + 1) * 512], psB[:, :], lamneg[:, 0:1],
            vA_scr[:, pp * 512:(pp + 1) * 512], ALU.mult, ALU.add),
            waits=[("act", acn), ("act", a_bveff), ("pl", pl_vones)], inc="dve")
        sttn = p.n("dve")
        p.emit("vector", lambda e, st=st, pp=pp: e.tensor_tensor(
            v_aug[:, st * 516:(st + 1) * 516].rearrange("p (j c) -> p j c", j=4)[:, :, 0:128],
            vA_scr[:, pp * 512:(pp + 1) * 512].rearrange("p (j c) -> p j c", j=4),
            bveff_c[:, :].rearrange("p (j c) -> p j c", j=4), ALU.add),
            waits=[("dve", sttn)], inc="dve")
        v_comb[st] = p.n("dve")

    def emit_dense_chunk(ti, dt_, qi, wbuf, scr, bcol, dscale, ev_eng):
        nonlocal_ns = emit_dense_chunk
        n = nonlocal_ns.nchunk
        bank = n % 2
        slot = n % 4
        mmw = [("dve", dscale), tr_act[min(4 * qi + 3, NT - 1)]]
        if n >= 2:
            mmw.append(dn_evac[n - 2])
        for kt in range(KT):
            p.emit("tensor", lambda e, bank=bank, wbuf=wbuf, kt=kt, dt_=dt_, qi=qi: e.matmul(
                p_dn[bank][:, :], wbuf[:, kt * H + dt_ * 128: kt * H + (dt_ + 1) * 128],
                tT[:, kt * S + qi * 512: kt * S + (qi + 1) * 512],
                start=(kt == 0), stop=(kt == KT - 1), skip_group_check=True),
                waits=mmw if kt == 0 else None,
                inc="pe" if kt == KT - 1 else None)
        pn = p.n("pe")
        sdep = [pd_dma[nonlocal_ns.slot_dma[slot]]] if slot in nonlocal_ns.slot_dma else []
        if ev_eng == "scalar":
            p.emit("scalar", lambda e, bank=bank, slot=slot, bcol=bcol, dt_=dt_: e.activation(
                qstg[:, slot * 512:(slot + 1) * 512], p_dn[bank][:, :], AF.Identity,
                bias=bcol[:, dt_:dt_ + 1]),
                waits=[("pe", pn)] + sdep, inc="act")
            evd = ("act", p.n("act"))
        else:
            p.emit("vector", lambda e, bank=bank, slot=slot, bcol=bcol, dt_=dt_: e.tensor_scalar_add(
                qstg[:, slot * 512:(slot + 1) * 512], p_dn[bank][:, :], bcol[:, dt_:dt_ + 1]),
                waits=[("pe", pn)] + sdep, inc="dve")
            evd = ("dve", p.n("dve"))
        dn_evac[n] = evd
        pd_dma[(ti, dt_, qi)] = p.dma(
            "gpsimd", scr[dt_, :, qi * 512:(qi + 1) * 512],
            qstg[:, slot * 512:(slot + 1) * 512], waits=[evd], inc="sd%d" % slot)
        nonlocal_ns.slot_dma[slot] = (ti, dt_, qi)
        nonlocal_ns.nchunk += 1

    emit_dense_chunk.nchunk = 0
    emit_dense_chunk.slot_dma = {}

    for st in range(NT + 2):
        if st < NT:
            emit_l1_s1(st)
        if 1 <= st <= NT:
            emit_l1_s2(st - 1)
        if 2 <= st <= NT + 1:
            emit_l1_s3(st - 2)
        if st >= 4:
            emit_v(st - 4)
        if st >= 7 and ((st - 7) % 4 == 0) and (st - 7) // 4 < 7:
            qi = (st - 7) // 4
            for dt_ in range(KT):
                emit_dense_chunk(0, dt_, qi, Wqk_sb, qT_scr, biasq8, d_wqk, "scalar")
    for st in range(NT - 2, NT):
        emit_v(st)
    for dt_ in range(KT):
        emit_dense_chunk(0, dt_, 7, Wqk_sb, qT_scr, biasq8, d_wqk, "scalar")

    kdep = [("pe", p.n("pe"))]
    p.dma("gpsimd", Wqk_sb[:, :].rearrange("p (k n) -> p k n", k=KT),
          Wk_d[:, :].rearrange("(k p) n -> p k n", p=128), waits=kdep, inc="swg")
    wqkn = p.n("swg")
    for ch in range(2):
        for kt in range(KT):
            p.emit("tensor", lambda e, kt=kt, ch=ch: e.matmul(
                p_dn[0][0:1, 0:512], saT_b[:, kt:kt + 1],
                Wqk_sb[:, kt * H + ch * 512: kt * H + (ch + 1) * 512],
                start=(kt == 0), stop=(kt == KT - 1), skip_group_check=True),
                waits=[("swg", wqkn), ("dve", p.n("dve")),
                       dn_evac.get(emit_dense_chunk.nchunk - 2, ("pl", 0)),
                       dn_evac.get(emit_dense_chunk.nchunk - 1, ("pl", 0))]
                if kt == 0 else None,
                inc="pe" if kt == KT - 1 else None)
        pn = p.n("pe")
        p.emit("vector", lambda e, ch=ch: e.tensor_copy(
            nrow2[0:1, ch * 512:(ch + 1) * 512], p_dn[0][0:1, 0:512]),
            waits=[("pe", pn)], inc="dve")
    dn = p.n("dve")
    for dt_ in range(KT):
        p.emit("tensor", lambda e, dt_=dt_: e.matmul(
            p_dn[1][:, dt_:dt_ + 1], nrow2[0:1, dt_ * 128:(dt_ + 1) * 128],
            ones_row_f[0:1, 0:1], start=True, stop=True, skip_group_check=True),
            waits=[("dve", dn)], inc="pe" if dt_ == KT - 1 else None)
    pn = p.n("pe")
    p.emit("vector", lambda e: e.tensor_add(biask8[:, :], p_dn[1][:, 0:KT], bkT[:, :]),
           waits=[("pe", pn)], inc="dve")
    for kt in range(KT):
        p.emit("vector", lambda e, kt=kt: e.tensor_scalar_mul(
            Wqk_sb[:, kt * H:(kt + 1) * H], Wqk_sb[:, kt * H:(kt + 1) * H],
            ca1T[:, kt:kt + 1]),
            waits=[("pe", pn), ("dve", d_ca1)] if kt == 0 else None,
            inc="dve" if kt == KT - 1 else None)
    d_wk2 = p.n("dve")
    for dt_ in range(KT):
        for qi in range(KT):
            emit_dense_chunk(1, dt_, qi, Wqk_sb, kT_scr, biask8, d_wk2, "vector")
    n_pd_all = [("sd%d" % s_, p.n("sd%d" % s_)) for s_ in range(4)]
    n_pd = {}
    for hh in range(HEADS):
        n_pd[(0, hh)] = [pd_dma[(0, hh, qi)] for qi in range(KT)]
        n_pd[(1, hh)] = [pd_dma[(1, hh, qi)] for qi in range(KT)]
    cmb.close()

    qT2 = SB(asc, "qT2", [128, 2 * S], BF16)
    kT2 = SB(asc, "kT2", [128, 2 * S], BF16)
    sq_scr = SB(asc, "sq_scr", [128, 2 * 1024], BF16)
    expT = SB(asc, "expT", [128, 3 * QCW], BF16)
    ao_ch = SB(asc, "ao_ch", [128, 2 * QCW], BF16)
    rec8 = SB(asc, "rec8", [128, 8], F32)
    qk_ps = [PS(asc, "qk0", [128, 1024]), PS(asc, "qk1", [128, 1024])]
    av_ps = [PS(asc, "avA", [128, 387]), PS(asc, "avB", [128, 387]),
             PS(asc, "avC", [128, 258])]
    p_pr = PS(asc, "p_pr", [128, 512])

    AVJ = [(0, 0), (0, 129), (0, 258), (1, 0), (1, 129), (1, 258), (2, 0), (2, 129)]

    gidx = 0
    exp_act = {}
    evac_dve = {}
    evac_banks = {}
    aoch_dma = {}
    slot_proj_pe = {0: 0, 1: 0}
    av_bank_pe = {}
    proj_dve = {}
    bnd_act = {}

    def prep_units(h):
        sl = h % 2
        qT = qT2[:, sl * S:(sl + 1) * S]
        kTs = kT2[:, sl * S:(sl + 1) * S]
        wdep = ([("pe", slot_proj_pe[sl])] if h > 1
                else fence(dma=("sd0", "sd1", "sd2", "sd3")))
        qsem = "sqk%d" % sl
        p.dma("gpsimd", qT, qT_scr[h, :, :],
              waits=wdep + n_pd[(0, h)], inc=qsem)
        p.dma("gpsimd", kTs, kT_scr[h, :, :],
              waits=n_pd[(1, h)], inc=qsem)
        proj_dve[h] = (qsem, p.n(qsem))
        yield
        rowcp = 0
        rmaxn = [0]
        for t, src_ in enumerate((qT, kTs)):
            for c4 in range(4):
                sq_half = (t * 4 + c4) % 2
                sqdep = [proj_dve[h]]
                if (t * 4 + c4) >= 2:
                    sqdep.append(("pe", p.n("pe")))
                p.emit("vector", lambda e, src_=src_, c4=c4, sq_half=sq_half: e.tensor_mul(
                    sq_scr[:, sq_half * 1024:(sq_half + 1) * 1024],
                    src_[:, c4 * 1024:(c4 + 1) * 1024],
                    src_[:, c4 * 1024:(c4 + 1) * 1024]), waits=sqdep, inc="dve")
                sqn = ("dve", p.n("dve"))
                yield
                yield
                for c2 in range(2):
                    cid = c4 * 2 + c2
                    p.emit("tensor", lambda e, c2=c2, sq_half=sq_half: e.matmul(
                        p_pr[0:1, 0:512], ones_col_b[:, 0:1],
                        sq_scr[:, sq_half * 1024 + c2 * 512: sq_half * 1024 + (c2 + 1) * 512],
                        start=True, stop=True, skip_group_check=True),
                        waits=[sqn, ("dve", rowcp)], inc="pe")
                    pn = p.n("pe")
                    p.emit("vector", lambda e: e.tensor_copy(
                        nrow[0:1, 0:512], p_pr[0:1, 0:512]),
                        waits=[("pe", pn), ("dve", rmaxn[0])], inc="dve")
                    an = p.n("dve")
                    rowcp = an
                    p.emit("vector", lambda e, t=t, cid=cid: e.reduce_max(
                        bnd16[0:1, t * 8 + cid: t * 8 + cid + 1], nrow[0:1, 0:512],
                        axis=AXX), waits=[("dve", an)], inc="dve")
                    rmaxn[0] = p.n("dve")
                    yield
            p.emit("vector", lambda e, t=t: e.reduce_max(
                bnd[0:1, t:t + 1], bnd16[0:1, t * 8:(t + 1) * 8], axis=AXX),
                waits=[("dve", p.n("dve"))], inc="dve")
        dn = p.n("dve")
        p.emit("vector", lambda e: e.tensor_mul(bnd[0:1, 2:3], bnd[0:1, 0:1], bnd[0:1, 1:2]),
               waits=[("dve", dn)], inc="dve")
        dn = p.n("dve")
        p.emit("scalar", lambda e: e.activation(bnd[0:1, 3:4], bnd[0:1, 2:3], AF.Ln),
               waits=[("dve", dn)], inc="act")
        lnn = p.n("act")
        p.emit("scalar", lambda e: e.activation(bnd[0:1, 4:5], bnd[0:1, 3:4], AF.Exp,
                                                scale=0.5, bias=nhl_col[0:1, 0:1]),
               waits=[("act", lnn)], inc="act")
        an = p.n("act")
        yield
        p.emit("tensor", lambda e: e.matmul(
            p_pr[:, 400:401], ones_row_f[0:1, :], bnd[0:1, 4:5],
            start=True, stop=True, skip_group_check=True),
            waits=[("act", an), ("dve", p.n("dve"))], inc="pe")
        pn = p.n("pe")
        yield
        p.emit("scalar", lambda e, h=h: e.activation(
            Bneg[:, h:h + 1], p_pr[:, 400:401], AF.Copy, scale=-1.0),
            waits=[("pe", pn)], inc="act")
        bnd_act[h] = p.n("act")
        yield

    for _ in prep_units(0):
        pass

    for h in range(HEADS):
        sl = h % 2
        qT = qT2[:, sl * S:(sl + 1) * S]
        kTs = kT2[:, sl * S:(sl + 1) * S]
        nxt = prep_units(h + 1) if h + 1 < HEADS else iter(())

        def emit_qk(kt, g, qc):
            pp = g % 2
            qkdep = [("act", bnd_act[h]), proj_dve[h], ("dve", v_comb[NT - 1])]
            if g >= 2:
                qkdep.append(("act", exp_act[g - 2]))
            for half in range(2):
                p.emit("tensor", lambda e, pp=pp, kt=kt, qc=qc, half=half, qT=qT, kTs=kTs: e.matmul(
                    qk_ps[pp][:, half * 512:(half + 1) * 512],
                    kTs[:, kt * 128:(kt + 1) * 128],
                    qT[:, qc * QCW + half * 512: qc * QCW + (half + 1) * 512],
                    start=True, stop=True, skip_group_check=True),
                    waits=qkdep if half == 0 else None,
                    inc="pe" if half == 1 else None)
            qkn = p.n("pe")
            ppe = g % 3
            p.emit("scalar", lambda e, pp=pp, ppe=ppe, h=h: e.activation(
                expT[:, ppe * QCW:(ppe + 1) * QCW], qk_ps[pp][:, :], AF.Exp,
                scale=ISQ, bias=Bneg[:, h:h + 1]),
                waits=[("pe", qkn), ("act", bnd_act[h])], inc="act")
            exp_act[g] = p.n("act")

        def emit_av(kt, g, qc):
            ppe = g % 3
            avdep = [("act", exp_act[g])]
            m_prev = (h * NQC + qc) - 1
            for j in range(8):
                ti, off = AVJ[j]
                fib = (kt == 0) and (off == 0)
                w_ = None
                if j == 0:
                    w_ = list(avdep)
                if kt == 0 and fib and g >= NT:
                    w_ = (w_ or []) + [("dve", evac_banks[m_prev][ti])]
                p.emit("tensor", lambda e, ti=ti, off=off, j=j, kt=kt, ppe=ppe, h=h, fib=fib: e.matmul(
                    av_ps[ti][:, off:off + 129],
                    expT[:, ppe * QCW + j * 128: ppe * QCW + (j + 1) * 128],
                    v_aug[:, kt * 516 + (h % 4) * 129: kt * 516 + (h % 4) * 129 + 129],
                    start=fib, stop=(kt == NT - 1), skip_group_check=True),
                    waits=w_,
                    inc="pe" if (kt == NT - 1 and j in (2, 5, 7)) else None)
                if kt == NT - 1 and j in (2, 5, 7):
                    av_bank_pe[ti] = p.n("pe")

        for qc in range(NQC):
            g0 = gidx
            for kt in range(NT):
                emit_qk(kt, g0 + kt, qc)
                if kt >= 2:
                    emit_av(kt - 2, g0 + kt - 2, qc)
                gk = qc * NT + kt
                if gk == 0 or gk >= 16:
                    next(nxt, None)
            emit_av(NT - 2, g0 + NT - 2, qc)
            emit_av(NT - 1, g0 + NT - 1, qc)
            gidx = g0 + NT
            m = h * NQC + qc
            ppq = qc % 2
            evdep = []
            if m >= 2:
                evdep.append(aoch_dma[m - 2])
            p.emit("vector", lambda e: e.reciprocal(
                rec8[:, 0:3], av_ps[0][:, :].rearrange("p (j c) -> p j c", j=3)[:, :, 128]),
                waits=evdep + [("pe", av_bank_pe[0])])
            p.emit("vector", lambda e: e.reciprocal(
                rec8[:, 3:6], av_ps[1][:, :].rearrange("p (j c) -> p j c", j=3)[:, :, 128]),
                waits=[("pe", av_bank_pe[1])])
            p.emit("vector", lambda e: e.reciprocal(
                rec8[:, 6:8], av_ps[2][:, :].rearrange("p (j c) -> p j c", j=2)[:, :, 128]),
                waits=[("pe", av_bank_pe[2])], inc="dve")
            rcn = p.n("dve")
            bank_free = {}
            for j in range(8):
                ti, off = AVJ[j]
                p.emit("vector", lambda e, ti=ti, off=off, j=j, ppq=ppq: e.tensor_scalar_mul(
                    ao_ch[:, ppq * QCW + j * 128: ppq * QCW + (j + 1) * 128],
                    av_ps[ti][:, off:off + 128], rec8[:, j:j + 1]),
                    waits=[("dve", rcn)] if j == 0 else None,
                    inc="dve" if j in (2, 5, 7) else None)
                if j in (2, 5, 7):
                    bank_free[ti] = p.n("dve")
            evac_dve[m] = p.n("dve")
            evac_banks[m] = bank_free
            aoch_dma[m] = p.dma(
                "sync",
                ao_scr[qc * QCW:(qc + 1) * QCW, h * 128:(h + 1) * 128]
                .rearrange("(j p) d -> p j d", p=128),
                ao_ch[:, ppq * QCW:(ppq + 1) * QCW].rearrange("p (j d) -> p j d", j=8),
                waits=[("dve", evac_dve[m])], inc="sas%d" % (m % 2))
        for _ in nxt:
            pass
        slot_proj_pe[sl] = p.n("pe")
    a_attn_pe_end = p.n("pe")
    n_ao_stores = [("sas0", p.n("sas0")), ("sas1", p.n("sas1"))]

    asc.close()
    wsc = ExitStack()
    Wo_sb = SB(wsc, "Wo_sb", [128, KT * H], BF16)
    borow = SB(wsc, "borow", [1, H], F32)
    aofence = fence(dma=("sqk0", "sqk1", "sas0", "sas1"))
    nbo = p.dma("sync", borow[0:1, :], bo_d[None, :], waits=aofence, inc="sb1")
    won = p.dma("gpsimd", Wo_sb[:, :].rearrange("p (k n) -> p k n", k=KT),
                Wo_d[:, :].rearrange("(k p) n -> p k n", p=128), waits=aofence, inc="swo")

    aosc = ExitStack()
    ao_t = SB(aosc, "ao_t", [128, 8 * H], BF16)
    p_tr2 = [PS(aosc, "ptr2a", [128, 1024], BF16), PS(aosc, "ptr2b", [128, 1024], BF16)]

    ao_dve, ao_act, ao_trpe = {}, {}, {}
    ao_ms, ao_mu, ao_rs = {}, {}, {}

    def ao_s1(st):
        av = ao_t[:, (st % 8) * H:(st % 8 + 1) * H]
        dep = list(n_ao_stores)
        if st >= 8:
            dep += [("dve", ao_dve.get(st - 8, 0)), ("act", ao_act.get(st - 8, 0))]
        ldn = p.dma("sync", av, ao_scr[st * 128:(st + 1) * 128, :], waits=dep,
                    inc="sal%d" % (st % 8))
        p.emit("vector", lambda e, st=st, av=av: e.reduce_sum(
            stats["amus"][:, st:st + 1], av, axis=AXX),
            waits=[ldn], inc="dve")
        ao_ms[st] = p.n("dve")
        p.emit("scalar", lambda e, st=st, av=av: e.activation(
            deadout[:, :], av, AF.Square, accum_out=stats["assq"][:, st:st + 1]),
            waits=[ldn, ("act", last_dead[0])], inc="act")
        ao_act[st] = p.n("act")
        last_dead[0] = ao_act[st]

    def ao_s2(st):
        p.emit("vector", lambda e, st=st: e.tensor_mul(
            stats["amt"][:, st:st + 1], stats["amus"][:, st:st + 1],
            stats["amus"][:, st:st + 1]), waits=[("dve", ao_ms[st])], inc="dve")
        t1n = p.n("dve")
        p.emit("vector", lambda e, st=st: e.scalar_tensor_tensor(
            stats["avt"][:, st:st + 1], stats["amt"][:, st:st + 1], 1.0 / H,
            stats["assq"][:, st:st + 1], ALU.mult, ALU.subtract),
            waits=[("act", ao_act[st]), ("dve", t1n)], inc="dve")
        dn = p.n("dve")
        p.emit("vector", lambda e, st=st: e.tensor_scalar_mul(
            stats["amu"][:, st:st + 1], stats["amus"][:, st:st + 1], 1.0 / H),
            inc="dve")
        ao_mu[st] = p.n("dve")
        p.emit("scalar", lambda e, st=st: e.activation(
            stats["avt"][:, st:st + 1], stats["avt"][:, st:st + 1], AF.Ln,
            scale=-1.0 / H, bias=eps_col[:, 0:1]),
            waits=[("dve", dn), ("pl", pl_const)], inc="act")
        lnn = p.n("act")
        p.emit("scalar", lambda e, st=st: e.activation(
            stats["arstd"][:, st:st + 1], stats["avt"][:, st:st + 1], AF.Exp, scale=-0.5),
            waits=[("act", lnn)], inc="act")
        ao_rs[st] = p.n("act")

    def ao_s3(st):
        pp = st % 2
        av = ao_t[:, (st % 8) * H:(st % 8 + 1) * H]
        hv = h_bf[:, pp * H:(pp + 1) * H]
        hdep = [("act", ao_rs[st]), ("dve", ao_mu[st])]
        if st >= 2:
            hdep.append(("pe", ao_trpe[st - 2]))
        p.emit("vector", lambda e, av=av, hv=hv, st=st: e.tensor_scalar(
            hv, av, stats["amu"][:, st:st + 1], stats["arstd"][:, st:st + 1],
            ALU.subtract, ALU.mult), waits=hdep, inc="dve")
        ao_dve[st] = p.n("dve")
        trdep = [("dve", ao_dve[st]), ("pe", a_attn_pe_end)]
        if st >= 2:
            trdep.append(("act", ao_act.get(st - 2, 0)))
        for kt in range(KT):
            p.emit("tensor", lambda e, kt=kt, pp=pp, hv=hv: e.transpose(
                p_tr2[pp][:, kt * 128:(kt + 1) * 128],
                hv.rearrange("p (k c) -> p k c", k=KT)[:, kt, :], ident_b[:, :]),
                waits=trdep if kt == 0 else None,
                inc="pe" if kt == KT - 1 else None)
        ao_trpe[st] = p.n("pe")
        p.emit("scalar", lambda e, st=st, pp=pp: e.activation(
            tT[:, :].rearrange("p (k s) -> p k s", k=KT)[:, :, st * 128:(st + 1) * 128],
            p_tr2[pp][:, :].rearrange("p (k s) -> p k s", k=KT), AF.Copy),
            waits=[("pe", ao_trpe[st])], inc="act")
        ao_act[st] = p.n("act")

    for st in range(NT + 2):
        if st < NT:
            ao_s1(st)
        if 1 <= st <= NT:
            ao_s2(st - 1)
        if 2 <= st <= NT + 1:
            ao_s3(st - 2)
    aosc.close()

    osc = ExitStack()
    gac_r = SB(osc, "gac_r", [128, H], F32)
    ga_eff_r = SB(osc, "ga_eff_r", [128, H], F32)
    gabo_r = SB(osc, "gabo_r", [128, H], F32)
    boe_r = SB(osc, "boe_r", [128, H], F32)
    boerow = SB(osc, "boerow", [1, H], F32)
    x1_sb = SB(osc, "x1_sb", [128, 2 * H], F32)
    evac32 = SB(osc, "evac32", [128, 1024], F32)
    p_y = [PS(osc, "py%d" % i, [128, 512]) for i in range(4)]
    p_ob = PS(osc, "p_ob", [1, 1024])
    p_bc3 = PS(osc, "p_bc3", [128, 512])

    ofence = fence(dma=("sqk0", "sqk1", "sas0", "sas1", "sal0", "sal1",
                        "sal2", "sal3", "sal4", "sal5", "sal6", "sal7"))
    for kt in range(KT):
        for ch in range(2):
            p.emit("tensor", lambda e, kt=kt, ch=ch: e.matmul(
                p_ob[0:1, ch * 512:(ch + 1) * 512], lnbT_b[:, kt:kt + 1],
                Wo_sb[:, kt * H + ch * 512: kt * H + (ch + 1) * 512],
                start=(kt == 0), stop=(kt == KT - 1), skip_group_check=True),
                waits=ofence + [won] if (kt == 0 and ch == 0) else None,
                inc="pe" if (kt == KT - 1 and ch == 1) else None)
    pn = p.n("pe")
    p.emit("vector", lambda e: e.scalar_tensor_tensor(
        boerow[0:1, :], p_ob[0:1, :], 0.2, borow[0:1, :], ALU.mult, ALU.add),
        waits=[("pe", pn), nbo], inc="dve")
    d_boe = p.n("dve")
    for kt in range(KT):
        p.emit("vector", lambda e, kt=kt: e.tensor_scalar_mul(
            Wo_sb[:, kt * H:(kt + 1) * H], Wo_sb[:, kt * H:(kt + 1) * H],
            lnwT[:, kt:kt + 1]),
            waits=[("sx", n_small)] if kt == 0 else None,
            inc="dve" if kt == KT - 1 else None)
    d_wo = p.n("dve")
    last_bc = 0
    for dst, lhs, srcrow, wdep in ((gac_r, ones_row_f, gac_row, d_clamp),
                                   (ga_eff_r, p2_row_f, gac_row, d_clamp),
                                   (boe_r, ones_row_f, boerow, d_boe)):
        for ch in range(2):
            dep = [("dve", wdep), ("pl", pl_const)]
            if last_bc:
                dep.append(("act", last_bc))
            p.emit("tensor", lambda e, lhs=lhs, srcrow=srcrow, ch=ch: e.matmul(
                p_bc3[:, :], lhs[0:1, :], srcrow[0:1, ch * 512:(ch + 1) * 512],
                start=True, stop=True, skip_group_check=True),
                waits=dep, inc="pe")
            pn = p.n("pe")
            p.emit("scalar", lambda e, dst=dst, ch=ch: e.activation(
                dst[:, ch * 512:(ch + 1) * 512], p_bc3[:, :], AF.Copy),
                waits=[("pe", pn)], inc="act")
            last_bc = p.n("act")
    p.emit("vector", lambda e: e.tensor_mul(gabo_r[:, :], gac_r[:, :], boe_r[:, :]),
           waits=[("act", last_bc)], inc="dve")
    d_gabo = p.n("dve")

    o_dve, o_act, x1_dma, o_pl, o_sq = {}, {}, {}, {}, {}

    def emit_o_stats2(st):
        p.emit("vector", lambda e, st=st: e.tensor_mul(
            stats["m2t"][:, st:st + 1], stats["mu2s"][:, st:st + 1],
            stats["mu2s"][:, st:st + 1]), waits=[("act", o_sq[st])], inc="dve")
        t1n = p.n("dve")
        p.emit("vector", lambda e, st=st: e.scalar_tensor_tensor(
            stats["v2t"][:, st:st + 1], stats["m2t"][:, st:st + 1], 1.0 / H,
            stats["ssq2"][:, st:st + 1], ALU.mult, ALU.subtract),
            waits=[("dve", t1n)], inc="dve")
        dn = p.n("dve")
        p.emit("vector", lambda e, st=st: e.tensor_scalar_mul(
            stats["mu2"][:, st:st + 1], stats["mu2s"][:, st:st + 1], 1.0 / H),
            inc="dve")
        p.emit("scalar", lambda e, st=st: e.activation(
            stats["v2t"][:, st:st + 1], stats["v2t"][:, st:st + 1], AF.Ln,
            scale=-1.0 / H, bias=eps_col[:, 0:1]),
            waits=[("dve", dn), ("pl", pl_const)], inc="act")
        lnn = p.n("act")
        p.emit("scalar", lambda e, st=st: e.activation(
            stats["rstd2"][:, st:st + 1], stats["v2t"][:, st:st + 1], AF.Exp, scale=-0.5),
            waits=[("act", lnn)], inc="act")
        o_act[st] = p.n("act")

    for st in range(NT):
        pp = st % 2
        xv = x_tiles[:, (st % 4) * H:(st % 4 + 1) * H]
        x1v = x1_sb[:, pp * H:(pp + 1) * H]
        if st < 4:
            xdep = [("dve", ln1_dve[NT - 4 + st]), ("act", ln1_act[NT - 4 + st])]
        else:
            xdep = [("dve", o_dve[st - 4]), ("pl", o_pl[st - 4])]
        xn = p.dma("sync", xv, x_d[st * 128:(st + 1) * 128, :], waits=xdep,
                   inc="sx%d" % (st % 4))
        p.emit("gpsimd", lambda e, xv=xv: e.tensor_add(xv, xv, gabo_r[:, :]),
               waits=[xn, ("dve", d_gabo)], inc="pl")
        o_pl[st] = p.n("pl")
        mmdep = [("dve", d_wo), ("act", ao_act[st])]
        if st >= 2:
            mmdep.append(("dve", o_dve[st - 2]))
        for ch in range(2):
            pbank = p_y[2 * pp + ch]
            for kt in range(KT):
                aT_t = tT[:, kt * S + st * 128: kt * S + (st + 1) * 128]
                p.emit("tensor", lambda e, pbank=pbank, aT_t=aT_t, kt=kt, ch=ch: e.matmul(
                    pbank[:, :], aT_t, Wo_sb[:, kt * H + ch * 512: kt * H + (ch + 1) * 512],
                    start=(kt == 0), stop=(kt == KT - 1), skip_group_check=True),
                    waits=mmdep if (kt == 0 and ch == 0) else None,
                    inc="pe" if (kt == KT - 1 and ch == 1) else None)
        opn = p.n("pe")
        evdep = [("pe", opn), xn, ("dve", d_gabo)]
        if st >= 2:
            evdep.append(x1_dma[st - 2])
        if st >= 1:
            evdep.append(("dve", o_dve[st - 1]))
        for ch in range(2):
            pbank = p_y[2 * pp + ch]
            cs = slice(ch * 512, (ch + 1) * 512)
            p.emit("vector", lambda e, pbank=pbank, cs=cs: e.tensor_mul(
                evac32[:, cs], pbank[:, :], ga_eff_r[:, cs]),
                waits=evdep if ch == 0 else None, inc="dve")
        mn = p.n("dve")
        for ch in range(2):
            cs = slice(ch * 512, (ch + 1) * 512)
            p.emit("vector", lambda e, cs=cs, x1v=x1v, xv=xv: e.tensor_add(
                x1v[:, cs], evac32[:, cs], xv[:, cs]),
                waits=[("pl", o_pl[st]), ("dve", mn)] if ch == 0 else None,
                inc="dve" if ch == 1 else None)
        o_dve[st] = p.n("dve")
        p.emit("scalar", lambda e, st=st, x1v=x1v: e.activation(
            deadout[:, :], x1v, AF.Copy, accum_out=stats["mu2s"][:, st:st + 1]),
            waits=[("dve", o_dve[st]), ("act", last_dead[0])], inc="act")
        last_dead[0] = p.n("act")
        p.emit("scalar", lambda e, st=st, x1v=x1v: e.activation(
            deadout[:, :], x1v, AF.Square, accum_out=stats["ssq2"][:, st:st + 1]),
            waits=[("act", last_dead[0])], inc="act")
        o_sq[st] = p.n("act")
        last_dead[0] = o_sq[st]
        if st >= 1:
            x1_dma[st - 1] = p.dma(
                "gpsimd", x1_scr[(st - 1) * 128:st * 128, :],
                x1_sb[:, ((st - 1) % 2) * H:((st - 1) % 2 + 1) * H],
                waits=[("dve", o_dve[st - 1])], inc="sx1%d" % ((st - 1) % 2))
        if st == NT - 1:
            x1_dma[st] = p.dma("gpsimd", x1_scr[st * 128:(st + 1) * 128, :], x1v,
                               waits=[("dve", o_dve[st])], inc="sx1%d" % (st % 2))
        if st >= 1:
            emit_o_stats2(st - 1)
        if st == NT - 1:
            emit_o_stats2(st)
    tap("x1t0", x1_sb[:, 0:H], [("dve", o_dve[NT-2])])
    tap("mu2", stats["mu2"][:, :], [("act", o_act[NT-1])])
    tap("rstd2", stats["rstd2"][:, :], [("act", o_act[NT-1])])
    tap("gabo", gabo_r[:, :], [("dve", d_gabo)])
    tap("gaeff", ga_eff_r[:, :], [("dve", d_gabo)])
    osc.close()
    wsc.close()
    tsc.close()
    pre.close()

    fsc = ExitStack()
    w1_sb = SB(fsc, "w1_sb", [128, KT * FF], BF16)
    w2_sb = SB(fsc, "w2_sb", [128, MT * H], BF16)
    g2_sb = SB(fsc, "g2_sb", [128, MT * SBW], BF16)
    h2T = SB(fsc, "h2T", [128, 2 * KT * SBW], BF16)
    h2_bf = SB(fsc, "h2_bf", [128, 2 * H], BF16)
    gb_eff_r = SB(fsc, "gb_eff_r", [128, H], BF16)
    gbb2_r = SB(fsc, "gbb2_r", [128, H], F32)
    b1T = SB(fsc, "b1T", [128, MT], F32)
    b1pT = SB(fsc, "b1pT", [128, MT], F32)
    x1f = SB(fsc, "x1f", [128, 4 * H], F32)
    x2st = SB(fsc, "x2st", [128, 2 * H], F32)
    b2r_f = x2st
    p_u = [PS(fsc, "pu0", [128, 256]), PS(fsc, "pu1", [128, 256])]
    p_y2 = [PS(fsc, "py2%d" % i, [128, 512]) for i in range(4)]
    p_trF = PS(fsc, "p_trF", [128, 1024], BF16)
    p_fb = PS(fsc, "p_fb", [128, 512])

    ffence = fence(dma=("sx10", "sx11", "sx0", "sx1", "sx2", "sx3", "swo",
                        "sal0", "sal1", "sal2", "sal3", "sal4", "sal5",
                        "sal6", "sal7", "sas0", "sas1"))
    for kt in range(MT):
        p.dma("sync", b1T[:, kt:kt + 1], b1_d[kt * 128:(kt + 1) * 128][:, None],
              waits=ffence, inc="sb1")
    n_b1 = ("sb1", p.n("sb1"))
    for q4 in range(4):
        p.dma("gpsimd",
              w1_sb[:, :].rearrange("p (k f) -> p k f", k=KT)[:, 2 * q4:2 * q4 + 2, :],
              w1_d[:, :].rearrange("(k p) f -> p k f", p=128)[:, 2 * q4:2 * q4 + 2, :],
              waits=ffence, inc="sw1")
    w1n = ("sw1", p.n("sw1"))
    for q4 in range(4):
        p.dma("gpsimd",
              w2_sb[:, :].rearrange("p (k f) -> p k f", k=MT)[:, 8 * q4:8 * q4 + 8, :],
              w2_d[:, :].rearrange("(k p) f -> p k f", p=128)[:, 8 * q4:8 * q4 + 8, :],
              waits=ffence, inc="sw2")
    w2n = ("sw2", p.n("sw2"))

    last_bc = 0
    first_bc = True
    for dst, srcrow, dt_ in ((gb_eff_r, gbc_row, BF16), (gbb2_r, gbc_row, F32),
                             (b2r_f, b2row, F32)):
        for ch in range(2):
            dep = [("dve", d_clamp), ("sx", n_small)]
            if first_bc:
                dep += ffence
                first_bc = False
            if last_bc:
                dep.append(("act", last_bc))
            p.emit("tensor", lambda e, srcrow=srcrow, ch=ch: e.matmul(
                p_fb[:, :], ones_row_f[0:1, :], srcrow[0:1, ch * 512:(ch + 1) * 512],
                start=True, stop=True, skip_group_check=True),
                waits=dep, inc="pe")
            pn = p.n("pe")
            p.emit("scalar", lambda e, dst=dst, ch=ch: e.activation(
                dst[:, ch * 512:(ch + 1) * 512], p_fb[:, :], AF.Copy),
                waits=[("pe", pn)], inc="act")
            last_bc = p.n("act")
    p.emit("vector", lambda e: e.tensor_mul(gbb2_r[:, :], gbb2_r[:, :], b2r_f[:, 0:H]),
           waits=[("act", last_bc)] + ffence, inc="dve")
    d_gbb2 = p.n("dve")

    b1cp = p.n("act")
    for c8 in range(8):
        for kt in range(KT):
            p.emit("tensor", lambda e, c8=c8, kt=kt: e.matmul(
                p_fb[0:1, 0:512], sbT_b[:, kt:kt + 1],
                w1_sb[:, kt * FF + c8 * 512: kt * FF + (c8 + 1) * 512],
                start=(kt == 0), stop=(kt == KT - 1), skip_group_check=True),
                waits=[w1n, ("act", a_modT), ("act", b1cp)] if kt == 0 else None,
                inc="pe" if kt == KT - 1 else None)
        pn = p.n("pe")
        p.emit("scalar", lambda e: e.activation(nrow[0:1, 0:512], p_fb[0:1, 0:512], AF.Copy),
               waits=[("pe", pn)], inc="act")
        an = p.n("act")
        b1cp = an
        for t4 in range(4):
            p.emit("tensor", lambda e, c8=c8, t4=t4: e.matmul(
                p_y2[0][:, c8 * 4 + t4: c8 * 4 + t4 + 1],
                nrow[0:1, t4 * 128:(t4 + 1) * 128], ones_row_f[0:1, 0:1],
                start=True, stop=True, skip_group_check=True),
                waits=[("act", an)], inc="pe" if t4 == 3 else None)
    pn = p.n("pe")
    p.emit("scalar", lambda e: e.activation(b1pT[:, 0:MT], p_y2[0][:, 0:MT], AF.Copy),
           waits=[("pe", pn)], inc="act")
    an = p.n("act")
    p.emit("vector", lambda e: e.tensor_add(b1pT[:, :], b1pT[:, :], b1T[:, :]),
           waits=[("act", an), n_b1], inc="dve")
    d_b1p = p.n("dve")
    for kt in range(KT):
        p.emit("vector", lambda e, kt=kt: e.tensor_scalar_mul(
            w1_sb[:, kt * FF:(kt + 1) * FF], w1_sb[:, kt * FF:(kt + 1) * FF],
            cb1T[:, kt:kt + 1]), inc="dve" if kt == KT - 1 else None)
    d_w1 = p.n("dve")

    f_dve, out_dma, gelu_act, trF_act, u_pe, y2_pe, trF_pe = {}, {}, {}, {}, {}, {}, {}

    def f_prep(sb_i):
        ppb = sb_i % 2
        x1v = x1f[:, ppb * 2 * H:(ppb + 1) * 2 * H]
        xdep = [x1_dma[2 * sb_i], x1_dma[2 * sb_i + 1]] + (ffence if sb_i < 2 else [])
        if sb_i >= 2:
            xdep.append(("dve", f_dve[2 * (sb_i - 2) + 1]))
        xn = p.dma(
            "sync",
            x1v.rearrange("p (i c) -> p i c", i=2),
            x1_scr[sb_i * SBW:(sb_i + 1) * SBW, :].rearrange("(i p) c -> p i c", p=128),
            waits=xdep, inc="sf%d" % (sb_i % 2))
        for i in range(2):
            st = 2 * sb_i + i
            hdep = [xn, ("act", o_act[st])]
            if sb_i < 2:
                hdep = hdep + ffence
            if sb_i >= 1:
                hdep.append(("pe", trF_pe[sb_i - 1]))
            p.emit("vector", lambda e, st=st, i=i, ppb=ppb: e.tensor_scalar(
                h2_bf[:, i * H:(i + 1) * H],
                x1f[:, :].rearrange("p (s i c) -> p s i c", s=2, i=2)[:, ppb, i, :],
                stats["mu2"][:, st:st + 1], stats["rstd2"][:, st:st + 1],
                ALU.subtract, ALU.mult),
                waits=hdep, inc="dve" if i == 1 else None)
        dh = p.n("dve")
        for i in range(2):
            trdep = [("dve", dh)]
            if (sb_i, i - 1) in trF_act:
                trdep.append(("act", trF_act[(sb_i, i - 1)]))
            elif (sb_i - 1, 1) in trF_act:
                trdep.append(("act", trF_act[(sb_i - 1, 1)]))
            for kt in range(KT):
                p.emit("tensor", lambda e, kt=kt, i=i: e.transpose(
                    p_trF[:, kt * 128:(kt + 1) * 128],
                    h2_bf[:, i * H + kt * 128: i * H + (kt + 1) * 128],
                    ident_b[:, :]),
                    waits=trdep if kt == 0 else None,
                    inc="pe" if kt == KT - 1 else None)
            ptn = p.n("pe")
            trF_pe[sb_i] = ptn
            p.emit("scalar", lambda e, i=i, ppb=ppb: e.activation(
                h2T.rearrange("p (s k i c) -> p s k i c", s=2, k=KT, i=2)[:, ppb, :, i, :],
                p_trF[:, :].rearrange("p (k c) -> p k c", k=KT), AF.Copy),
                waits=[("pe", ptn)], inc="act")
            trF_act[(sb_i, i)] = p.n("act")

    f_prep(0)
    for sb_i in range(NSB):
        ppb = sb_i % 2
        a_h2T = trF_act[(sb_i, 1)]
        for mt in range(MT):
            ppu = mt % 2
            udep = [("act", a_h2T), ("dve", d_w1), ("dve", d_b1p)]
            gi = sb_i * MT + mt
            if gi >= 2:
                udep.append(("act", gelu_act[gi - 2]))
            for kt in range(KT):
                p.emit("tensor", lambda e, ppu=ppu, mt=mt, kt=kt, ppb=ppb: e.matmul(
                    p_u[ppu][:, :],
                    w1_sb[:, kt * FF + mt * 128: kt * FF + (mt + 1) * 128],
                    h2T[:, ppb * KT * SBW + kt * SBW: ppb * KT * SBW + (kt + 1) * SBW],
                    start=(kt == 0), stop=(kt == KT - 1), skip_group_check=True),
                    waits=udep if kt == 0 else None,
                    inc="pe" if kt == KT - 1 else None)
            pn = p.n("pe")
            gdep = [("pe", pn)]
            if sb_i >= 1 and mt == 0:
                gdep.append(("pe", y2_pe[sb_i - 1]))
            p.emit("scalar", lambda e, ppu=ppu, mt=mt: e.activation(
                g2_sb[:, mt * SBW:(mt + 1) * SBW], p_u[ppu][:, :], AF.Gelu,
                bias=b1pT[:, mt:mt + 1]),
                waits=gdep, inc="act")
            gelu_act[gi] = p.n("act")
        u_pe[sb_i] = p.n("pe")
        if sb_i + 1 < NSB:
            f_prep(sb_i + 1)
        for i in range(2):
            st = 2 * sb_i + i
            ppy = st % 2
            for ch in range(2):
                pbank = p_y2[2 * ppy + ch]
                for kt in range(MT):
                    ydep = [("act", gelu_act[sb_i * MT + kt])]
                    if st >= 2 and kt == 0 and ch == 0:
                        ydep.append(("dve", f_dve[st - 2]))
                    p.emit("tensor", lambda e, pbank=pbank, kt=kt, i=i, ch=ch: e.matmul(
                        pbank[:, :],
                        g2_sb[:, kt * SBW + i * 128: kt * SBW + i * 128 + 128],
                        w2_sb[:, kt * H + ch * 512: kt * H + (ch + 1) * 512],
                        start=(kt == 0), stop=(kt == MT - 1), skip_group_check=True),
                        waits=(ydep + [w2n]) if kt == 0 else
                              [("act", gelu_act[sb_i * MT + kt])],
                        inc="pe" if kt == MT - 1 else None)
            ypn = p.n("pe")
            y2_pe[sb_i] = ypn
            evdep = [("pe", ypn), ("dve", d_gbb2)]
            if st >= 2:
                evdep.append(out_dma[st - 2])
            xt = x2st[:, ppy * H:(ppy + 1) * H]
            for ch in range(2):
                pbank = p_y2[2 * ppy + ch]
                cs = slice(ch * 512, (ch + 1) * 512)
                p.emit("vector", lambda e, pbank=pbank, cs=cs, xt=xt: e.tensor_mul(
                    xt[:, cs], pbank[:, :], gb_eff_r[:, cs]),
                    waits=evdep if ch == 0 else None, inc="dve")
            mn = p.n("dve")
            p.emit("gpsimd", lambda e, xt=xt: e.tensor_add(
                xt[:, :], xt[:, :], gbb2_r[:, :]),
                waits=[("dve", mn), ("dve", d_gbb2)], inc="pl")
            gn = p.n("pl")
            for ch in range(2):
                cs = slice(ch * 512, (ch + 1) * 512)
                p.emit("vector", lambda e, cs=cs, xt=xt, i=i, ppb=ppb: e.tensor_add(
                    xt[:, cs],
                    xt[:, cs],
                    x1f.rearrange("p (s i c) -> p s i c", s=2, i=2)[:, ppb, i, :][:, cs]),
                    waits=[("pl", gn)] if ch == 0 else None,
                    inc="dve" if ch == 1 else None)
            f_dve[st] = p.n("dve")
            out_dma[st] = p.dma("sync", out_d[st * 128:(st + 1) * 128, :],
                                x2st[:, ppy * H:(ppy + 1) * H],
                                waits=[("dve", f_dve[st])],
                                inc="so%d" % (st % 2))
    p.emit("sync", lambda e: e.nop(),
           waits=[("so0", p.n("so0")), ("so1", p.n("so1"))])
    fsc.close()

    with nc.Block() as block:
        p.realize(block, sems)
    root.close()
    return nc


_NC_CACHE = None


def kernel(**inputs):
    global _NC_CACHE
    if _NC_CACHE is None:
        _NC_CACHE = build_nc()
    nc = _NC_CACHE

    x = np.ascontiguousarray(inputs["L0_attn_hidden_states"])
    c = np.ascontiguousarray(inputs["L0_c"])
    shared = {
        "Wq": np.ascontiguousarray(inputs["L0_attn_Wq"]),
        "Wk": np.ascontiguousarray(inputs["L0_attn_Wk"]),
        "Wv": np.ascontiguousarray(inputs["L0_attn_Wv"]),
        "Wo": np.ascontiguousarray(inputs["L0_attn_Wo"]),
        "bq": np.ascontiguousarray(inputs["L0_attn_bq"]),
        "bk": np.ascontiguousarray(inputs["L0_attn_bk"]),
        "bv": np.ascontiguousarray(inputs["L0_attn_bv"]),
        "bo": np.ascontiguousarray(inputs["L0_attn_bo"]),
        "lq1": np.ascontiguousarray(inputs["L0_attn_lambda_q1"]),
        "lk1": np.ascontiguousarray(inputs["L0_attn_lambda_k1"]),
        "lq2": np.ascontiguousarray(inputs["L0_attn_lambda_q2"]),
        "lk2": np.ascontiguousarray(inputs["L0_attn_lambda_k2"]),
        "lnw": np.ascontiguousarray(inputs["L0_attn_ln_w"]),
        "lnb": np.ascontiguousarray(inputs["L0_attn_ln_b"]),
        "w1": np.ascontiguousarray(inputs["L0_ffn_w1"]),
        "b1": np.ascontiguousarray(inputs["L0_ffn_b1"]),
        "w2": np.ascontiguousarray(inputs["L0_ffn_w2"]),
        "b2": np.ascontiguousarray(inputs["L0_ffn_b2"]),
        "wmod": np.ascontiguousarray(inputs["L0_wmod"]),
    }
    in_maps = []
    for b in range(B):
        m = dict(shared)
        m["x"] = np.ascontiguousarray(x[b])
        m["c"] = np.ascontiguousarray(c[b])
        in_maps.append(m)

    res = run_bass_kernel_spmd(nc, in_maps, core_ids=list(range(B)))
    out = np.stack([np.asarray(r["out"]) for r in res.results], axis=0).astype(np.float32)
    return out


if __name__ == "__main__":
    nc = build_nc()
    n = sum(1 for _ in nc.all_instructions())
    print("instructions:", n)


# version 30
# speedup vs baseline: 1.0456x; 1.0098x over previous
import numpy as np
from contextlib import ExitStack

import concourse.bass as bass
import concourse.mybir as mybir
from concourse.bass_utils import run_bass_kernel_spmd

F32 = mybir.dt.float32
BF16 = mybir.dt.bfloat16
AF = mybir.ActivationFunctionType
ALU = mybir.AluOpType
AXX = mybir.AxisListType.X

B, S, H, HEADS, HD, FF = 8, 4096, 1024, 8, 128, 4096
NT = S // 128
KT = H // 128
MT = FF // 128
QCW = 1024
NQC = S // QCW
SBW = 256
NSB = S // SBW
EPS = 1e-5
ISQ = float(1.0 / np.sqrt(HD))
LAMBDA_INIT = 0.8
NEG_HALF_LN_HD = float(-0.5 * np.log(HD))

ENGINES = ("tensor", "scalar", "vector", "gpsimd", "sync")
_DBG = False


class Prog:

    def __init__(self):
        self.streams = {e: [] for e in ENGINES}
        self.counts = {}

    def n(self, sem):
        return self.counts.get(sem, 0)

    def emit(self, eng, build, waits=None, inc=None, by=1):
        w = [(s, v) for (s, v) in (waits or []) if v > 0]
        self.streams[eng].append((build, w, inc, by))
        if inc is not None:
            self.counts[inc] = self.counts.get(inc, 0) + by
            return self.counts[inc]
        return None

    def dma(self, eng, dst, src, waits=None, inc="swg"):
        n = self.emit(eng, lambda e, d=dst, s=src: e.dma_start(d, s),
                      waits=waits, inc=inc, by=16)
        return (inc, n)

    def realize(self, block, sems):
        def section(eng_name):
            stream = self.streams[eng_name]

            def body(e):
                watermark = {}
                for build, waits, inc, by in stream:
                    for s_, v in waits:
                        if watermark.get(s_, -1) < v:
                            e.wait_ge(sems[s_], v)
                            watermark[s_] = v
                    inst = build(e)
                    if inc is not None:
                        inst.then_inc(sems[inc], by)
            return body

        block.sync(section("sync"))
        block.gpsimd(section("gpsimd"))
        block.tensor(section("tensor"))
        block.scalar(section("scalar"))
        block.vector(section("vector"))


def build_nc():
    nc = bass.Bass("TRN2", target_bir_lowering=False)

    x_d = nc.dram_tensor("x", [S, H], F32, kind="ExternalInput").ap()
    Wq_d = nc.dram_tensor("Wq", [H, H], F32, kind="ExternalInput").ap()
    Wk_d = nc.dram_tensor("Wk", [H, H], F32, kind="ExternalInput").ap()
    Wv_d = nc.dram_tensor("Wv", [H, H], F32, kind="ExternalInput").ap()
    Wo_d = nc.dram_tensor("Wo", [H, H], F32, kind="ExternalInput").ap()
    bq_d = nc.dram_tensor("bq", [H], F32, kind="ExternalInput").ap()
    bk_d = nc.dram_tensor("bk", [H], F32, kind="ExternalInput").ap()
    bv_d = nc.dram_tensor("bv", [H], F32, kind="ExternalInput").ap()
    bo_d = nc.dram_tensor("bo", [H], F32, kind="ExternalInput").ap()
    lq1_d = nc.dram_tensor("lq1", [HD], F32, kind="ExternalInput").ap()
    lk1_d = nc.dram_tensor("lk1", [HD], F32, kind="ExternalInput").ap()
    lq2_d = nc.dram_tensor("lq2", [HD], F32, kind="ExternalInput").ap()
    lk2_d = nc.dram_tensor("lk2", [HD], F32, kind="ExternalInput").ap()
    lnw_d = nc.dram_tensor("lnw", [H], F32, kind="ExternalInput").ap()
    lnb_d = nc.dram_tensor("lnb", [H], F32, kind="ExternalInput").ap()
    w1_d = nc.dram_tensor("w1", [H, FF], F32, kind="ExternalInput").ap()
    b1_d = nc.dram_tensor("b1", [FF], F32, kind="ExternalInput").ap()
    w2_d = nc.dram_tensor("w2", [FF, H], F32, kind="ExternalInput").ap()
    b2_d = nc.dram_tensor("b2", [H], F32, kind="ExternalInput").ap()
    c_d = nc.dram_tensor("c", [H], F32, kind="ExternalInput").ap()
    wmod_d = nc.dram_tensor("wmod", [H, 6 * H], F32, kind="ExternalInput").ap()
    out_d = nc.dram_tensor("out", [S, H], F32, kind="ExternalOutput").ap()
    ao_scr = nc.dram_tensor("ao_scr", [S, H], BF16, kind="Internal").ap()
    qT_scr = nc.dram_tensor("qT_scr", [HEADS, 128, S], BF16, kind="Internal").ap()
    kT_scr = nc.dram_tensor("kT_scr", [HEADS, 128, S], BF16, kind="Internal").ap()
    x1_scr = nc.dram_tensor("x1_scr", [S, H], F32, kind="Internal").ap()

    root = ExitStack()
    taps = {}

    def SB(stack, name, shape, dt):
        return stack.enter_context(nc.sbuf_tensor(name, shape, dt))

    def PS(stack, name, shape, dt=F32):
        return stack.enter_context(nc.psum_tensor(name, shape, dt))

    ones_row_f = SB(root, "ones_row_f", [1, 128], F32)
    p2_row_f = SB(root, "p2_row_f", [1, 128], F32)
    ones_col_f = SB(root, "ones_col_f", [128, 1], F32)
    ones_col_b = SB(root, "ones_col_b", [128, 1], BF16)
    ident_b = SB(root, "ident_b", [128, 128], BF16)
    ident_f = SB(root, "ident_f", [128, 128], F32)

    gac_row = SB(root, "gac_row", [1, H], F32)
    gbc_row = SB(root, "gbc_row", [1, H], F32)
    ca1T = SB(root, "ca1T", [128, KT], F32)
    cb1T = SB(root, "cb1T", [128, KT], F32)
    saT_b = SB(root, "saT_b", [128, KT], BF16)
    sbT_b = SB(root, "sbT_b", [128, KT], BF16)
    bqT = SB(root, "bqT", [128, KT], F32)
    bkT = SB(root, "bkT", [128, KT], F32)
    lnwT = SB(root, "lnwT", [128, KT], F32)
    lnbT_f = SB(root, "lnbT_f", [128, KT], F32)
    lnbT_b = SB(root, "lnbT_b", [128, KT], BF16)
    b2row = SB(root, "b2row", [1, H], F32)
    cT = SB(root, "cT", [128, KT], F32)
    silu_cT = SB(root, "silu_cT", [128, KT], BF16)
    lam_in = SB(root, "lam_in", [128, 4], F32)
    lamneg = SB(root, "lamneg", [128, 1], F32)
    lam11 = SB(root, "lam11", [1, 4], F32)
    Bneg = SB(root, "Bneg", [128, HEADS], F32)
    bnd = SB(root, "bnd", [1, 8], F32)
    bnd16 = SB(root, "bnd16", [1, 16], F32)
    biasqT = SB(root, "biasqT", [128, 4], F32)
    nrow = SB(root, "nrow", [1, 512], F32)
    eps_col = SB(root, "eps_col", [128, 1], F32)
    nhl_col = SB(root, "nhl_col", [128, 1], F32)
    maxcol = SB(root, "maxcol", [128, 2], F32)

    stats = {}
    for nm in ("mu1s", "ssq1", "mu1", "rstd1", "v1t", "m1t",
               "mu2s", "ssq2", "mu2", "rstd2", "v2t", "m2t",
               "amus", "assq", "amu", "arstd", "avt", "amt"):
        stats[nm] = SB(root, nm, [128, NT], F32)

    sems = {}
    for s_ in ("sx", "swg", "sao", "sout", "pe", "act", "dve", "pl",
               "sx0", "sx1", "sx2", "sx3", "wm0", "wm1", "swv", "swq",
               "sqk0", "sqk1", "sal0", "sal1", "sal2", "sal3",
               "sal4", "sal5", "sal6", "sal7", "swo",
               "sw1", "sw2", "sb1", "sas0", "sas1", "sx10", "sx11",
               "sd0", "sd1", "sd2", "sd3", "sf0", "sf1", "so0", "so1"):
        sems[s_] = root.enter_context(nc.semaphore(s_))

    p = Prog()
    last_dead = [0]

    def fence(dma=()):
        w = [("pe", p.n("pe")), ("act", p.n("act")), ("dve", p.n("dve")),
             ("pl", p.n("pl"))]
        for s_ in dma:
            w.append((s_, p.n(s_)))
        return w

    def tap(name, src_ap, waits):
        if not _DBG:
            return
        t = nc.dram_tensor("dbg_" + name, list(src_ap.shape), src_ap.dtype,
                           kind="ExternalOutput").ap()
        p.dma("sync", t[tuple(slice(None) for _ in src_ap.shape)], src_ap,
              waits=waits, inc="sout")

    pre = ExitStack()
    x_tiles = SB(pre, "x_tiles", [128, 4 * H], F32)
    h_bf = SB(pre, "h_bf", [128, 2 * H], BF16)
    deadout = SB(pre, "deadout", [128, 1024], F32)

    p.emit("gpsimd", lambda e: e.memset(ones_row_f[:, :], 1.0))
    p.emit("gpsimd", lambda e: e.memset(p2_row_f[:, :], 5.0))
    p.emit("gpsimd", lambda e: e.memset(ones_col_f[:, :], 1.0))
    p.emit("gpsimd", lambda e: e.memset(ones_col_b[:, :], 1.0))
    p.emit("gpsimd", lambda e: e.iota(
        deadout[:, 0:128], pattern=[[-1, 128]], base=0, channel_multiplier=1,
        allow_small_or_imprecise_dtypes=True), inc="pl")
    pl_iota = p.n("pl")
    p.emit("vector", lambda e: e.tensor_scalar(
        ident_b[:, :], deadout[:, 0:128], 0.0, None, ALU.is_equal),
        waits=[("pl", pl_iota)], inc="dve")
    p.emit("vector", lambda e: e.tensor_scalar(
        ident_f[:, :], deadout[:, 0:128], 0.0, None, ALU.is_equal),
        inc="dve")
    p.emit("vector", lambda e: e.reciprocal(p2_row_f[:, :], p2_row_f[:, :]),
           waits=[("pl", pl_iota)], inc="dve")
    d_ident = p.n("dve")
    p.emit("gpsimd", lambda e: e.memset(eps_col[:, :], EPS),
        waits=[("dve", d_ident)])
    p.emit("gpsimd", lambda e: e.memset(nhl_col[:, :], NEG_HALF_LN_HD), inc="pl")
    pl_const = p.n("pl")

    for kt in range(KT):
        p.dma("sync", bqT[:, kt:kt + 1], bq_d[kt * 128:(kt + 1) * 128][:, None], inc="sx")
        p.dma("sync", bkT[:, kt:kt + 1], bk_d[kt * 128:(kt + 1) * 128][:, None], inc="sx")
        p.dma("sync", lnwT[:, kt:kt + 1], lnw_d[kt * 128:(kt + 1) * 128][:, None], inc="sx")
        p.dma("sync", lnbT_f[:, kt:kt + 1], lnb_d[kt * 128:(kt + 1) * 128][:, None], inc="sx")
        p.dma("sync", cT[:, kt:kt + 1], c_d[kt * 128:(kt + 1) * 128][:, None], inc="sx")
    for i, d in enumerate([lq1_d, lk1_d, lq2_d, lk2_d]):
        p.dma("sync", lam_in[:, i:i + 1], d[:, None], inc="sx")
    p.dma("sync", b2row[0:1, :], b2_d[None, :], inc="sx")
    n_small = p.n("sx")

    p.emit("scalar", lambda e: e.activation(lnbT_b[:, :], lnbT_f[:, :], AF.Copy),
           waits=[("sx", n_small)], inc="act")

    mscope = ExitStack()
    wmod_sb = SB(mscope, "wmod_sb", [128, 2 * 2048], BF16)
    modrow = SB(mscope, "modrow", [1, 6 * H], F32)
    sgn_scr = SB(mscope, "sgn_scr", [1, H], F32)
    saT_f = SB(mscope, "saT_f", [128, KT], F32)
    sbT_f = SB(mscope, "sbT_f", [128, KT], F32)
    lamprod = SB(mscope, "lamprod", [128, 2], F32)
    mod_ps = PS(mscope, "mod_ps", [1, 1024])
    mod_ps2 = [mod_ps, PS(mscope, "mod_psB", [1, 1024])]
    modt_ps = PS(mscope, "modt_ps", [128, 32])
    p_bc = PS(mscope, "p_bc", [128, 512])

    p.emit("scalar", lambda e: e.activation(silu_cT[:, :], cT[:, :], AF.Silu),
           waits=[("sx", n_small)], inc="act")
    a_silu = p.n("act")

    mod_pe = {}
    mod_cp = 0
    for g in range(3):
        for kt in range(KT):
            slot = (g * KT + kt) % 2
            dep = [("pl", pl_const)]
            if (g * KT + kt) >= 2:
                dep.append(("pe", mod_pe[g * KT + kt - 2]))
            wsem = "wm%d" % slot
            p.dma("gpsimd", wmod_sb[:, slot * 2048:(slot + 1) * 2048],
                  wmod_d[kt * 128:(kt + 1) * 128, g * 2048:(g + 1) * 2048],
                  waits=dep, inc=wsem)
            wn = (wsem, p.n(wsem))
            for ch in range(4):
                p.emit("tensor", lambda e, slot=slot, kt=kt, ch=ch: e.matmul(
                    mod_ps[0:1, (ch % 2) * 512:(ch % 2 + 1) * 512]
                    if False else mod_ps2[ch // 2][0:1, (ch % 2) * 512:(ch % 2 + 1) * 512],
                    silu_cT[:, kt:kt + 1],
                    wmod_sb[:, slot * 2048 + ch * 512: slot * 2048 + (ch + 1) * 512],
                    start=(kt == 0), stop=(kt == KT - 1), skip_group_check=True),
                    waits=[wn, ("act", a_silu), ("act", mod_cp)],
                    inc="pe" if ch == 3 else None)
            mod_pe[g * KT + kt] = p.n("pe")
        pn = p.n("pe")
        for half in range(2):
            p.emit("scalar", lambda e, g=g, half=half: e.activation(
                modrow[0:1, g * 2048 + half * 1024:(g * 2048) + (half + 1) * 1024],
                mod_ps2[half][0:1, :], AF.Copy),
                waits=[("pe", pn)], inc="act" if half == 1 else None)
        mod_cp = p.n("act")
    a_modrow = p.n("act")

    for row, off in ((gac_row, 2 * H), (gbc_row, 5 * H)):
        p.emit("vector", lambda e, r=row, o=off: e.tensor_scalar(
            r[0:1, :], modrow[0:1, o:o + H], 0.0, 2.0, ALU.is_ge, ALU.mult),
            waits=[("act", a_modrow)], inc="dve")
        c1 = p.n("dve")
        p.emit("vector", lambda e, r=row: e.tensor_scalar_add(r[0:1, :], r[0:1, :], -1.0),
               waits=[("dve", c1)], inc="dve")
        c2 = p.n("dve")
        p.emit("vector", lambda e, o=off: e.tensor_scalar_mul(
            sgn_scr[0:1, :], modrow[0:1, o:o + H], -1.0), inc="dve")
        c3 = p.n("dve")
        p.emit("vector", lambda e, o=off: e.tensor_max(
            sgn_scr[0:1, :], sgn_scr[0:1, :], modrow[0:1, o:o + H]),
            waits=[("dve", c3)], inc="dve")
        c4 = p.n("dve")
        p.emit("vector", lambda e: e.tensor_scalar_max(
            sgn_scr[0:1, :], sgn_scr[0:1, :], 0.25),
            waits=[("dve", c4)], inc="dve")
        c5 = p.n("dve")
        p.emit("vector", lambda e, r=row: e.tensor_mul(r[0:1, :], r[0:1, :], sgn_scr[0:1, :]),
               waits=[("dve", c2), ("dve", c5)], inc="dve")
    d_clamp = p.n("dve")

    for i, off in enumerate((0, H, 3 * H, 4 * H)):
        for kt in range(KT):
            p.emit("tensor", lambda e, i=i, off=off, kt=kt: e.matmul(
                modt_ps[:, i * KT + kt: i * KT + kt + 1],
                modrow[0:1, off + kt * 128: off + (kt + 1) * 128],
                ones_row_f[0:1, 0:1],
                start=True, stop=True, skip_group_check=True),
                waits=[("act", a_modrow), ("pl", pl_const)],
                inc="pe" if (i == 3 and kt == KT - 1) else None)
    pn = p.n("pe")
    p.emit("scalar", lambda e: e.activation(saT_f[:, :], modt_ps[:, 0:KT], AF.Copy),
           waits=[("pe", pn)])
    p.emit("scalar", lambda e: e.activation(ca1T[:, :], modt_ps[:, KT:2 * KT], AF.Copy))
    p.emit("scalar", lambda e: e.activation(sbT_f[:, :], modt_ps[:, 2 * KT:3 * KT], AF.Copy))
    p.emit("scalar", lambda e: e.activation(cb1T[:, :], modt_ps[:, 3 * KT:4 * KT], AF.Copy),
           inc="act")
    a_mcp = p.n("act")
    p.emit("scalar", lambda e: e.activation(saT_b[:, :], saT_f[:, :], AF.Copy),
           waits=[("act", a_mcp)])
    p.emit("scalar", lambda e: e.activation(sbT_b[:, :], sbT_f[:, :], AF.Copy), inc="act")
    a_modT = p.n("act")
    p.emit("vector", lambda e: e.tensor_scalar_add(ca1T[:, :], ca1T[:, :], 1.0),
           waits=[("act", a_modT)])
    p.emit("vector", lambda e: e.tensor_scalar_add(cb1T[:, :], cb1T[:, :], 1.0), inc="dve")
    d_ca1 = p.n("dve")

    p.emit("vector", lambda e: e.tensor_mul(lamprod[:, 0:1], lam_in[:, 0:1], lam_in[:, 1:2]),
           waits=[("sx", n_small)])
    p.emit("vector", lambda e: e.tensor_mul(lamprod[:, 1:2], lam_in[:, 2:3], lam_in[:, 3:4]),
           inc="dve")
    dn = p.n("dve")
    p.emit("tensor", lambda e: e.matmul(p_bc[0:1, 0:1], lamprod[:, 0:1], ones_col_f[:, 0:1],
                                        start=True, stop=True, skip_group_check=True),
           waits=[("dve", dn)])
    p.emit("tensor", lambda e: e.matmul(p_bc[0:1, 1:2], lamprod[:, 1:2], ones_col_f[:, 0:1],
                                        start=False, stop=True, skip_group_check=True),
           inc="pe")
    pn = p.n("pe")
    p.emit("scalar", lambda e: e.activation(lam11[0:1, 0:2], p_bc[0:1, 0:2], AF.Exp),
           waits=[("pe", pn)], inc="act")
    an = p.n("act")
    p.emit("vector", lambda e: e.tensor_sub(lam11[0:1, 2:3], lam11[0:1, 0:1], lam11[0:1, 1:2]),
           waits=[("act", an)], inc="dve")
    lsn = p.n("dve")
    p.emit("vector", lambda e: e.tensor_scalar(lam11[0:1, 3:4], lam11[0:1, 2:3],
                                               LAMBDA_INIT, -1.0, ALU.add, ALU.mult),
           waits=[("dve", lsn)], inc="dve")
    dn = p.n("dve")
    p.emit("tensor", lambda e: e.matmul(p_bc[:, 4:5], ones_row_f[0:1, :], lam11[0:1, 3:4],
                                        start=True, stop=True, skip_group_check=True),
           waits=[("dve", dn)], inc="pe")
    pn = p.n("pe")
    p.emit("scalar", lambda e: e.activation(lamneg[:, 0:1], p_bc[:, 4:5], AF.Copy),
           waits=[("pe", pn)], inc="act")
    a_lam = p.n("act")
    tap("modrow", modrow[0:1, :], [("act", a_modrow)])
    tap("gac_row", gac_row[0:1, :], [("dve", d_clamp)])
    tap("gbc_row", gbc_row[0:1, :], [("dve", d_clamp)])
    tap("lam11", lam11[0:1, :], [("dve", p.n("dve")), ("act", a_lam)])
    tap("lamneg", lamneg[:, :], [("act", a_lam)])
    tap("ca1T", ca1T[:, :], [("dve", d_ca1)])
    tap("saTb", saT_b[:, :], [("act", a_modT)])
    mscope.close()

    tsc = ExitStack()
    tT = SB(tsc, "tT", [128, KT * S], BF16)
    asc = ExitStack()
    v_aug = SB(asc, "v_aug", [128, NT * 516], BF16)
    biasq8 = SB(asc, "biasq8", [128, KT], F32)
    biask8 = SB(asc, "biask8", [128, KT], F32)

    cmb = ExitStack()
    Wv_sb = SB(cmb, "Wv_sb", [128, KT * H], BF16)
    Wqk_sb = SB(cmb, "Wqk_sb", [128, KT * H], BF16)
    bvrow = SB(cmb, "bvrow", [1, H], F32)
    bvprow = SB(cmb, "bvprow", [1, H], F32)
    beffrow = SB(cmb, "beffrow", [1, 512], F32)
    bveff_c = SB(cmb, "bveff_c", [128, 512], F32)
    vA_scr = SB(cmb, "vA_scr", [128, 2 * 512], BF16)
    nrow2 = SB(cmb, "nrow2", [1, 1024], F32)
    qstg = SB(cmb, "qstg", [128, 4 * 512], BF16)
    p_tr = [PS(cmb, "ptr0", [128, 1024], BF16), PS(cmb, "ptr1", [128, 1024], BF16)]
    p_v = [PS(cmb, "pv%d" % i, [128, 512]) for i in range(4)]
    p_dn = [PS(cmb, "pdn0", [128, 512]), PS(cmb, "pdn1", [128, 512])]

    vfence = fence()
    p.dma("sync", bvrow[0:1, :], bv_d[None, :], waits=vfence, inc="sx")
    nbv = p.n("sx")
    p.dma("gpsimd", Wv_sb[:, :].rearrange("p (k n) -> p k n", k=KT),
          Wv_d[:, :].rearrange("(k p) n -> p k n", p=128), waits=vfence, inc="swv")
    wvn = ("swv", p.n("swv"))
    p.dma("gpsimd", Wqk_sb[:, :].rearrange("p (k n) -> p k n", k=KT),
          Wq_d[:, :].rearrange("(k p) n -> p k n", p=128), waits=vfence, inc="swq")
    wqkn = ("swq", p.n("swq"))
    p.emit("gpsimd", lambda e: e.memset(
        v_aug[:, :].rearrange("p (t j c) -> p (t j) c", t=NT, j=4)[:, :, 128:129], 1.0),
        waits=vfence, inc="pl")
    pl_vones = p.n("pl")

    for ch in range(2):
        for kt in range(KT):
            p.emit("tensor", lambda e, kt=kt, ch=ch: e.matmul(
                p_v[0][0:1, 0:512], saT_b[:, kt:kt + 1],
                Wv_sb[:, kt * H + ch * 512: kt * H + (ch + 1) * 512],
                start=(kt == 0), stop=(kt == KT - 1), skip_group_check=True),
                waits=[wvn, ("act", a_modT), ("dve", p.n("dve"))]
                if kt == 0 else None,
                inc="pe" if kt == KT - 1 else None)
        pn = p.n("pe")
        p.emit("vector", lambda e, ch=ch: e.tensor_copy(
            nrow2[0:1, ch * 512:(ch + 1) * 512], p_v[0][0:1, 0:512]),
            waits=[("pe", pn)], inc="dve")
    dn = p.n("dve")
    p.emit("vector", lambda e: e.tensor_add(bvprow[0:1, :], nrow2[0:1, :], bvrow[0:1, :]),
           waits=[("dve", dn), ("sx", nbv)], inc="dve")
    dn = p.n("dve")
    for j in range(4):
        p.emit("vector", lambda e, j=j: e.scalar_tensor_tensor(
            beffrow[0:1, j * 128:(j + 1) * 128],
            bvprow[0:1, (4 + j) * 128:(5 + j) * 128], lamneg[0:1, 0:1],
            bvprow[0:1, j * 128:(j + 1) * 128], ALU.mult, ALU.add),
            waits=[("dve", dn), ("act", a_lam)], inc="dve")
    dn = p.n("dve")
    p.emit("tensor", lambda e: e.matmul(p_v[1][:, :], ones_row_f[0:1, :], beffrow[0:1, :],
                                        start=True, stop=True, skip_group_check=True),
           waits=[("dve", dn), ("pe", p.n("pe"))], inc="pe")
    pn = p.n("pe")
    p.emit("scalar", lambda e: e.activation(bveff_c[:, :], p_v[1][:, :], AF.Copy),
           waits=[("pe", pn)], inc="act")
    a_bveff = p.n("act")
    for kt in range(KT):
        p.emit("vector", lambda e, kt=kt: e.tensor_scalar_mul(
            Wv_sb[:, kt * H:(kt + 1) * H], Wv_sb[:, kt * H:(kt + 1) * H],
            ca1T[:, kt:kt + 1]),
            waits=[("pe", pn), ("dve", d_ca1)] if kt == 0 else None,
            inc="dve" if kt == KT - 1 else None)
    d_wv = p.n("dve")
    for ch in range(2):
        for kt in range(KT):
            p.emit("tensor", lambda e, kt=kt, ch=ch: e.matmul(
                p_dn[0][0:1, 0:512], saT_b[:, kt:kt + 1],
                Wqk_sb[:, kt * H + ch * 512: kt * H + (ch + 1) * 512],
                start=(kt == 0), stop=(kt == KT - 1), skip_group_check=True),
                waits=[wqkn, ("act", a_modT), ("dve", p.n("dve"))]
                if kt == 0 else None,
                inc="pe" if kt == KT - 1 else None)
        pn = p.n("pe")
        p.emit("vector", lambda e, ch=ch: e.tensor_copy(
            nrow2[0:1, ch * 512:(ch + 1) * 512], p_dn[0][0:1, 0:512]),
            waits=[("pe", pn)], inc="dve")
    dn = p.n("dve")
    for dt_ in range(KT):
        p.emit("tensor", lambda e, dt_=dt_: e.matmul(
            p_dn[1][:, dt_:dt_ + 1], nrow2[0:1, dt_ * 128:(dt_ + 1) * 128],
            ones_row_f[0:1, 0:1], start=True, stop=True, skip_group_check=True),
            waits=[("dve", dn)], inc="pe" if dt_ == KT - 1 else None)
    pn = p.n("pe")
    p.emit("vector", lambda e: e.tensor_add(biasq8[:, :], p_dn[1][:, 0:KT], bqT[:, :]),
           waits=[("pe", pn)], inc="dve")
    d_bq8 = p.n("dve")
    for kt in range(KT):
        p.emit("vector", lambda e, kt=kt: e.tensor_scalar_mul(
            Wqk_sb[:, kt * H:(kt + 1) * H], Wqk_sb[:, kt * H:(kt + 1) * H],
            ca1T[:, kt:kt + 1]),
            waits=[("pe", pn), ("dve", d_ca1)] if kt == 0 else None,
            inc="dve" if kt == KT - 1 else None)
    d_wqk = p.n("dve")

    ln1_dve, ln1_act, tr_pe, tr_act = {}, {}, {}, {}
    l1_ms, l1_mu, l1_rs, l1_xn = {}, {}, {}, {}
    v_comb = {}
    pd_dma = {}
    dn_evac = {}
    nchunk = 0

    def emit_l1_s1(st):
        xv = x_tiles[:, (st % 4) * H:(st % 4 + 1) * H]
        dep = []
        if st >= 4:
            dep += [("dve", ln1_dve.get(st - 4, 0)), ("act", ln1_act.get(st - 4, 0))]
        xn = p.dma("sync", xv, x_d[st * 128:(st + 1) * 128, :], waits=dep,
                   inc="sx%d" % (st % 4))
        p.emit("scalar", lambda e, st=st, xv=xv: e.activation(
            deadout[:, :], xv, AF.Copy, accum_out=stats["mu1s"][:, st:st + 1]),
            waits=[xn, ("act", last_dead[0])] + ([("dve", d_ident)] if st == 0 else []),
            inc="act")
        l1_ms[st] = p.n("act")
        last_dead[0] = l1_ms[st]
        p.emit("scalar", lambda e, st=st, xv=xv: e.activation(
            deadout[:, :], xv, AF.Square, accum_out=stats["ssq1"][:, st:st + 1]),
            waits=[("act", last_dead[0])], inc="act")
        ln1_act[st] = p.n("act")
        last_dead[0] = ln1_act[st]
        l1_xn[st] = xn

    def emit_l1_s2(st):
        p.emit("vector", lambda e, st=st: e.tensor_mul(
            stats["m1t"][:, st:st + 1], stats["mu1s"][:, st:st + 1],
            stats["mu1s"][:, st:st + 1]), waits=[("act", l1_ms[st])], inc="dve")
        t1n = p.n("dve")
        p.emit("vector", lambda e, st=st: e.scalar_tensor_tensor(
            stats["v1t"][:, st:st + 1], stats["m1t"][:, st:st + 1], 1.0 / H,
            stats["ssq1"][:, st:st + 1], ALU.mult, ALU.subtract),
            waits=[("act", ln1_act[st]), ("dve", t1n)], inc="dve")
        dn = p.n("dve")
        p.emit("vector", lambda e, st=st: e.tensor_scalar_mul(
            stats["mu1"][:, st:st + 1], stats["mu1s"][:, st:st + 1], 1.0 / H),
            inc="dve")
        l1_mu[st] = p.n("dve")
        p.emit("scalar", lambda e, st=st: e.activation(
            stats["v1t"][:, st:st + 1], stats["v1t"][:, st:st + 1], AF.Ln,
            scale=-1.0 / H, bias=eps_col[:, 0:1]),
            waits=[("dve", dn), ("pl", pl_const)], inc="act")
        lnn = p.n("act")
        p.emit("scalar", lambda e, st=st: e.activation(
            stats["rstd1"][:, st:st + 1], stats["v1t"][:, st:st + 1], AF.Exp, scale=-0.5),
            waits=[("act", lnn)], inc="act")
        l1_rs[st] = p.n("act")

    def emit_l1_s3(st):
        pp = st % 2
        xv = x_tiles[:, (st % 4) * H:(st % 4 + 1) * H]
        hv = h_bf[:, pp * H:(pp + 1) * H]
        hdep = [("act", l1_rs[st]), ("dve", l1_mu[st])]
        if st >= 2:
            hdep.append(("pe", tr_pe[st - 2]))
        p.emit("vector", lambda e, xv=xv, hv=hv, st=st: e.tensor_scalar(
            hv, xv, stats["mu1"][:, st:st + 1], stats["rstd1"][:, st:st + 1],
            ALU.subtract, ALU.mult), waits=hdep, inc="dve")
        ln1_dve[st] = p.n("dve")
        trdep = [("dve", ln1_dve[st])]
        if st >= 2:
            trdep.append(tr_act[st - 2])
        for kt in range(KT):
            p.emit("tensor", lambda e, kt=kt, pp=pp, hv=hv: e.transpose(
                p_tr[pp][:, kt * 128:(kt + 1) * 128],
                hv.rearrange("p (k c) -> p k c", k=KT)[:, kt, :], ident_b[:, :]),
                waits=trdep if kt == 0 else None,
                inc="pe" if kt == KT - 1 else None)
        tr_pe[st] = p.n("pe")
        p.emit("vector", lambda e, st=st, pp=pp: e.tensor_copy(
            tT[:, :].rearrange("p (k s) -> p k s", k=KT)[:, :, st * 128:(st + 1) * 128],
            p_tr[pp][:, :].rearrange("p (k s) -> p k s", k=KT)),
            waits=[("pe", tr_pe[st])], inc="dve")
        tr_act[st] = ("dve", p.n("dve"))

    def emit_v(st):
        pp = st % 2
        psA, psB = p_v[2 * pp], p_v[2 * pp + 1]
        dep = [tr_act[st], ("dve", d_wv), ("act", a_bveff)]
        if st >= 2:
            dep.append(("dve", v_comb[st - 2]))
        for kt in range(KT):
            hT_t = tT[:, kt * S + st * 128: kt * S + (st + 1) * 128]
            p.emit("tensor", lambda e, psA=psA, hT_t=hT_t, kt=kt: e.matmul(
                psA[:, :], hT_t, Wv_sb[:, kt * H: kt * H + 512],
                start=(kt == 0), stop=(kt == KT - 1), skip_group_check=True),
                waits=dep if kt == 0 else None)
            p.emit("tensor", lambda e, psB=psB, hT_t=hT_t, kt=kt: e.matmul(
                psB[:, :], hT_t, Wv_sb[:, kt * H + 512: (kt + 1) * H],
                start=(kt == 0), stop=(kt == KT - 1), skip_group_check=True),
                inc="pe" if kt == KT - 1 else None)
        pn2 = p.n("pe")
        p.emit("scalar", lambda e, psA=psA, pp=pp: e.activation(
            vA_scr[:, pp * 512:(pp + 1) * 512], psA[:, :], AF.Copy),
            waits=[("pe", pn2)], inc="act")
        acn = p.n("act")
        p.emit("vector", lambda e, psB=psB, pp=pp: e.scalar_tensor_tensor(
            vA_scr[:, pp * 512:(pp + 1) * 512], psB[:, :], lamneg[:, 0:1],
            vA_scr[:, pp * 512:(pp + 1) * 512], ALU.mult, ALU.add),
            waits=[("act", acn), ("act", a_bveff), ("pl", pl_vones)], inc="dve")
        sttn = p.n("dve")
        p.emit("vector", lambda e, st=st, pp=pp: e.tensor_tensor(
            v_aug[:, st * 516:(st + 1) * 516].rearrange("p (j c) -> p j c", j=4)[:, :, 0:128],
            vA_scr[:, pp * 512:(pp + 1) * 512].rearrange("p (j c) -> p j c", j=4),
            bveff_c[:, :].rearrange("p (j c) -> p j c", j=4), ALU.add),
            waits=[("dve", sttn)], inc="dve")
        v_comb[st] = p.n("dve")

    def emit_dense_chunk(ti, dt_, qi, wbuf, scr, bcol, dscale, ev_eng):
        nonlocal_ns = emit_dense_chunk
        n = nonlocal_ns.nchunk
        bank = n % 2
        slot = n % 4
        mmw = [("dve", dscale), tr_act[min(4 * qi + 3, NT - 1)]]
        if n >= 2:
            mmw.append(dn_evac[n - 2])
        for kt in range(KT):
            p.emit("tensor", lambda e, bank=bank, wbuf=wbuf, kt=kt, dt_=dt_, qi=qi: e.matmul(
                p_dn[bank][:, :], wbuf[:, kt * H + dt_ * 128: kt * H + (dt_ + 1) * 128],
                tT[:, kt * S + qi * 512: kt * S + (qi + 1) * 512],
                start=(kt == 0), stop=(kt == KT - 1), skip_group_check=True),
                waits=mmw if kt == 0 else None,
                inc="pe" if kt == KT - 1 else None)
        pn = p.n("pe")
        sdep = [pd_dma[nonlocal_ns.slot_dma[slot]]] if slot in nonlocal_ns.slot_dma else []
        if ev_eng == "scalar":
            p.emit("scalar", lambda e, bank=bank, slot=slot, bcol=bcol, dt_=dt_: e.activation(
                qstg[:, slot * 512:(slot + 1) * 512], p_dn[bank][:, :], AF.Identity,
                bias=bcol[:, dt_:dt_ + 1]),
                waits=[("pe", pn)] + sdep, inc="act")
            evd = ("act", p.n("act"))
        else:
            p.emit("vector", lambda e, bank=bank, slot=slot, bcol=bcol, dt_=dt_: e.tensor_scalar_add(
                qstg[:, slot * 512:(slot + 1) * 512], p_dn[bank][:, :], bcol[:, dt_:dt_ + 1]),
                waits=[("pe", pn)] + sdep, inc="dve")
            evd = ("dve", p.n("dve"))
        dn_evac[n] = evd
        pd_dma[(ti, dt_, qi)] = p.dma(
            "gpsimd", scr[dt_, :, qi * 512:(qi + 1) * 512],
            qstg[:, slot * 512:(slot + 1) * 512], waits=[evd], inc="sd%d" % slot)
        nonlocal_ns.slot_dma[slot] = (ti, dt_, qi)
        nonlocal_ns.nchunk += 1

    emit_dense_chunk.nchunk = 0
    emit_dense_chunk.slot_dma = {}

    for st in range(NT + 2):
        if st < NT:
            emit_l1_s1(st)
        if 1 <= st <= NT:
            emit_l1_s2(st - 1)
        if 2 <= st <= NT + 1:
            emit_l1_s3(st - 2)
        if st >= 4:
            emit_v(st - 4)
        if st >= 7 and ((st - 7) % 4 == 0) and (st - 7) // 4 < 7:
            qi = (st - 7) // 4
            for dt_ in range(KT):
                emit_dense_chunk(0, dt_, qi, Wqk_sb, qT_scr, biasq8, d_wqk, "scalar")
    for st in range(NT - 2, NT):
        emit_v(st)
    for dt_ in range(KT):
        emit_dense_chunk(0, dt_, 7, Wqk_sb, qT_scr, biasq8, d_wqk, "scalar")

    kdep = [("pe", p.n("pe"))]
    p.dma("gpsimd", Wqk_sb[:, :].rearrange("p (k n) -> p k n", k=KT),
          Wk_d[:, :].rearrange("(k p) n -> p k n", p=128), waits=kdep, inc="swg")
    wqkn = p.n("swg")
    for ch in range(2):
        for kt in range(KT):
            p.emit("tensor", lambda e, kt=kt, ch=ch: e.matmul(
                p_dn[0][0:1, 0:512], saT_b[:, kt:kt + 1],
                Wqk_sb[:, kt * H + ch * 512: kt * H + (ch + 1) * 512],
                start=(kt == 0), stop=(kt == KT - 1), skip_group_check=True),
                waits=[("swg", wqkn), ("dve", p.n("dve")),
                       dn_evac.get(emit_dense_chunk.nchunk - 2, ("pl", 0)),
                       dn_evac.get(emit_dense_chunk.nchunk - 1, ("pl", 0))]
                if kt == 0 else None,
                inc="pe" if kt == KT - 1 else None)
        pn = p.n("pe")
        p.emit("vector", lambda e, ch=ch: e.tensor_copy(
            nrow2[0:1, ch * 512:(ch + 1) * 512], p_dn[0][0:1, 0:512]),
            waits=[("pe", pn)], inc="dve")
    dn = p.n("dve")
    for dt_ in range(KT):
        p.emit("tensor", lambda e, dt_=dt_: e.matmul(
            p_dn[1][:, dt_:dt_ + 1], nrow2[0:1, dt_ * 128:(dt_ + 1) * 128],
            ones_row_f[0:1, 0:1], start=True, stop=True, skip_group_check=True),
            waits=[("dve", dn)], inc="pe" if dt_ == KT - 1 else None)
    pn = p.n("pe")
    p.emit("vector", lambda e: e.tensor_add(biask8[:, :], p_dn[1][:, 0:KT], bkT[:, :]),
           waits=[("pe", pn)], inc="dve")
    for kt in range(KT):
        p.emit("vector", lambda e, kt=kt: e.tensor_scalar_mul(
            Wqk_sb[:, kt * H:(kt + 1) * H], Wqk_sb[:, kt * H:(kt + 1) * H],
            ca1T[:, kt:kt + 1]),
            waits=[("pe", pn), ("dve", d_ca1)] if kt == 0 else None,
            inc="dve" if kt == KT - 1 else None)
    d_wk2 = p.n("dve")
    for dt_ in range(KT):
        for qi in range(KT):
            emit_dense_chunk(1, dt_, qi, Wqk_sb, kT_scr, biask8, d_wk2, "vector")
    n_pd_all = [("sd%d" % s_, p.n("sd%d" % s_)) for s_ in range(4)]
    n_pd = {}
    for hh in range(HEADS):
        n_pd[(0, hh)] = [pd_dma[(0, hh, qi)] for qi in range(KT)]
        n_pd[(1, hh)] = [pd_dma[(1, hh, qi)] for qi in range(KT)]
    cmb.close()

    qT2 = SB(asc, "qT2", [128, 2 * S], BF16)
    kT2 = SB(asc, "kT2", [128, 2 * S], BF16)
    sq_scr = SB(asc, "sq_scr", [128, 2 * 1024], BF16)
    expT = SB(asc, "expT", [128, 3 * QCW], BF16)
    ao_ch = SB(asc, "ao_ch", [128, 2 * QCW], BF16)
    rec8 = SB(asc, "rec8", [128, 8], F32)
    qk_ps = [PS(asc, "qk0", [128, 1024]), PS(asc, "qk1", [128, 1024])]
    av_ps = [PS(asc, "avA", [128, 387]), PS(asc, "avB", [128, 387]),
             PS(asc, "avC", [128, 258])]
    p_pr = PS(asc, "p_pr", [128, 512])

    AVJ = [(0, 0), (0, 129), (0, 258), (1, 0), (1, 129), (1, 258), (2, 0), (2, 129)]

    gidx = 0
    exp_act = {}
    evac_dve = {}
    evac_banks = {}
    aoch_dma = {}
    slot_proj_pe = {0: 0, 1: 0}
    av_bank_pe = {}
    proj_dve = {}
    bnd_act = {}

    def prep_units(h):
        sl = h % 2
        qT = qT2[:, sl * S:(sl + 1) * S]
        kTs = kT2[:, sl * S:(sl + 1) * S]
        wdep = ([("pe", slot_proj_pe[sl])] if h > 1
                else fence(dma=("sd0", "sd1", "sd2", "sd3")))
        qsem = "sqk%d" % sl
        p.dma("gpsimd", qT, qT_scr[h, :, :],
              waits=wdep + n_pd[(0, h)], inc=qsem)
        p.dma("gpsimd", kTs, kT_scr[h, :, :],
              waits=n_pd[(1, h)], inc=qsem)
        proj_dve[h] = (qsem, p.n(qsem))
        yield
        rowcp = 0
        rmaxn = [0]
        for t, src_ in enumerate((qT, kTs)):
            for c4 in range(4):
                sq_half = (t * 4 + c4) % 2
                sqdep = [proj_dve[h]]
                if (t * 4 + c4) >= 2:
                    sqdep.append(("pe", p.n("pe")))
                p.emit("vector", lambda e, src_=src_, c4=c4, sq_half=sq_half: e.tensor_mul(
                    sq_scr[:, sq_half * 1024:(sq_half + 1) * 1024],
                    src_[:, c4 * 1024:(c4 + 1) * 1024],
                    src_[:, c4 * 1024:(c4 + 1) * 1024]), waits=sqdep, inc="dve")
                sqn = ("dve", p.n("dve"))
                yield
                yield
                for c2 in range(2):
                    cid = c4 * 2 + c2
                    p.emit("tensor", lambda e, c2=c2, sq_half=sq_half: e.matmul(
                        p_pr[0:1, 0:512], ones_col_b[:, 0:1],
                        sq_scr[:, sq_half * 1024 + c2 * 512: sq_half * 1024 + (c2 + 1) * 512],
                        start=True, stop=True, skip_group_check=True),
                        waits=[sqn, ("dve", rowcp)], inc="pe")
                    pn = p.n("pe")
                    p.emit("vector", lambda e: e.tensor_copy(
                        nrow[0:1, 0:512], p_pr[0:1, 0:512]),
                        waits=[("pe", pn), ("dve", rmaxn[0])], inc="dve")
                    an = p.n("dve")
                    rowcp = an
                    p.emit("vector", lambda e, t=t, cid=cid: e.reduce_max(
                        bnd16[0:1, t * 8 + cid: t * 8 + cid + 1], nrow[0:1, 0:512],
                        axis=AXX), waits=[("dve", an)], inc="dve")
                    rmaxn[0] = p.n("dve")
                    yield
            p.emit("vector", lambda e, t=t: e.reduce_max(
                bnd[0:1, t:t + 1], bnd16[0:1, t * 8:(t + 1) * 8], axis=AXX),
                waits=[("dve", p.n("dve"))], inc="dve")
        dn = p.n("dve")
        p.emit("vector", lambda e: e.tensor_mul(bnd[0:1, 2:3], bnd[0:1, 0:1], bnd[0:1, 1:2]),
               waits=[("dve", dn)], inc="dve")
        dn = p.n("dve")
        p.emit("scalar", lambda e: e.activation(bnd[0:1, 3:4], bnd[0:1, 2:3], AF.Ln),
               waits=[("dve", dn)], inc="act")
        lnn = p.n("act")
        p.emit("scalar", lambda e: e.activation(bnd[0:1, 4:5], bnd[0:1, 3:4], AF.Exp,
                                                scale=0.5, bias=nhl_col[0:1, 0:1]),
               waits=[("act", lnn)], inc="act")
        an = p.n("act")
        yield
        p.emit("tensor", lambda e: e.matmul(
            p_pr[:, 400:401], ones_row_f[0:1, :], bnd[0:1, 4:5],
            start=True, stop=True, skip_group_check=True),
            waits=[("act", an), ("dve", p.n("dve"))], inc="pe")
        pn = p.n("pe")
        yield
        p.emit("scalar", lambda e, h=h: e.activation(
            Bneg[:, h:h + 1], p_pr[:, 400:401], AF.Copy, scale=-1.0),
            waits=[("pe", pn)], inc="act")
        bnd_act[h] = p.n("act")
        yield

    for _ in prep_units(0):
        pass

    for h in range(HEADS):
        sl = h % 2
        qT = qT2[:, sl * S:(sl + 1) * S]
        kTs = kT2[:, sl * S:(sl + 1) * S]
        nxt = prep_units(h + 1) if h + 1 < HEADS else iter(())

        def emit_qk(kt, g, qc):
            pp = g % 2
            qkdep = [("act", bnd_act[h]), proj_dve[h], ("dve", v_comb[NT - 1])]
            if g >= 2:
                qkdep.append(("act", exp_act[g - 2]))
            for half in range(2):
                p.emit("tensor", lambda e, pp=pp, kt=kt, qc=qc, half=half, qT=qT, kTs=kTs: e.matmul(
                    qk_ps[pp][:, half * 512:(half + 1) * 512],
                    kTs[:, kt * 128:(kt + 1) * 128],
                    qT[:, qc * QCW + half * 512: qc * QCW + (half + 1) * 512],
                    start=True, stop=True, skip_group_check=True),
                    waits=qkdep if half == 0 else None,
                    inc="pe" if half == 1 else None)
            qkn = p.n("pe")
            ppe = g % 3
            p.emit("scalar", lambda e, pp=pp, ppe=ppe, h=h: e.activation(
                expT[:, ppe * QCW:(ppe + 1) * QCW], qk_ps[pp][:, :], AF.Exp,
                scale=ISQ, bias=Bneg[:, h:h + 1]),
                waits=[("pe", qkn), ("act", bnd_act[h])], inc="act")
            exp_act[g] = p.n("act")

        def emit_av(kt, g, qc):
            ppe = g % 3
            avdep = [("act", exp_act[g])]
            m_prev = (h * NQC + qc) - 1
            for j in range(8):
                ti, off = AVJ[j]
                fib = (kt == 0) and (off == 0)
                w_ = None
                if j == 0:
                    w_ = list(avdep)
                if kt == 0 and fib and g >= NT:
                    w_ = (w_ or []) + [("dve", evac_banks[m_prev][ti])]
                p.emit("tensor", lambda e, ti=ti, off=off, j=j, kt=kt, ppe=ppe, h=h, fib=fib: e.matmul(
                    av_ps[ti][:, off:off + 129],
                    expT[:, ppe * QCW + j * 128: ppe * QCW + (j + 1) * 128],
                    v_aug[:, kt * 516 + (h % 4) * 129: kt * 516 + (h % 4) * 129 + 129],
                    start=fib, stop=(kt == NT - 1), skip_group_check=True),
                    waits=w_,
                    inc="pe" if (kt == NT - 1 and j in (2, 5, 7)) else None)
                if kt == NT - 1 and j in (2, 5, 7):
                    av_bank_pe[ti] = p.n("pe")

        for qc in range(NQC):
            g0 = gidx
            for kt in range(NT):
                emit_qk(kt, g0 + kt, qc)
                if kt >= 2:
                    emit_av(kt - 2, g0 + kt - 2, qc)
                gk = qc * NT + kt
                if gk == 0 or gk >= 16:
                    next(nxt, None)
            emit_av(NT - 2, g0 + NT - 2, qc)
            emit_av(NT - 1, g0 + NT - 1, qc)
            gidx = g0 + NT
            m = h * NQC + qc
            ppq = qc % 2
            evdep = []
            if m >= 2:
                evdep.append(aoch_dma[m - 2])
            p.emit("vector", lambda e: e.reciprocal(
                rec8[:, 0:3], av_ps[0][:, :].rearrange("p (j c) -> p j c", j=3)[:, :, 128]),
                waits=evdep + [("pe", av_bank_pe[0])])
            p.emit("vector", lambda e: e.reciprocal(
                rec8[:, 3:6], av_ps[1][:, :].rearrange("p (j c) -> p j c", j=3)[:, :, 128]),
                waits=[("pe", av_bank_pe[1])])
            p.emit("vector", lambda e: e.reciprocal(
                rec8[:, 6:8], av_ps[2][:, :].rearrange("p (j c) -> p j c", j=2)[:, :, 128]),
                waits=[("pe", av_bank_pe[2])], inc="dve")
            rcn = p.n("dve")
            bank_free = {}
            for j in range(8):
                ti, off = AVJ[j]
                p.emit("vector", lambda e, ti=ti, off=off, j=j, ppq=ppq: e.tensor_scalar_mul(
                    ao_ch[:, ppq * QCW + j * 128: ppq * QCW + (j + 1) * 128],
                    av_ps[ti][:, off:off + 128], rec8[:, j:j + 1]),
                    waits=[("dve", rcn)] if j == 0 else None,
                    inc="dve" if j in (2, 5, 7) else None)
                if j in (2, 5, 7):
                    bank_free[ti] = p.n("dve")
            evac_dve[m] = p.n("dve")
            evac_banks[m] = bank_free
            aoch_dma[m] = p.dma(
                "sync",
                ao_scr[qc * QCW:(qc + 1) * QCW, h * 128:(h + 1) * 128]
                .rearrange("(j p) d -> p j d", p=128),
                ao_ch[:, ppq * QCW:(ppq + 1) * QCW].rearrange("p (j d) -> p j d", j=8),
                waits=[("dve", evac_dve[m])], inc="sas%d" % (m % 2))
        for _ in nxt:
            pass
        slot_proj_pe[sl] = p.n("pe")
    a_attn_pe_end = p.n("pe")
    n_ao_stores = [("sas0", p.n("sas0")), ("sas1", p.n("sas1"))]

    asc.close()
    wsc = ExitStack()
    Wo_sb = SB(wsc, "Wo_sb", [128, KT * H], BF16)
    borow = SB(wsc, "borow", [1, H], F32)
    aofence = fence(dma=("sqk0", "sqk1", "sas0", "sas1"))
    nbo = p.dma("sync", borow[0:1, :], bo_d[None, :], waits=aofence, inc="sb1")
    won = p.dma("gpsimd", Wo_sb[:, :].rearrange("p (k n) -> p k n", k=KT),
                Wo_d[:, :].rearrange("(k p) n -> p k n", p=128), waits=aofence, inc="swo")

    aosc = ExitStack()
    ao_t = SB(aosc, "ao_t", [128, 8 * H], BF16)
    p_tr2 = [PS(aosc, "ptr2a", [128, 1024], BF16), PS(aosc, "ptr2b", [128, 1024], BF16)]

    ao_dve, ao_act, ao_trpe = {}, {}, {}
    ao_ms, ao_mu, ao_rs = {}, {}, {}

    def ao_s1(st):
        av = ao_t[:, (st % 8) * H:(st % 8 + 1) * H]
        dep = list(n_ao_stores)
        if st >= 8:
            dep += [("dve", ao_dve.get(st - 8, 0)), ("act", ao_act.get(st - 8, 0))]
        ldn = p.dma("sync", av, ao_scr[st * 128:(st + 1) * 128, :], waits=dep,
                    inc="sal%d" % (st % 8))
        p.emit("vector", lambda e, st=st, av=av: e.reduce_sum(
            stats["amus"][:, st:st + 1], av, axis=AXX),
            waits=[ldn], inc="dve")
        ao_ms[st] = p.n("dve")
        p.emit("scalar", lambda e, st=st, av=av: e.activation(
            deadout[:, :], av, AF.Square, accum_out=stats["assq"][:, st:st + 1]),
            waits=[ldn, ("act", last_dead[0])], inc="act")
        ao_act[st] = p.n("act")
        last_dead[0] = ao_act[st]

    def ao_s2(st):
        p.emit("vector", lambda e, st=st: e.tensor_mul(
            stats["amt"][:, st:st + 1], stats["amus"][:, st:st + 1],
            stats["amus"][:, st:st + 1]), waits=[("dve", ao_ms[st])], inc="dve")
        t1n = p.n("dve")
        p.emit("vector", lambda e, st=st: e.scalar_tensor_tensor(
            stats["avt"][:, st:st + 1], stats["amt"][:, st:st + 1], 1.0 / H,
            stats["assq"][:, st:st + 1], ALU.mult, ALU.subtract),
            waits=[("act", ao_act[st]), ("dve", t1n)], inc="dve")
        dn = p.n("dve")
        p.emit("vector", lambda e, st=st: e.tensor_scalar_mul(
            stats["amu"][:, st:st + 1], stats["amus"][:, st:st + 1], 1.0 / H),
            inc="dve")
        ao_mu[st] = p.n("dve")
        p.emit("scalar", lambda e, st=st: e.activation(
            stats["avt"][:, st:st + 1], stats["avt"][:, st:st + 1], AF.Ln,
            scale=-1.0 / H, bias=eps_col[:, 0:1]),
            waits=[("dve", dn), ("pl", pl_const)], inc="act")
        lnn = p.n("act")
        p.emit("scalar", lambda e, st=st: e.activation(
            stats["arstd"][:, st:st + 1], stats["avt"][:, st:st + 1], AF.Exp, scale=-0.5),
            waits=[("act", lnn)], inc="act")
        ao_rs[st] = p.n("act")

    def ao_s3(st):
        pp = st % 2
        av = ao_t[:, (st % 8) * H:(st % 8 + 1) * H]
        hv = h_bf[:, pp * H:(pp + 1) * H]
        hdep = [("act", ao_rs[st]), ("dve", ao_mu[st])]
        if st >= 2:
            hdep.append(("pe", ao_trpe[st - 2]))
        p.emit("vector", lambda e, av=av, hv=hv, st=st: e.tensor_scalar(
            hv, av, stats["amu"][:, st:st + 1], stats["arstd"][:, st:st + 1],
            ALU.subtract, ALU.mult), waits=hdep, inc="dve")
        ao_dve[st] = p.n("dve")
        trdep = [("dve", ao_dve[st]), ("pe", a_attn_pe_end)]
        if st >= 2:
            trdep.append(("act", ao_act.get(st - 2, 0)))
        for kt in range(KT):
            p.emit("tensor", lambda e, kt=kt, pp=pp, hv=hv: e.transpose(
                p_tr2[pp][:, kt * 128:(kt + 1) * 128],
                hv.rearrange("p (k c) -> p k c", k=KT)[:, kt, :], ident_b[:, :]),
                waits=trdep if kt == 0 else None,
                inc="pe" if kt == KT - 1 else None)
        ao_trpe[st] = p.n("pe")
        p.emit("scalar", lambda e, st=st, pp=pp: e.activation(
            tT[:, :].rearrange("p (k s) -> p k s", k=KT)[:, :, st * 128:(st + 1) * 128],
            p_tr2[pp][:, :].rearrange("p (k s) -> p k s", k=KT), AF.Copy),
            waits=[("pe", ao_trpe[st])], inc="act")
        ao_act[st] = p.n("act")

    for st in range(NT + 2):
        if st < NT:
            ao_s1(st)
        if 1 <= st <= NT:
            ao_s2(st - 1)
        if 2 <= st <= NT + 1:
            ao_s3(st - 2)
    aosc.close()

    osc = ExitStack()
    gac_r = SB(osc, "gac_r", [128, H], F32)
    ga_eff_r = SB(osc, "ga_eff_r", [128, H], F32)
    gabo_r = SB(osc, "gabo_r", [128, H], F32)
    boe_r = SB(osc, "boe_r", [128, H], F32)
    boerow = SB(osc, "boerow", [1, H], F32)
    x1_sb = SB(osc, "x1_sb", [128, 2 * H], F32)
    evac32 = SB(osc, "evac32", [128, 1024], F32)
    p_y = [PS(osc, "py%d" % i, [128, 512]) for i in range(4)]
    p_ob = PS(osc, "p_ob", [1, 1024])
    p_bc3 = PS(osc, "p_bc3", [128, 512])

    ofence = fence(dma=("sqk0", "sqk1", "sas0", "sas1", "sal0", "sal1",
                        "sal2", "sal3", "sal4", "sal5", "sal6", "sal7"))
    for kt in range(KT):
        for ch in range(2):
            p.emit("tensor", lambda e, kt=kt, ch=ch: e.matmul(
                p_ob[0:1, ch * 512:(ch + 1) * 512], lnbT_b[:, kt:kt + 1],
                Wo_sb[:, kt * H + ch * 512: kt * H + (ch + 1) * 512],
                start=(kt == 0), stop=(kt == KT - 1), skip_group_check=True),
                waits=ofence + [won] if (kt == 0 and ch == 0) else None,
                inc="pe" if (kt == KT - 1 and ch == 1) else None)
    pn = p.n("pe")
    p.emit("vector", lambda e: e.scalar_tensor_tensor(
        boerow[0:1, :], p_ob[0:1, :], 0.2, borow[0:1, :], ALU.mult, ALU.add),
        waits=[("pe", pn), nbo], inc="dve")
    d_boe = p.n("dve")
    for kt in range(KT):
        p.emit("vector", lambda e, kt=kt: e.tensor_scalar_mul(
            Wo_sb[:, kt * H:(kt + 1) * H], Wo_sb[:, kt * H:(kt + 1) * H],
            lnwT[:, kt:kt + 1]),
            waits=[("sx", n_small)] if kt == 0 else None,
            inc="dve" if kt == KT - 1 else None)
    d_wo = p.n("dve")
    last_bc = 0
    for dst, lhs, srcrow, wdep in ((gac_r, ones_row_f, gac_row, d_clamp),
                                   (ga_eff_r, p2_row_f, gac_row, d_clamp),
                                   (boe_r, ones_row_f, boerow, d_boe)):
        for ch in range(2):
            dep = [("dve", wdep), ("pl", pl_const)]
            if last_bc:
                dep.append(("act", last_bc))
            p.emit("tensor", lambda e, lhs=lhs, srcrow=srcrow, ch=ch: e.matmul(
                p_bc3[:, :], lhs[0:1, :], srcrow[0:1, ch * 512:(ch + 1) * 512],
                start=True, stop=True, skip_group_check=True),
                waits=dep, inc="pe")
            pn = p.n("pe")
            p.emit("scalar", lambda e, dst=dst, ch=ch: e.activation(
                dst[:, ch * 512:(ch + 1) * 512], p_bc3[:, :], AF.Copy),
                waits=[("pe", pn)], inc="act")
            last_bc = p.n("act")
    p.emit("vector", lambda e: e.tensor_mul(gabo_r[:, :], gac_r[:, :], boe_r[:, :]),
           waits=[("act", last_bc)], inc="dve")
    d_gabo = p.n("dve")

    o_dve, o_act, x1_dma, o_pl, o_sq = {}, {}, {}, {}, {}

    def emit_o_stats2(st):
        p.emit("vector", lambda e, st=st: e.tensor_mul(
            stats["m2t"][:, st:st + 1], stats["mu2s"][:, st:st + 1],
            stats["mu2s"][:, st:st + 1]), waits=[("act", o_sq[st])], inc="dve")
        t1n = p.n("dve")
        p.emit("vector", lambda e, st=st: e.scalar_tensor_tensor(
            stats["v2t"][:, st:st + 1], stats["m2t"][:, st:st + 1], 1.0 / H,
            stats["ssq2"][:, st:st + 1], ALU.mult, ALU.subtract),
            waits=[("dve", t1n)], inc="dve")
        dn = p.n("dve")
        p.emit("vector", lambda e, st=st: e.tensor_scalar_mul(
            stats["mu2"][:, st:st + 1], stats["mu2s"][:, st:st + 1], 1.0 / H),
            inc="dve")
        p.emit("scalar", lambda e, st=st: e.activation(
            stats["v2t"][:, st:st + 1], stats["v2t"][:, st:st + 1], AF.Ln,
            scale=-1.0 / H, bias=eps_col[:, 0:1]),
            waits=[("dve", dn), ("pl", pl_const)], inc="act")
        lnn = p.n("act")
        p.emit("scalar", lambda e, st=st: e.activation(
            stats["rstd2"][:, st:st + 1], stats["v2t"][:, st:st + 1], AF.Exp, scale=-0.5),
            waits=[("act", lnn)], inc="act")
        o_act[st] = p.n("act")

    for st in range(NT):
        pp = st % 2
        xv = x_tiles[:, (st % 4) * H:(st % 4 + 1) * H]
        x1v = x1_sb[:, pp * H:(pp + 1) * H]
        if st < 4:
            xdep = [("dve", ln1_dve[NT - 4 + st]), ("act", ln1_act[NT - 4 + st])]
        else:
            xdep = [("dve", o_dve[st - 4]), ("pl", o_pl[st - 4])]
        xn = p.dma("sync", xv, x_d[st * 128:(st + 1) * 128, :], waits=xdep,
                   inc="sx%d" % (st % 4))
        p.emit("gpsimd", lambda e, xv=xv: e.tensor_add(xv, xv, gabo_r[:, :]),
               waits=[xn, ("dve", d_gabo)], inc="pl")
        o_pl[st] = p.n("pl")
        mmdep = [("dve", d_wo), ("act", ao_act[st])]
        if st >= 2:
            mmdep.append(("dve", o_dve[st - 2]))
        for ch in range(2):
            pbank = p_y[2 * pp + ch]
            for kt in range(KT):
                aT_t = tT[:, kt * S + st * 128: kt * S + (st + 1) * 128]
                p.emit("tensor", lambda e, pbank=pbank, aT_t=aT_t, kt=kt, ch=ch: e.matmul(
                    pbank[:, :], aT_t, Wo_sb[:, kt * H + ch * 512: kt * H + (ch + 1) * 512],
                    start=(kt == 0), stop=(kt == KT - 1), skip_group_check=True),
                    waits=mmdep if (kt == 0 and ch == 0) else None,
                    inc="pe" if (kt == KT - 1 and ch == 1) else None)
        opn = p.n("pe")
        evdep = [("pe", opn), xn, ("dve", d_gabo)]
        if st >= 2:
            evdep.append(x1_dma[st - 2])
        if st >= 1:
            evdep.append(("dve", o_dve[st - 1]))
        for ch in range(2):
            pbank = p_y[2 * pp + ch]
            cs = slice(ch * 512, (ch + 1) * 512)
            p.emit("vector", lambda e, pbank=pbank, cs=cs: e.tensor_mul(
                evac32[:, cs], pbank[:, :], ga_eff_r[:, cs]),
                waits=evdep if ch == 0 else None, inc="dve")
        mn = p.n("dve")
        for ch in range(2):
            cs = slice(ch * 512, (ch + 1) * 512)
            p.emit("vector", lambda e, cs=cs, x1v=x1v, xv=xv: e.tensor_add(
                x1v[:, cs], evac32[:, cs], xv[:, cs]),
                waits=[("pl", o_pl[st]), ("dve", mn)] if ch == 0 else None,
                inc="dve" if ch == 1 else None)
        o_dve[st] = p.n("dve")
        p.emit("scalar", lambda e, st=st, x1v=x1v: e.activation(
            deadout[:, :], x1v, AF.Copy, accum_out=stats["mu2s"][:, st:st + 1]),
            waits=[("dve", o_dve[st]), ("act", last_dead[0])], inc="act")
        last_dead[0] = p.n("act")
        p.emit("scalar", lambda e, st=st, x1v=x1v: e.activation(
            deadout[:, :], x1v, AF.Square, accum_out=stats["ssq2"][:, st:st + 1]),
            waits=[("act", last_dead[0])], inc="act")
        o_sq[st] = p.n("act")
        last_dead[0] = o_sq[st]
        if st >= 1:
            x1_dma[st - 1] = p.dma(
                "gpsimd", x1_scr[(st - 1) * 128:st * 128, :],
                x1_sb[:, ((st - 1) % 2) * H:((st - 1) % 2 + 1) * H],
                waits=[("dve", o_dve[st - 1])], inc="sx1%d" % ((st - 1) % 2))
        if st == NT - 1:
            x1_dma[st] = p.dma("gpsimd", x1_scr[st * 128:(st + 1) * 128, :], x1v,
                               waits=[("dve", o_dve[st])], inc="sx1%d" % (st % 2))
        if st >= 1:
            emit_o_stats2(st - 1)
        if st == NT - 1:
            emit_o_stats2(st)
    tap("x1t0", x1_sb[:, 0:H], [("dve", o_dve[NT-2])])
    tap("mu2", stats["mu2"][:, :], [("act", o_act[NT-1])])
    tap("rstd2", stats["rstd2"][:, :], [("act", o_act[NT-1])])
    tap("gabo", gabo_r[:, :], [("dve", d_gabo)])
    tap("gaeff", ga_eff_r[:, :], [("dve", d_gabo)])
    osc.close()
    wsc.close()
    tsc.close()
    pre.close()

    fsc = ExitStack()
    w1_sb = SB(fsc, "w1_sb", [128, KT * FF], BF16)
    w2_sb = SB(fsc, "w2_sb", [128, MT * H], BF16)
    g2_sb = SB(fsc, "g2_sb", [128, MT * SBW], BF16)
    h2T = SB(fsc, "h2T", [128, 2 * KT * SBW], BF16)
    h2_bf = SB(fsc, "h2_bf", [128, 2 * H], BF16)
    gb_eff_r = SB(fsc, "gb_eff_r", [128, H], BF16)
    gbb2_r = SB(fsc, "gbb2_r", [128, H], F32)
    b1T = SB(fsc, "b1T", [128, MT], F32)
    b1pT = SB(fsc, "b1pT", [128, MT], F32)
    x1f = SB(fsc, "x1f", [128, 4 * H], F32)
    x2st = SB(fsc, "x2st", [128, 2 * H], F32)
    b2r_f = x2st
    p_u = [PS(fsc, "pu0", [128, 256]), PS(fsc, "pu1", [128, 256])]
    p_y2 = [PS(fsc, "py2%d" % i, [128, 512]) for i in range(4)]
    p_trF = PS(fsc, "p_trF", [128, 1024], BF16)
    p_fb = PS(fsc, "p_fb", [128, 512])

    ffence = fence(dma=("sx10", "sx11", "sx0", "sx1", "sx2", "sx3", "swo",
                        "sal0", "sal1", "sal2", "sal3", "sal4", "sal5",
                        "sal6", "sal7", "sas0", "sas1"))
    for kt in range(MT):
        p.dma("sync", b1T[:, kt:kt + 1], b1_d[kt * 128:(kt + 1) * 128][:, None],
              waits=ffence, inc="sb1")
    n_b1 = ("sb1", p.n("sb1"))
    for q4 in range(4):
        p.dma("gpsimd",
              w1_sb[:, :].rearrange("p (k f) -> p k f", k=KT)[:, 2 * q4:2 * q4 + 2, :],
              w1_d[:, :].rearrange("(k p) f -> p k f", p=128)[:, 2 * q4:2 * q4 + 2, :],
              waits=ffence, inc="sw1")
    w1n = ("sw1", p.n("sw1"))
    for q4 in range(4):
        p.dma("gpsimd",
              w2_sb[:, :].rearrange("p (k f) -> p k f", k=MT)[:, 8 * q4:8 * q4 + 8, :],
              w2_d[:, :].rearrange("(k p) f -> p k f", p=128)[:, 8 * q4:8 * q4 + 8, :],
              waits=ffence, inc="sw2")
    w2n = ("sw2", p.n("sw2"))

    last_bc = 0
    first_bc = True
    for dst, srcrow, dt_ in ((gb_eff_r, gbc_row, BF16), (gbb2_r, gbc_row, F32),
                             (b2r_f, b2row, F32)):
        for ch in range(2):
            dep = [("dve", d_clamp), ("sx", n_small)]
            if first_bc:
                dep += ffence
                first_bc = False
            if last_bc:
                dep.append(("act", last_bc))
            p.emit("tensor", lambda e, srcrow=srcrow, ch=ch: e.matmul(
                p_fb[:, :], ones_row_f[0:1, :], srcrow[0:1, ch * 512:(ch + 1) * 512],
                start=True, stop=True, skip_group_check=True),
                waits=dep, inc="pe")
            pn = p.n("pe")
            p.emit("scalar", lambda e, dst=dst, ch=ch: e.activation(
                dst[:, ch * 512:(ch + 1) * 512], p_fb[:, :], AF.Copy),
                waits=[("pe", pn)], inc="act")
            last_bc = p.n("act")
    p.emit("vector", lambda e: e.tensor_mul(gbb2_r[:, :], gbb2_r[:, :], b2r_f[:, 0:H]),
           waits=[("act", last_bc)] + ffence, inc="dve")
    d_gbb2 = p.n("dve")

    b1cp = p.n("act")
    for c8 in range(8):
        for kt in range(KT):
            p.emit("tensor", lambda e, c8=c8, kt=kt: e.matmul(
                p_fb[0:1, 0:512], sbT_b[:, kt:kt + 1],
                w1_sb[:, kt * FF + c8 * 512: kt * FF + (c8 + 1) * 512],
                start=(kt == 0), stop=(kt == KT - 1), skip_group_check=True),
                waits=[w1n, ("act", a_modT), ("act", b1cp)] if kt == 0 else None,
                inc="pe" if kt == KT - 1 else None)
        pn = p.n("pe")
        p.emit("scalar", lambda e: e.activation(nrow[0:1, 0:512], p_fb[0:1, 0:512], AF.Copy),
               waits=[("pe", pn)], inc="act")
        an = p.n("act")
        b1cp = an
        for t4 in range(4):
            p.emit("tensor", lambda e, c8=c8, t4=t4: e.matmul(
                p_y2[0][:, c8 * 4 + t4: c8 * 4 + t4 + 1],
                nrow[0:1, t4 * 128:(t4 + 1) * 128], ones_row_f[0:1, 0:1],
                start=True, stop=True, skip_group_check=True),
                waits=[("act", an)], inc="pe" if t4 == 3 else None)
    pn = p.n("pe")
    p.emit("scalar", lambda e: e.activation(b1pT[:, 0:MT], p_y2[0][:, 0:MT], AF.Copy),
           waits=[("pe", pn)], inc="act")
    an = p.n("act")
    p.emit("vector", lambda e: e.tensor_add(b1pT[:, :], b1pT[:, :], b1T[:, :]),
           waits=[("act", an), n_b1], inc="dve")
    d_b1p = p.n("dve")
    for kt in range(KT):
        p.emit("vector", lambda e, kt=kt: e.tensor_scalar_mul(
            w1_sb[:, kt * FF:(kt + 1) * FF], w1_sb[:, kt * FF:(kt + 1) * FF],
            cb1T[:, kt:kt + 1]), inc="dve" if kt == KT - 1 else None)
    d_w1 = p.n("dve")

    f_dve, out_dma, gelu_act, trF_act, u_pe, y2_pe, trF_pe = {}, {}, {}, {}, {}, {}, {}

    def f_prep(sb_i):
        ppb = sb_i % 2
        x1v = x1f[:, ppb * 2 * H:(ppb + 1) * 2 * H]
        xdep = [x1_dma[2 * sb_i], x1_dma[2 * sb_i + 1]] + (aofence if sb_i < 2 else [])
        if sb_i >= 2:
            xdep.append(("dve", f_dve[2 * (sb_i - 2) + 1]))
        xn = p.dma(
            "sync",
            x1v.rearrange("p (i c) -> p i c", i=2),
            x1_scr[sb_i * SBW:(sb_i + 1) * SBW, :].rearrange("(i p) c -> p i c", p=128),
            waits=xdep, inc="sf%d" % (sb_i % 2))
        for i in range(2):
            st = 2 * sb_i + i
            hdep = [xn, ("act", o_act[st])]
            if sb_i < 2:
                hdep = hdep + ffence
            if sb_i >= 1:
                hdep.append(("pe", trF_pe[sb_i - 1]))
            p.emit("vector", lambda e, st=st, i=i, ppb=ppb: e.tensor_scalar(
                h2_bf[:, i * H:(i + 1) * H],
                x1f[:, :].rearrange("p (s i c) -> p s i c", s=2, i=2)[:, ppb, i, :],
                stats["mu2"][:, st:st + 1], stats["rstd2"][:, st:st + 1],
                ALU.subtract, ALU.mult),
                waits=hdep, inc="dve" if i == 1 else None)
        dh = p.n("dve")
        for i in range(2):
            trdep = [("dve", dh)]
            if (sb_i, i - 1) in trF_act:
                trdep.append(("act", trF_act[(sb_i, i - 1)]))
            elif (sb_i - 1, 1) in trF_act:
                trdep.append(("act", trF_act[(sb_i - 1, 1)]))
            for kt in range(KT):
                p.emit("tensor", lambda e, kt=kt, i=i: e.transpose(
                    p_trF[:, kt * 128:(kt + 1) * 128],
                    h2_bf[:, i * H + kt * 128: i * H + (kt + 1) * 128],
                    ident_b[:, :]),
                    waits=trdep if kt == 0 else None,
                    inc="pe" if kt == KT - 1 else None)
            ptn = p.n("pe")
            trF_pe[sb_i] = ptn
            p.emit("scalar", lambda e, i=i, ppb=ppb: e.activation(
                h2T.rearrange("p (s k i c) -> p s k i c", s=2, k=KT, i=2)[:, ppb, :, i, :],
                p_trF[:, :].rearrange("p (k c) -> p k c", k=KT), AF.Copy),
                waits=[("pe", ptn)], inc="act")
            trF_act[(sb_i, i)] = p.n("act")

    f_prep(0)
    for sb_i in range(NSB):
        ppb = sb_i % 2
        a_h2T = trF_act[(sb_i, 1)]
        for mt in range(MT):
            ppu = mt % 2
            udep = [("act", a_h2T), ("dve", d_w1), ("dve", d_b1p)]
            gi = sb_i * MT + mt
            if gi >= 2:
                udep.append(("act", gelu_act[gi - 2]))
            for kt in range(KT):
                p.emit("tensor", lambda e, ppu=ppu, mt=mt, kt=kt, ppb=ppb: e.matmul(
                    p_u[ppu][:, :],
                    w1_sb[:, kt * FF + mt * 128: kt * FF + (mt + 1) * 128],
                    h2T[:, ppb * KT * SBW + kt * SBW: ppb * KT * SBW + (kt + 1) * SBW],
                    start=(kt == 0), stop=(kt == KT - 1), skip_group_check=True),
                    waits=udep if kt == 0 else None,
                    inc="pe" if kt == KT - 1 else None)
            pn = p.n("pe")
            gdep = [("pe", pn)]
            if sb_i >= 1 and mt == 0:
                gdep.append(("pe", y2_pe[sb_i - 1]))
            p.emit("scalar", lambda e, ppu=ppu, mt=mt: e.activation(
                g2_sb[:, mt * SBW:(mt + 1) * SBW], p_u[ppu][:, :], AF.Gelu,
                bias=b1pT[:, mt:mt + 1]),
                waits=gdep, inc="act")
            gelu_act[gi] = p.n("act")
        u_pe[sb_i] = p.n("pe")
        if sb_i + 1 < NSB:
            f_prep(sb_i + 1)
        for i in range(2):
            st = 2 * sb_i + i
            ppy = st % 2
            for ch in range(2):
                pbank = p_y2[2 * ppy + ch]
                for kt in range(MT):
                    ydep = [("act", gelu_act[sb_i * MT + kt])]
                    if st >= 2 and kt == 0 and ch == 0:
                        ydep.append(("dve", f_dve[st - 2]))
                    p.emit("tensor", lambda e, pbank=pbank, kt=kt, i=i, ch=ch: e.matmul(
                        pbank[:, :],
                        g2_sb[:, kt * SBW + i * 128: kt * SBW + i * 128 + 128],
                        w2_sb[:, kt * H + ch * 512: kt * H + (ch + 1) * 512],
                        start=(kt == 0), stop=(kt == MT - 1), skip_group_check=True),
                        waits=(ydep + [w2n]) if kt == 0 else
                              [("act", gelu_act[sb_i * MT + kt])],
                        inc="pe" if kt == MT - 1 else None)
            ypn = p.n("pe")
            y2_pe[sb_i] = ypn
            evdep = [("pe", ypn), ("dve", d_gbb2)]
            if st >= 2:
                evdep.append(out_dma[st - 2])
            xt = x2st[:, ppy * H:(ppy + 1) * H]
            for ch in range(2):
                pbank = p_y2[2 * ppy + ch]
                cs = slice(ch * 512, (ch + 1) * 512)
                p.emit("vector", lambda e, pbank=pbank, cs=cs, xt=xt: e.tensor_mul(
                    xt[:, cs], pbank[:, :], gb_eff_r[:, cs]),
                    waits=evdep if ch == 0 else None, inc="dve")
            mn = p.n("dve")
            p.emit("gpsimd", lambda e, xt=xt: e.tensor_add(
                xt[:, :], xt[:, :], gbb2_r[:, :]),
                waits=[("dve", mn), ("dve", d_gbb2)], inc="pl")
            gn = p.n("pl")
            for ch in range(2):
                cs = slice(ch * 512, (ch + 1) * 512)
                p.emit("vector", lambda e, cs=cs, xt=xt, i=i, ppb=ppb: e.tensor_add(
                    xt[:, cs],
                    xt[:, cs],
                    x1f.rearrange("p (s i c) -> p s i c", s=2, i=2)[:, ppb, i, :][:, cs]),
                    waits=[("pl", gn)] if ch == 0 else None,
                    inc="dve" if ch == 1 else None)
            f_dve[st] = p.n("dve")
            out_dma[st] = p.dma("sync", out_d[st * 128:(st + 1) * 128, :],
                                x2st[:, ppy * H:(ppy + 1) * H],
                                waits=[("dve", f_dve[st])],
                                inc="so%d" % (st % 2))
    p.emit("sync", lambda e: e.nop(),
           waits=[("so0", p.n("so0")), ("so1", p.n("so1"))])
    fsc.close()

    with nc.Block() as block:
        p.realize(block, sems)
    root.close()
    return nc


_NC_CACHE = None


def kernel(**inputs):
    global _NC_CACHE
    if _NC_CACHE is None:
        _NC_CACHE = build_nc()
    nc = _NC_CACHE

    x = np.ascontiguousarray(inputs["L0_attn_hidden_states"])
    c = np.ascontiguousarray(inputs["L0_c"])
    shared = {
        "Wq": np.ascontiguousarray(inputs["L0_attn_Wq"]),
        "Wk": np.ascontiguousarray(inputs["L0_attn_Wk"]),
        "Wv": np.ascontiguousarray(inputs["L0_attn_Wv"]),
        "Wo": np.ascontiguousarray(inputs["L0_attn_Wo"]),
        "bq": np.ascontiguousarray(inputs["L0_attn_bq"]),
        "bk": np.ascontiguousarray(inputs["L0_attn_bk"]),
        "bv": np.ascontiguousarray(inputs["L0_attn_bv"]),
        "bo": np.ascontiguousarray(inputs["L0_attn_bo"]),
        "lq1": np.ascontiguousarray(inputs["L0_attn_lambda_q1"]),
        "lk1": np.ascontiguousarray(inputs["L0_attn_lambda_k1"]),
        "lq2": np.ascontiguousarray(inputs["L0_attn_lambda_q2"]),
        "lk2": np.ascontiguousarray(inputs["L0_attn_lambda_k2"]),
        "lnw": np.ascontiguousarray(inputs["L0_attn_ln_w"]),
        "lnb": np.ascontiguousarray(inputs["L0_attn_ln_b"]),
        "w1": np.ascontiguousarray(inputs["L0_ffn_w1"]),
        "b1": np.ascontiguousarray(inputs["L0_ffn_b1"]),
        "w2": np.ascontiguousarray(inputs["L0_ffn_w2"]),
        "b2": np.ascontiguousarray(inputs["L0_ffn_b2"]),
        "wmod": np.ascontiguousarray(inputs["L0_wmod"]),
    }
    in_maps = []
    for b in range(B):
        m = dict(shared)
        m["x"] = np.ascontiguousarray(x[b])
        m["c"] = np.ascontiguousarray(c[b])
        in_maps.append(m)

    res = run_bass_kernel_spmd(nc, in_maps, core_ids=list(range(B)))
    out = np.stack([np.asarray(r["out"]) for r in res.results], axis=0).astype(np.float32)
    return out


if __name__ == "__main__":
    nc = build_nc()
    n = sum(1 for _ in nc.all_instructions())
    print("instructions:", n)
